# version 37
# baseline (speedup 1.0000x reference)
import numpy as np
import concourse.bass as bass
import concourse.mybir as mybir
from concourse.bass_utils import run_bass_kernel_spmd
from contextlib import ExitStack

F32 = mybir.dt.float32
BF16 = mybir.dt.bfloat16
AF = mybir.ActivationFunctionType
ALU = mybir.AluOpType

NCORES = 8
T = 1024
D = 2048
INTER = 8192
NH = 32
EPS = 1e-5
RING_N = 4
RING_W = 2048


def _hadamard(n):
    h = np.array([[1.0]], dtype=np.float32)
    while h.shape[0] < n:
        h = np.block([[h, h], [h, -h]])
    return np.ascontiguousarray(h, dtype=np.float32)


def _perm_q():
    p = np.empty(D, dtype=np.int64)
    for h in range(NH):
        g = h // 4
        tile = (g // 2) * 4 + (h % 4)
        half = g % 2
        r0 = tile * 128 + half * 64
        p[r0:r0 + 64] = np.arange(h * 64, h * 64 + 64)
    return p


def _mk_blocks(W, out_perm=None, in_perm=None):
    W2 = np.asarray(W, dtype=np.float32)
    if out_perm is not None:
        W2 = W2[out_perm]
    if in_perm is not None:
        W2 = W2[:, in_perm]
    O, I = W2.shape
    Mt, Kt = O // 128, I // 128
    a = W2.reshape(Mt, 128, Kt, 128).transpose(0, 3, 2, 1)
    return np.ascontiguousarray(a.reshape(Mt, 128, Kt * 128), dtype=np.float32)


def _vec_tiles(v, nt):
    return np.ascontiguousarray(np.asarray(v, dtype=np.float32).reshape(nt, 128).T)


def host_prepare(inputs):
    g = {}
    cu = np.asarray(inputs["L0_attn_cu_seqlens_q"]).ravel().tolist()
    seg_lens = [int(cu[i + 1] - cu[i]) for i in range(len(cu) - 1)]
    if cu[0] != 0 or any(sl != T for sl in seg_lens):
        raise NotImplementedError(f"kernel built for segments of {T}; got cu_seqlens={cu}")
    x = np.ascontiguousarray(
        np.asarray(inputs["L0_attn_hidden_states"], dtype=np.float32).reshape(8 * T, D))
    pq = _perm_q()
    idx = np.arange(D)
    pu = (idx & 127) * 16 + (idx >> 7)
    idx8 = np.arange(INTER)
    pd = (idx8 & 127) * 64 + (idx8 >> 7)

    g["wq"] = _mk_blocks(inputs["L0_attn_Wq"], out_perm=pq)
    g["wk"] = _mk_blocks(inputs["L0_attn_Wk"])
    g["wv"] = _mk_blocks(inputs["L0_attn_Wv"])
    g["wo"] = _mk_blocks(inputs["L0_attn_Wo"], in_perm=pq)
    g["wu"] = _mk_blocks(inputs["L0_ffn_W_up"], in_perm=pu)
    g["wg"] = _mk_blocks(inputs["L0_ffn_W_gate"], in_perm=pu)
    wd = _mk_blocks(inputs["L0_ffn_W_down"], in_perm=pd)
    g["wd"] = np.ascontiguousarray(wd.reshape(16, 128, 4, 2048).transpose(0, 2, 1, 3))

    wsug = np.asarray(inputs["L0_ffn_Wscale_ug"], np.float32)
    g["fall"] = np.ascontiguousarray(np.concatenate([
        _vec_tiles(inputs["L0_ffn_SU_ug"], 16),
        _vec_tiles(inputs["L0_ffn_SU_dp"], 64),
        _vec_tiles(wsug[:INTER], 64),
        _vec_tiles(wsug[INTER:], 64),
        _vec_tiles(inputs["L0_ffn_Wscale_dp"], 16)], axis=1))

    kk, qq = np.meshgrid(np.arange(128), np.arange(128), indexing="ij")
    g["cstall"] = np.ascontiguousarray(np.concatenate([
        _hadamard(128),
        np.kron(_hadamard(16), np.eye(8, dtype=np.float32)),
        np.kron(_hadamard(64), np.eye(2, dtype=np.float32)),
        (kk <= qq).astype(np.float32),
        np.eye(128, dtype=np.float32)], axis=1))

    shards = [np.ascontiguousarray(x[c * T:(c + 1) * T, :].T.reshape(16, 128, T))
              for c in range(NCORES)]
    return g, shards



class Sem:
    def __init__(self, es, nc, name):
        self.h = es.enter_context(nc.semaphore(name))
        self.n = 0


class Prog:
    KEYS = ("t", "s", "v", "g", "y")
    SELF_ORDER = ("s", "v")

    def __init__(self):
        self.ops = {k: [] for k in self.KEYS}
        self.last = {}

    def add(self, eng, fn, sem=None, k=1):
        if eng in self.SELF_ORDER and eng in self.last:
            lsem, lval = self.last[eng]
            self.ops[eng].append(
                (lambda e, _s=lsem, _v=lval: e.wait_ge(_s, _v), None, 0))
        self.ops[eng].append((fn, sem.h if sem is not None else None, k))
        if sem is not None:
            sem.n += k
            if eng in self.SELF_ORDER:
                self.last[eng] = (sem.h, sem.n)
            return sem.n

    def wait(self, eng, sem, val):
        if val is None:
            raise RuntimeError("wait target is None")
        self.ops[eng].append(
            (lambda e, _s=sem.h, _v=val: e.wait_ge(_s, _v), None, 0))

    def run(self, handle, key):
        for fn, semh, k in self.ops[key]:
            inst = fn(handle)
            if semh is not None:
                inst.then_inc(semh, k)


def build_program(stop_after="all", taps=()):
    nc = bass.Bass("TRN2", target_bir_lowering=False, debug=False)
    es = ExitStack()

    dx = nc.dram_tensor("x", [16, 128, T], F32, kind="ExternalInput").ap()
    dwq = nc.dram_tensor("wq", [16, 128, 2048], F32, kind="ExternalInput").ap()
    dwk = nc.dram_tensor("wk", [4, 128, 2048], F32, kind="ExternalInput").ap()
    dwv = nc.dram_tensor("wv", [4, 128, 2048], F32, kind="ExternalInput").ap()
    dwo = nc.dram_tensor("wo", [16, 128, 2048], F32, kind="ExternalInput").ap()
    dwu = nc.dram_tensor("wu", [64, 128, 2048], F32, kind="ExternalInput").ap()
    dwg = nc.dram_tensor("wg", [64, 128, 2048], F32, kind="ExternalInput").ap()
    dwd = nc.dram_tensor("wd", [16, 4, 128, 2048], F32, kind="ExternalInput").ap()
    dcst = nc.dram_tensor("cstall", [128, 640], F32, kind="ExternalInput").ap()
    dfm = nc.dram_tensor("fall", [128, 224], F32, kind="ExternalInput").ap()
    dy = nc.dram_tensor("y", [16, 128, T], F32, kind="ExternalOutput").ap()
    dz = nc.dram_tensor("zscratch", [64, 128, T], BF16, kind="Internal").ap()

    big0 = es.enter_context(nc.sbuf_tensor("big0", [128, 16 * T], BF16))
    big1 = es.enter_context(nc.sbuf_tensor("big1", [128, 16 * T], BF16))
    big2 = es.enter_context(nc.sbuf_tensor("big2", [128, 16 * T], BF16))
    big3 = es.enter_context(nc.sbuf_tensor("big3", [128, 16 * T], BF16))
    big4 = es.enter_context(nc.sbuf_tensor("big4", [128, 16 * T], BF16))
    scrA = es.enter_context(nc.sbuf_tensor("scrA", [128, 4096], BF16))
    scrB = es.enter_context(nc.sbuf_tensor("scrB", [128, 4096], BF16))
    vtbuf = es.enter_context(nc.sbuf_tensor("vtbuf", [128, 64 * 65], BF16))
    esb = es.enter_context(nc.sbuf_tensor("esb", [128, 2048], BF16))
    wring = es.enter_context(nc.sbuf_tensor("wring", [128, RING_N * RING_W], BF16))
    cst = es.enter_context(nc.sbuf_tensor("cst", [128, 712], BF16))
    fmisc = es.enter_context(nc.sbuf_tensor("fmisc", [128, 352], F32))

    hadU, hadA, hadB = cst[:, 0:128], cst[:, 128:256], cst[:, 256:384]
    tri, iden2 = cst[:, 384:512], cst[:, 512:640]
    ones_bf = cst[:, 640:641]
    onesrow_bf = cst[0:1, 644:708]
    su_eff = fmisc[:, 0:16]
    cu_eff = fmisc[:, 16:80]
    wsu_raw = fmisc[:, 80:144]
    wsg64 = fmisc[:, 144:208]
    wsd64 = fmisc[:, 208:224]
    ones_f = fmisc[:, 224:352]

    pA = es.enter_context(nc.psum_tensor("pA", [128, 1024], F32))
    pB = es.enter_context(nc.psum_tensor("pB", [128, 1024], F32))
    pS = es.enter_context(nc.psum_tensor("pS", [128, 2048], F32))
    pC = pS[:, 0:1024]
    pD = pS[:, 1024:2048]
    pT = pS[:, 1024:2048].bitcast(BF16)

    s_s = Sem(es, nc, "s_s")
    s_pe = Sem(es, nc, "s_pe")
    s_sc = Sem(es, nc, "s_sc")
    s_ve = Sem(es, nc, "s_ve")
    s_gp = Sem(es, nc, "s_gp")
    s_pre = Sem(es, nc, "s_pre")
    rs = [Sem(es, nc, f"rs{k}") for k in range(RING_N)]
    s_z = [Sem(es, nc, "s_z0"), Sem(es, nc, "s_z1")]
    s_x = [Sem(es, nc, "s_x0"), Sem(es, nc, "s_x1")]
    s_bd = [Sem(es, nc, "s_bd0"), Sem(es, nc, "s_bd1")]
    s_od = [Sem(es, nc, "s_od0"), Sem(es, nc, "s_od1")]
    s_op = [Sem(es, nc, "s_op0"), Sem(es, nc, "s_op1")]

    P = Prog()

    b0 = big0.rearrange("p (t n) -> p t n", t=16)
    b1 = big1.rearrange("p (t n) -> p t n", t=16)
    b2 = big2.rearrange("p (t n) -> p t n", t=16)
    b3 = big3.rearrange("p (t n) -> p t n", t=16)
    b4 = big4.rearrange("p (t n) -> p t n", t=16)
    kbf = scrA.rearrange("p (t n) -> p t n", t=4)
    zb6 = scrA.rearrange("p (t n) -> p t n", t=4)
    vtmp2 = scrB[:, 0:2048].rearrange("p (t n) -> p t n", t=2)
    vt3 = vtbuf.rearrange("p (t n) -> p t n", t=64)
    r_f32 = big1.bitcast(F32).rearrange("p (t n) -> p t n", t=16)

    def h2B(v):
        r, tt = divmod(v, 16)
        return [b0, b1, b2, b4][r][:, tt, :]

    P.add("g", lambda g: g.dma_start(out=cst[:, 0:640], in_=dcst), s_pre, 16)
    P.add("g", lambda g: g.dma_start(out=fmisc[:, 0:224], in_=dfm), s_pre, 16)
    P.add("g", lambda g: g.dma_start(
        out=b0[:, :, :], in_=dx.rearrange("t p n -> p t n")), s_pre, 16)
    PRE_T = 48

    ringl = []
    ringl += [(f"wk{m}", dwk[m]) for m in range(4)]
    ringl += [(f"wv{m}", dwv[m]) for m in range(4)]
    ringl += [(f"wq{m}", dwq[m]) for m in range(16)]
    for half in range(2):
        ringl += [(f"wo{half}_{m}", dwo[m]) for m in range(16)]
    for m in range(64):
        ringl += [(f"wu{m}", dwu[m]), (f"wg{m}", dwg[m])]
    for m in range(16):
        ringl += [(f"wd{m}_{kq}", dwd[m, kq]) for kq in range(4)]

    g_target = {lab: (rs[j % RING_N], 16 * (j // RING_N + 1))
                for j, (lab, _s) in enumerate(ringl)}
    ring_idx = {lab: j for j, (lab, _s) in enumerate(ringl)}
    ring_free = {}
    emitted = [0]

    def ring_slot(lab):
        j = ring_idx[lab] % RING_N
        return wring[:, j * RING_W:(j + 1) * RING_W]

    def emit_loads():
        i = emitted[0]
        while i < len(ringl):
            lab, src = ringl[i]
            if i >= RING_N:
                if (i - RING_N) not in ring_free:
                    break
                P.wait("g", s_pe, ring_free[i - RING_N])
            P.add("g", lambda g, _d=ring_slot(lab), _s=src:
                  g.dma_start(out=_d, in_=_s), rs[i % RING_N], 16)
            i += 1
        emitted[0] = i

    tapreg = {
        "xbf": (big0, [128, 16 * T], BF16), "q": (big1, [128, 16 * T], BF16),
        "kv": (scrA, [128, 6144], BF16), "vt": (vtbuf, [128, 64 * 65], BF16),
        "attn": (big2, [128, 16 * T], BF16), "x2": (big3, [128, 16 * T], BF16),
        "x1": (big4, [128, 16 * T], BF16),
        "h0": (big0, [128, 16 * T], BF16), "h1": (big1, [128, 16 * T], BF16),
        "h2": (big2, [128, 16 * T], BF16), "h4": (big4, [128, 16 * T], BF16),
    }
    dtaps = {name: nc.dram_tensor("tap_" + name, tapreg[name][1], tapreg[name][2],
                                  kind="ExternalOutput").ap() for name in taps}

    def finish():
        emit_loads()
        for name in taps:
            P.wait("y", s_ve, s_ve.n)
            P.wait("y", s_sc, s_sc.n)
            P.wait("y", s_pe, s_pe.n)
            P.add("y", lambda y, _d=dtaps[name], _a=tapreg[name][0][:, :]:
                  y.dma_start(out=_d, in_=_a), s_s, 16)
        if s_s.n:
            P.wait("y", s_s, s_s.n)
        for _sm in (s_od[0], s_od[1], s_op[0], s_op[1]):
            if _sm.n:
                P.wait("y", _sm, _sm.n)
        with nc.Block() as block:
            block.tensor(lambda t: P.run(t, "t"))
            block.scalar(lambda s: P.run(s, "s"))
            block.vector(lambda v: P.run(v, "v"))
            block.gpsimd(lambda g: P.run(g, "g"))
            block.sync(lambda y: P.run(y, "y"))
        es.close()
        return nc

    emit_loads()
    P.add("g", lambda g: g.memset(ones_bf, 1.0), s_gp)
    P.add("g", lambda g: g.memset(cst[:, 644:708], 1.0), s_gp)
    P.add("g", lambda g: g.memset(ones_f, 1.0), s_gp)
    P.add("g", lambda g: g.memset(vt3[:, :, 64:65], 1.0), s_gp)
    GP_MEMSETS = s_gp.n

    P.add("v", lambda v: v.memset(pS[:, :], 0.0), s_ve)
    P.wait("v", s_pre, PRE_T)
    c1 = float(1.0 / (64.0 * np.sqrt(2048.0)))
    c2 = float(1.0 / np.sqrt(8192.0))
    P.add("v", lambda v: v.tensor_scalar(out=su_eff, in0=su_eff, scalar1=c1,
                                         scalar2=None, op0=ALU.mult), s_ve)
    P.add("v", lambda v: v.tensor_tensor(out=cu_eff, in0=cu_eff, in1=wsu_raw,
                                         op=ALU.mult), s_ve)
    P.add("v", lambda v: v.tensor_scalar(out=cu_eff, in0=cu_eff, scalar1=c2,
                                         scalar2=None, op0=ALU.mult), s_ve)
    P.add("v", lambda v: v.tensor_scalar(out=wsg64, in0=wsg64, scalar1=64.0,
                                         scalar2=None, op0=ALU.mult), s_ve)
    VE_SCALES = P.add("v", lambda v: v.tensor_scalar(out=wsd64, in0=wsd64, scalar1=64.0,
                                                     scalar2=None, op0=ALU.mult), s_ve)

    evac_of = {}
    kv_plan = ([(f"wk{m}", m) for m in range(4)] +
               [(f"wv{m}", m) for m in range(4)])
    tb_evac = {}
    kbf_evac = {}

    P.wait("t", s_pre, PRE_T)
    for nb, (lab, m) in enumerate(kv_plan):
        ps = pA if nb % 2 == 0 else pB
        P.wait("t", *g_target[lab])
        if nb >= 2:
            sm, tv = evac_of[kv_plan[nb - 2][0]]
            P.wait("t", sm, tv)
        w3 = ring_slot(lab).rearrange("p (k j) -> p k j", k=16)

        def mmqkv(t, _ps=ps, _w3=w3):
            i = None
            for nh in range(2):
                for k in range(16):
                    i = t.matmul(_ps[:, nh * 512:(nh + 1) * 512], _w3[:, k, :],
                                 b0[:, k, nh * 512:(nh + 1) * 512],
                                 start=(k == 0), stop=(k == 15))
            return i
        done = P.add("t", mmqkv, s_pe)
        ring_free[ring_idx[lab]] = s_pe.n

        if lab.startswith("wk"):
            P.wait("v", s_pe, done)
            ev = P.add("v", lambda v, _m=m, _ps=ps: v.tensor_copy(
                out=kbf[:, _m, :], in_=_ps[:]), s_ve)
            evac_of[lab] = (s_ve, ev)
            kbf_evac[m] = ev
        else:
            P.wait("v", s_pe, done)
            ev = P.add("v", lambda v, _m=m, _ps=ps: v.tensor_copy(
                out=vtmp2[:, _m % 2, :], in_=_ps[:]), s_ve)
            evac_of[lab] = (s_ve, ev)
            P.wait("t", s_ve, ev)
            P.wait("t", s_pre, PRE_T)
            if m >= 1:
                P.wait("t", s_ve, tb_evac[m - 1])

            def trans(t, _m=m):
                vt = vtmp2[:, _m % 2, :]
                i = None
                for kt in range(8):
                    i = t.transpose(pT[:, kt * 128:(kt + 1) * 128],
                                    vt[:, kt * 128:(kt + 1) * 128], iden2)
                return i
            tb_done = P.add("t", trans, s_pe)
            P.wait("v", s_pe, tb_done)
            P.wait("v", s_gp, GP_MEMSETS)
            pv8 = pT[:, 0:1024].rearrange("p (a b) -> p a b", a=8)
            P.add("v", lambda v, _m=m: v.tensor_copy(
                out=vt3[:, 16 * _m:16 * _m + 8, 0:64], in_=pv8[:, :, 0:64]), s_ve)
            tb_evac[m] = P.add("v", lambda v, _m=m: v.tensor_copy(
                out=vt3[:, 16 * _m + 8:16 * _m + 16, 0:64],
                in_=pv8[:, :, 64:128]), s_ve)
        emit_loads()

    if stop_after == "ph1":
        return finish()

    q_evac = {}
    wq_state = {"m": 0, "c": 0}

    def emit_wq_chunk():
        m, c = wq_state["m"], wq_state["c"]
        if m >= 16:
            return False
        lab = f"wq{m}"
        if c == 0:
            P.wait("t", *g_target[lab])
            if m >= 1:
                P.wait("t", s_sc, q_evac[m - 1])
            else:
                P.wait("t", evac_of["wv2"][0], evac_of["wv2"][1])
        nh, k0 = c // 4, (c % 4) * 4
        w3 = ring_slot(lab).rearrange("p (k j) -> p k j", k=16)

        def chunk(t, _w3=w3, _nh=nh, _k0=k0):
            i = None
            for k in range(_k0, _k0 + 4):
                i = t.matmul(pA[:, _nh * 512:(_nh + 1) * 512], _w3[:, k, :],
                             b0[:, k, _nh * 512:(_nh + 1) * 512],
                             start=(k == 0), stop=(k == 15))
            return i
        d = P.add("t", chunk, s_pe)
        if c == 7:
            ring_free[ring_idx[lab]] = s_pe.n
            P.wait("s", s_pe, d)
            q_evac[m] = P.add("s", lambda s, _m=m: s.copy(
                out=b1[:, _m, :], in_=pA[:, :]), s_sc)
            wq_state["m"], wq_state["c"] = m + 1, 0
            emit_loads()
        else:
            wq_state["c"] = c + 1
        return True

    def ensure_wq(m):
        while wq_state["m"] <= m and (wq_state["m"] < 16):
            emit_wq_chunk()

    ehAB = [esb[:, 0:2048], scrB[:, 0:2048]]
    rbsb = scrB[:, 2048:3072]
    row_f = [big3[:, 0:2048].bitcast(F32)[64:65, 0:1024],
             big3[:, 4096:6144].bitcast(F32)[64:65, 0:1024]]
    recbf = [big3[64:65, 2048:3072], big3[64:65, 3072:4096]]
    onesblk = cst[:, 644:708]

    st = {"expA": None, "expB": None,
          "jS": 0,
          "eh": [None, None],
          "tl": {1: {"mult": None, "copyA": None, "copyB": None}},
          "pend": {1: None}}

    def c0f(qc, kt):
        return max(0, 128 * kt - 512 * qc)

    r0_f32 = big4.bitcast(F32).rearrange("p (t n) -> p t n", t=16)
    rsq0 = big3[:, 6144:10240].bitcast(F32)
    wo0_state = {"m": 0, "c": 0, "on": False, "gate": None}
    r0_done = {}
    st0_done = {}
    st0_q = []
    gsq0 = {}

    def emit_st0(upto):
        while st0_q and st0_q[0] <= upto:
            m_ = st0_q.pop(0)
            P.wait("t", s_ve, r0_done[m_])
            P.wait("t", s_gp, gsq0[m_])

            def st0mm(t, _m=m_):
                t.matmul(pA[0:1, 512:1024], ones_f[:, 0:1], r0_f32[:, _m, :],
                         start=(_m == 0), stop=(_m == 15))
                return t.matmul(pA[32:33, 512:1024], ones_f[:, 0:1],
                                rsq0[:, (_m % 4) * 512:(_m % 4) * 512 + 512],
                                start=(_m == 0), stop=(_m == 15),
                                tile_position=(0, 32))
            st0_done[m_] = P.add("t", st0mm, s_pe)

    def emit_wo0_chunk():
        m, c = wo0_state["m"], wo0_state["c"]
        if not wo0_state["on"] or m >= 16:
            return False
        lab = f"wo0_{m}"
        if c == 0:
            emit_st0(m - 3)
            P.wait("t", *g_target[lab])
            if m >= 1:
                P.wait("t", s_ve, r0_done[m - 1])
            else:
                P.wait("t", s_ve, wo0_state["gate"])
                P.wait("t", s_sc, q_evac[15])
        w3 = ring_slot(lab).rearrange("p (k j) -> p k j", k=16)

        def chunk(t, _w3=w3, _k0=c * 4):
            i_ = None
            for k in range(_k0, _k0 + 4):
                i_ = t.matmul(pA[:, 0:512], _w3[:, k, :], b2[:, k, 0:512],
                              start=(k == 0), stop=(k == 15))
            return i_
        d = P.add("t", chunk, s_pe)
        if c < 3:
            wo0_state["c"] = c + 1
            return True
        ring_free[ring_idx[lab]] = s_pe.n
        P.wait("v", s_pe, d)
        r0_done[m] = P.add("v", lambda v, _m=m: v.tensor_tensor(
            out=r0_f32[:, _m, :], in0=pA[:, 0:512], in1=b0[:, _m, 0:512],
            op=ALU.add), s_ve)
        P.wait("g", s_ve, r0_done[m])
        if m >= 4:
            P.wait("g", s_pe, st0_done[m - 4])
        gsq0[m] = P.add("g", lambda g, _m=m: g.tensor_mul(
            out=rsq0[:, (_m % 4) * 512:(_m % 4) * 512 + 512],
            in0=r0_f32[:, _m, :], in1=r0_f32[:, _m, :]), s_gp)
        st0_q.append(m)
        wo0_state["m"], wo0_state["c"] = m + 1, 0
        emit_loads()
        return True

    def emit_filler():
        if not emit_wq_chunk():
            emit_wo0_chunk()

    def emit_norm_recips(pV, rpar):
        P.add("s", lambda s, _r=rpar, _p=pV: s.activation(
            out=row_f[_r], in_=_p[64:65, 0:1024], func=AF.Ln), s_sc)
        return P.add("s", lambda s, _r=rpar: s.activation(
            out=recbf[_r], in_=row_f[_r], func=AF.Exp, scale=-1.0), s_sc)

    def emit_norm_finish(qt, qc, cast, tk, pV, rpar):
        tl = st["tl"][tk]
        qcc = slice(qc * 512, (qc + 1) * 512)
        P.wait("t", s_sc, cast)
        if tl["copyA"] is not None:
            P.wait("t", s_ve, tl["copyA"])

        def bcA(t, _p=pV, _r=rpar):
            return t.matmul(_p[64:128, 0:512], onesblk[64:65, 0:64],
                            recbf[_r][0:1, 0:512],
                            start=True, stop=True, tile_position=(64, 64))
        bA = P.add("t", bcA, s_pe)
        if tl["copyB"] is not None:
            P.wait("t", s_ve, tl["copyB"])

        def bcB(t, _p=pV, _r=rpar):
            return t.matmul(_p[64:128, 512:1024], onesblk[64:65, 0:64],
                            recbf[_r][0:1, 512:1024],
                            start=True, stop=True, tile_position=(64, 64))
        bB = P.add("t", bcB, s_pe)
        P.wait("v", s_pe, bA)
        tl["copyA"] = P.add("v", lambda v, _p=pV: v.tensor_copy(
            out=rbsb[0:64, 0:512], in_=_p[64:128, 0:512]), s_ve)
        P.add("v", lambda v, _q=qt, _c=qcc, _p=pV: v.tensor_tensor(
            out=b2[0:64, _q, _c], in0=_p[0:64, 0:512], in1=rbsb[0:64, 0:512],
            op=ALU.mult), s_ve)
        P.wait("v", s_pe, bB)
        tl["copyB"] = P.add("v", lambda v, _p=pV: v.tensor_copy(
            out=rbsb[64:128, 512:1024], in_=_p[64:128, 512:1024]), s_ve)
        tl["mult"] = P.add("v", lambda v, _q=qt, _c=qcc, _p=pV: v.tensor_tensor(
            out=b2[64:128, _q, _c], in0=_p[0:64, 512:1024],
            in1=rbsb[64:128, 512:1024], op=ALU.mult), s_ve)
        if qt == 15 and qc == 0:
            wo0_state["on"] = True
            wo0_state["gate"] = tl["mult"]

    def emit_qc_iter(p, gp, i, qc):
        qt = 4 * gp + i
        if qc == 0:
            ensure_wq(qt)
        exp_ctr = {}
        mask_ctr = {}
        exp_par = {}
        q_hat = 16 * qc + p

        def emit_pv_quad(j, last_kt):
            kt0, kt1 = 2 * j, 2 * j + 1
            for side in ("A", "B"):
                for kt in (kt0, kt1):
                    if (side, kt) in mask_ctr:
                        P.wait("t", s_ve, mask_ctr[(side, kt)])
            P.wait("t", s_sc, exp_ctr[j])
            if kt0 == 0 and st["tl"][1]["mult"] is not None:
                P.wait("t", s_ve, st["tl"][1]["mult"])
            par = exp_par[j]

            def pvq(t, _gp=gp, _qc=qc, _k0=kt0, _k1=kt1, _last=last_kt,
                    _par=par):
                i_ = None
                for kt in (_k0, _k1):
                    c0 = c0f(_qc, kt)
                    for g, eoff, off in ((2 * _gp, 0, 0), (2 * _gp + 1, 1024, 512)):
                        i_ = t.matmul(
                            pB[0:65, off + c0:off + 512],
                            vt3[:, g * 8 + kt, 0:65],
                            ehAB[_par][:, eoff + (kt % 2) * 512 + c0:
                                       eoff + (kt % 2) * 512 + 512],
                            start=(kt == 0), stop=(kt == _last))
                return i_
            d = P.add("t", pvq, s_pe)
            st["eh"][par] = d
            return d

        kts = [0, 1, 2, 3] if qc == 0 else list(range(8))
        njs = len(kts) // 2
        for j in range(njs):
            kt0, kt1 = 2 * j, 2 * j + 1
            if p == 0 and qc == 0 and j == 0:
                P.wait("t", s_sc, q_evac[qt])
                P.wait("t", s_ve, kbf_evac[gp])
                P.wait("t", s_ve, tb_evac[3])
                P.wait("t", s_gp, GP_MEMSETS)
            elif qc == 0 and j == 0:
                P.wait("t", s_sc, q_evac[qt])

            def spair(t, _gp=gp, _qt=qt, _qc=qc, _k0=kt0, _k1=kt1,
                      _qb=0, _soff=0):
                i_ = None
                for kt in (_k0, _k1):
                    c0 = c0f(_qc, kt)
                    i_ = t.matmul(
                        pS[:, _soff + (kt % 2) * 512 + c0:
                           _soff + (kt % 2) * 512 + 512],
                        kbf[_qb:_qb + 64, _gp, kt * 128:(kt + 1) * 128],
                        b1[_qb:_qb + 64, _qt, _qc * 512 + c0:(_qc + 1) * 512],
                        start=True, stop=True)
                return i_
            if st["expA"] is not None:
                P.wait("t", s_sc, st["expA"])
            sqA = P.add("t", lambda t, _f=spair: _f(t), s_pe)
            emit_filler()
            if st["expB"] is not None:
                P.wait("t", s_sc, st["expB"])
            sqB = P.add("t", lambda t, _f=spair: _f(t, _qb=64, _soff=1024), s_pe)

            if j == min(1, njs - 1) and st["pend"][1] is not None:
                emit_filler()
                emit_filler()
                emit_norm_finish(*st["pend"][1])
                st["pend"][1] = None

            emit_filler()

            c0e = c0f(qc, kt0)
            par = st["jS"] % 2
            st["jS"] += 1
            exp_par[j] = par
            if p == 0 and qc == 0 and j <= 1:
                P.wait("s", s_ve, tb_evac[3])
            if st["eh"][par] is not None:
                P.wait("s", s_pe, st["eh"][par])
            P.wait("s", s_pe, sqA)
            ecA = P.add(
                "s", lambda s, _eh=ehAB[par], _c0=c0e: s.activation(
                    out=_eh[:, _c0:1024], in_=pS[:, _c0:1024], func=AF.Exp,
                    scale=0.125), s_sc)
            P.wait("s", s_pe, sqB)
            ecB = P.add(
                "s", lambda s, _eh=ehAB[par], _c0=c0e: s.activation(
                    out=_eh[:, 1024 + _c0:2048], in_=pS[:, 1024 + _c0:2048],
                    func=AF.Exp, scale=0.125), s_sc)
            exp_ctr[j] = ecB
            exp_ctr[(j, "A")] = ecA
            st["expA"] = ecA
            st["expB"] = ecB
            for kt in (kt0, kt1):
                if 128 * kt >= 512 * qc:
                    c0 = c0f(qc, kt)
                    for side, eoff, ec in (("A", 0, ecA), ("B", 1024, ecB)):
                        P.wait("v", s_sc, ec)
                        mask_ctr[(side, kt)] = P.add(
                            "v", lambda v, _eh=ehAB[par],
                            _o=eoff + (kt % 2) * 512 + c0:
                            v.tensor_tensor(out=_eh[:, _o:_o + 128],
                                            in0=_eh[:, _o:_o + 128], in1=tri,
                                            op=ALU.mult), s_ve)

            if j >= 1:
                emit_pv_quad(j - 1, kts[-1])
                emit_filler()
        pv_last = emit_pv_quad(njs - 1, kts[-1])
        P.wait("s", s_pe, pv_last)
        st["pend"][1] = (qt, qc, emit_norm_recips(pB, q_hat % 2), 1, pB,
                         q_hat % 2)
        emit_filler()
        emit_loads()

    pairs = [(g_, i_) for g_ in range(4) for i_ in range(4)]
    for p, (gp, i) in enumerate(pairs):
        emit_qc_iter(p, gp, i, 0)
    ensure_wq(15)
    for p, (gp, i) in enumerate(pairs):
        emit_qc_iter(p, gp, i, 1)
    emit_norm_finish(*st["pend"][1])
    st["pend"][1] = None
    b2_ready = st["tl"][1]["mult"]
    while emit_wo0_chunk():
        pass
    emit_st0(15)
    SC_AB = s_sc.n

    if stop_after == "ph2":
        return finish()

    rowA = scrB[0:1, 0:1024].bitcast(F32)
    rowB = scrB[0:1, 1024:2048].bitcast(F32)
    rsq = scrB[:, 2048:4096].bitcast(F32)
    msq0 = esb[:, 0:1024].bitcast(F32)[0:1, 0:512]
    x2_last = None

    def emit_ln_tail(sum_ap, sq_ap, rX, half, cb=None):
        P.add("v", lambda v: v.tensor_scalar(out=rowA, in0=sum_ap,
                                             scalar1=float(1.0 / D), scalar2=None,
                                             op0=ALU.mult), s_ve)
        P.add("v", lambda v: v.tensor_tensor(out=rowB, in0=rowA, in1=rowA,
                                             op=ALU.mult), s_ve)
        stats_read = P.add("v", lambda v: v.tensor_scalar(
            out=msq0, in0=sq_ap, scalar1=float(1.0 / D), scalar2=None,
            op0=ALU.mult), s_ve)
        vv_done = P.add("v", lambda v: v.tensor_tensor(
            out=rowB, in0=msq0, in1=rowB, op=ALU.subtract), s_ve)
        vv_done = P.add("v", lambda v: v.tensor_scalar(
            out=rowB, in0=rowB, scalar1=float(EPS), scalar2=None, op0=ALU.add), s_ve)
        P.wait("s", s_ve, vv_done)
        P.add("s", lambda s: s.activation(out=rowB, in_=rowB, func=AF.Ln), s_sc)
        a_done = P.add("s", lambda s: s.activation(out=rowB, in_=rowB,
                                                   func=AF.Exp, scale=-0.5), s_sc)
        P.wait("v", s_sc, a_done)
        P.add("v", lambda v: v.tensor_tensor(out=rowA, in0=rowA, in1=rowB,
                                             op=ALU.mult), s_ve)
        b_done = P.add("v", lambda v: v.tensor_scalar(
            out=rowA, in0=rowA, scalar1=-1.0, scalar2=None, op0=ALU.mult), s_ve)
        P.wait("t", s_ve, b_done)

        def bcln(t):
            t.matmul(pD[:, 0:512], ones_f[0:1, 0:128], rowB,
                     start=True, stop=True)
            return t.matmul(pD[:, 512:1024], ones_f[0:1, 0:128], rowA,
                            start=True, stop=True)
        bc1 = P.add("t", bcln, s_pe)

        x2t = rsq[:, 512:1024]
        P.wait("v", s_pe, bc1)
        last = None
        for m in range(16):
            P.add("v", lambda v, _m=m, _r=rX: v.tensor_tensor(
                out=x2t, in0=_r[:, _m, :], in1=pD[:, 0:512],
                op=ALU.mult), s_ve)
            last = P.add("v", lambda v, _m=m, _h=half: v.tensor_tensor(
                out=b3[:, _m, _h * 512:(_h + 1) * 512], in0=x2t,
                in1=pD[:, 512:1024], op=ALU.add), s_ve)
            if cb is not None:
                cb(m, last)
        return stats_read, last

    P.wait("v", s_pe, st0_done[15])
    P.wait("t", s_sc, SC_AB)
    LN0_PA_READ, x2_h0_last = emit_ln_tail(
        pA[0:1, 512:1024], pA[32:33, 512:1024], r0_f32, 0)

    xp0 = esb.rearrange("p (a n) -> p a n", a=4)
    ph40 = {"t": 0, "j": 0, "on": False}
    z1h0 = {}
    z1ev0 = {}
    z1out0 = {}
    xpd0 = {}
    x1rd0 = {}
    h16x0 = {}
    x1ev0 = {}

    def emit_ph40_chunk():
        if not ph40["on"]:
            return False
        t_i, j = ph40["t"], ph40["j"]
        if t_i < 16:
            if t_i >= 2:
                P.wait("v", s_pe, z1h0[t_i - 2])
            xpd0[t_i] = P.add("v", lambda v, _t=t_i: v.tensor_scalar(
                out=xp0[:, _t % 2, :], in0=b3[:, _t, 0:512],
                scalar1=su_eff[:, _t:_t + 1], scalar2=None, op0=ALU.mult), s_ve)
            P.wait("t", s_ve, xpd0[t_i])
            if t_i == 0:
                P.wait("t", s_ve, b2_ready)
                P.wait("t", s_pre, PRE_T)
            if t_i >= 2:
                P.wait("t", s_sc, z1ev0[t_i - 2])
            z1h0[t_i] = P.add("t", lambda t, _t=t_i: t.matmul(
                pB[:, (_t % 2) * 512:(_t % 2) * 512 + 512], hadU,
                xp0[:, _t % 2, :], start=True, stop=True), s_pe)
            P.wait("s", s_pe, z1h0[t_i])
            jj = t_i // 2
            if jj >= 2 and t_i % 2 == 0:
                P.wait("s", s_z[jj % 2], z1out0[jj - 2])
            z1ev0[t_i] = P.add("s", lambda s, _t=t_i: s.copy(
                out=zb6[:, (_t // 2 % 2) * 2 + (_t % 2), 0:512],
                in_=pB[:, (_t % 2) * 512:(_t % 2) * 512 + 512]), s_sc)
            if t_i % 2 == 1:
                P.wait("y", s_sc, z1ev0[t_i])
                z1out0[jj] = P.add("y", lambda y, _j=jj: y.dma_start(
                    out=dz[2 * _j:2 * _j + 2].rearrange("t p n -> p t n")[:, :, 0:512],
                    in_=zb6[:, (_j % 2) * 2:(_j % 2) * 2 + 2, 0:512]), s_z[jj % 2], 16)
            ph40["t"] = t_i + 1
            return True
        if j < 8:
            if j == 0:
                P.wait("y", s_z[0], z1out0[6])
                P.wait("y", s_z[1], z1out0[7])
            if j >= 2:
                P.wait("y", s_pe, h16x0[j - 2])
            x1rd0[j] = P.add("y", lambda y, _j=j: y.dma_start(
                out=zb6[:, (_j % 2) * 2:(_j % 2) * 2 + 2, 0:512],
                in_=dz16[:, 2 * _j:2 * _j + 2, 0:512]), s_x[j % 2], 16)
            P.wait("t", s_x[j % 2], x1rd0[j])
            if j == 0:
                P.wait("t", s_pre, PRE_T)
                P.wait("t", s_sc, z1ev0[15])
            else:
                P.wait("t", s_sc, x1ev0[2 * (j - 1) + 1])

            def h16(t, _j=j):
                i_ = None
                for uu in range(2):
                    i_ = t.matmul(pB[:, uu * 512:(uu + 1) * 512], hadA,
                                  zb6[:, (_j % 2) * 2 + uu, 0:512],
                                  start=True, stop=True)
                return i_
            h16x0[j] = P.add("t", h16, s_pe)
            for uu in range(2):
                P.wait("s", s_pe, h16x0[j])
                x1ev0[2 * j + uu] = P.add("s", lambda s, _j=j, _uu=uu: s.copy(
                    out=b4[:, 2 * _j + _uu, 0:512],
                    in_=pB[:, _uu * 512:(_uu + 1) * 512]), s_sc)
            ph40["j"] = j + 1
            return True
        return False

    dz16 = dz[0:16].rearrange("t (b u) n -> (t b) u n", u=16)
    ph40["on"] = True

    o_done = {}
    r_done = {}
    sq_done = {}
    st_done = {}
    for m in range(17):
        if m < 16:
            lab = f"wo1_{m}"
            P.wait("t", *g_target[lab])
            if m >= 1:
                P.wait("t", s_ve, r_done[m - 1])
            else:
                P.wait("t", s_ve, LN0_PA_READ)
            w3 = ring_slot(lab).rearrange("p (k j) -> p k j", k=16)

            def omm(t, _w3=w3):
                i = None
                for k in range(16):
                    i = t.matmul(pA[:, 0:512], _w3[:, k, :],
                                 b2[:, k, 512:1024],
                                 start=(k == 0), stop=(k == 15))
                return i
            o_done[m] = P.add("t", omm, s_pe)
            ring_free[ring_idx[lab]] = s_pe.n

            P.wait("v", s_pe, o_done[m])
            r_done[m] = P.add("v", lambda v, _m=m: v.tensor_tensor(
                out=r_f32[:, _m, :], in0=pA[:, 0:512],
                in1=b0[:, _m, 512:1024], op=ALU.add), s_ve)

            P.wait("s", s_ve, r_done[m])
            if m >= 2:
                P.wait("s", s_pe, st_done[m - 2])
            sq_done[m] = P.add("s", lambda s, _m=m: s.activation(
                out=rsq[:, (_m % 2) * 512:(_m % 2) * 512 + 512],
                in_=r_f32[:, _m, :], func=AF.Square), s_sc)
        if m >= 1:
            mm_ = m - 1
            P.wait("t", s_ve, r_done[mm_])
            P.wait("t", s_sc, sq_done[mm_])

            def stmm(t, _m=mm_):
                t.matmul(pC[0:1, 0:512], ones_f[:, 0:1], r_f32[:, _m, :],
                         start=(_m == 0), stop=(_m == 15))
                return t.matmul(pC[0:1, 512:1024], ones_f[:, 0:1],
                                rsq[:, (_m % 2) * 512:(_m % 2) * 512 + 512],
                                start=(_m == 0), stop=(_m == 15))
            st_done[mm_] = P.add("t", stmm, s_pe)
        emit_ph40_chunk()
        emit_ph40_chunk()
        emit_loads()
    while emit_ph40_chunk():
        pass

    xp1 = scrB.rearrange("p (a n) -> p a n", a=8)
    z1h1 = {}
    z1ev1 = {}
    z1out1 = {}

    def ph41_cb(t_i, x2_ctr):
        if t_i >= 2:
            P.wait("v", s_pe, z1h1[t_i - 2])
        xpd = P.add("v", lambda v, _t=t_i: v.tensor_scalar(
            out=xp1[:, _t % 2, :], in0=b3[:, _t, 512:1024],
            scalar1=su_eff[:, _t:_t + 1], scalar2=None, op0=ALU.mult), s_ve)
        P.wait("t", s_ve, xpd)
        if t_i >= 2:
            P.wait("t", s_sc, z1ev1[t_i - 2])
        z1h1[t_i] = P.add("t", lambda t, _t=t_i: t.matmul(
            pA[:, (_t % 2) * 512:(_t % 2) * 512 + 512], hadU,
            xp1[:, _t % 2, :], start=True, stop=True), s_pe)
        P.wait("s", s_pe, z1h1[t_i])
        jj = t_i // 2
        if jj >= 2 and t_i % 2 == 0:
            P.wait("s", s_z[jj % 2], z1out1[jj - 2])
        z1ev1[t_i] = P.add("s", lambda s, _t=t_i: s.copy(
            out=zb6[:, (_t // 2 % 2) * 2 + (_t % 2), 512:1024],
            in_=pA[:, (_t % 2) * 512:(_t % 2) * 512 + 512]), s_sc)
        if t_i % 2 == 1:
            P.wait("y", s_sc, z1ev1[t_i])
            z1out1[jj] = P.add("y", lambda y, _j=jj: y.dma_start(
                out=dz[2 * _j:2 * _j + 2].rearrange("t p n -> p t n")[:, :, 512:1024],
                in_=zb6[:, (_j % 2) * 2:(_j % 2) * 2 + 2, 512:1024]), s_z[jj % 2], 16)

    P.wait("v", s_pe, st_done[15])
    P.wait("t", s_ve, x2_h0_last)
    _, x2_last = emit_ln_tail(pC[0:1, 0:512], pC[0:1, 512:1024], r_f32, 1,
                              cb=ph41_cb)

    x1rd1 = {}
    h16x1 = {}
    x1ev1 = {}
    for j in range(8):
        if j == 0:
            P.wait("y", s_z[0], z1out1[6])
            P.wait("y", s_z[1], z1out1[7])
        if j >= 2:
            P.wait("y", s_pe, h16x1[j - 2])
        x1rd1[j] = P.add("y", lambda y, _j=j: y.dma_start(
            out=zb6[:, (_j % 2) * 2:(_j % 2) * 2 + 2, 512:1024],
            in_=dz16[:, 2 * _j:2 * _j + 2, 512:1024]), s_x[j % 2], 16)
        P.wait("t", s_x[j % 2], x1rd1[j])
        if j == 0:
            P.wait("t", s_sc, z1ev1[15])
        else:
            P.wait("t", s_sc, x1ev1[2 * (j - 1) + 1])

        def h16b(t, _j=j):
            i_ = None
            for uu in range(2):
                i_ = t.matmul(pA[:, uu * 512:(uu + 1) * 512], hadA,
                              zb6[:, (_j % 2) * 2 + uu, 512:1024],
                              start=True, stop=True)
            return i_
        h16x1[j] = P.add("t", h16b, s_pe)
        for uu in range(2):
            P.wait("s", s_pe, h16x1[j])
            x1ev1[2 * j + uu] = P.add("s", lambda s, _j=j, _uu=uu: s.copy(
                out=b4[:, 2 * _j + _uu, 512:1024],
                in_=pA[:, _uu * 512:(_uu + 1) * 512]), s_sc)
        emit_loads()
    X1_READY = x1ev1[15]

    if stop_after == "ph3":
        return finish()

    silu_f = scrB[:, 0:2048].bitcast(F32)
    htmp_f = scrB[:, 2048:4096].bitcast(F32)
    hbf = vtbuf[:, 0:4096].rearrange("p (a n) -> p a n", a=4)
    zev = {}
    zout = {}
    hevac = {}
    zh = {}
    silu_done = {}

    def emit_z(m):
        psGz = pC if m % 2 == 0 else pD
        P.wait("t", s_ve, hevac[m])
        P.wait("t", s_sc, silu_done[m])

        def zmm(t, _ps=psGz, _m=m):
            i = None
            for (c0, c1) in [(0, 512), (512, 1024)]:
                i = t.matmul(_ps[:, c0:c1], hadU, hbf[:, _m % 2, c0:c1],
                             start=True, stop=True)
            return i
        zh[m] = P.add("t", zmm, s_pe)
        P.wait("s", s_pe, zh[m])
        j = m // 2
        if j >= 2 and m % 2 == 0:
            P.wait("s", s_z[j % 2], zout[j - 2])
        zev[m] = P.add("s", lambda s, _ps=psGz, _m=m: s.copy(
            out=zb6[:, (_m // 2 % 2) * 2 + (_m % 2), :], in_=_ps[:]), s_sc)
        if m % 2 == 1:
            P.wait("y", s_sc, zev[m])
            zout[j] = P.add("y", lambda y, _j=j: y.dma_start(
                out=dz[2 * _j:2 * _j + 2].rearrange("t p n -> p t n"),
                in_=zb6[:, (_j % 2) * 2:(_j % 2) * 2 + 2, :]), s_z[j % 2], 16)

    for m in range(64):
        upl, gal = f"wu{m}", f"wg{m}"
        psU = pA if m % 2 == 0 else pB
        psG = pC if m % 2 == 0 else pD
        P.wait("t", *g_target[upl])
        if m == 0:
            P.wait("t", s_sc, X1_READY)
        if m >= 2:
            P.wait("t", s_ve, hevac[m - 2])

        def upmm(t, _ps=psU, _lab=upl):
            w3 = ring_slot(_lab).rearrange("p (k j) -> p k j", k=16)
            i = None
            for nh in range(2):
                for k in range(16):
                    i = t.matmul(_ps[:, nh * 512:(nh + 1) * 512], w3[:, k, :],
                                 b4[:, k, nh * 512:(nh + 1) * 512],
                                 start=(k == 0), stop=(k == 15))
            return i
        up_done = P.add("t", upmm, s_pe)
        ring_free[ring_idx[upl]] = s_pe.n

        P.wait("t", *g_target[gal])
        if m >= 2:
            P.wait("t", s_sc, zev[m - 2])

        def gamm(t, _ps=psG, _lab=gal):
            w3 = ring_slot(_lab).rearrange("p (k j) -> p k j", k=16)
            i = None
            for nh in range(2):
                for k in range(16):
                    i = t.matmul(_ps[:, nh * 512:(nh + 1) * 512], w3[:, k, :],
                                 b4[:, k, nh * 512:(nh + 1) * 512],
                                 start=(k == 0), stop=(k == 15))
            return i
        ga_done = P.add("t", gamm, s_pe)
        ring_free[ring_idx[gal]] = s_pe.n

        if m >= 1:
            emit_z(m - 1)

        P.wait("s", s_pe, ga_done)
        if m >= 1:
            P.wait("s", s_ve, hevac[m - 1])
        silu_done[m] = P.add("s", lambda s, _ps=psG, _m=m: s.activation(
            out=silu_f[:, :], in_=_ps[:], func=AF.Silu, scale=wsg64[:, _m:_m + 1]),
            s_sc)

        P.wait("v", s_sc, silu_done[m])
        P.wait("v", s_pe, up_done)
        if m >= 2:
            P.wait("v", s_pe, zh[m - 2])
        P.add("v", lambda v, _ps=psU: v.tensor_tensor(
            out=htmp_f[:, :], in0=silu_f[:, :], in1=_ps[:], op=ALU.mult), s_ve)
        hevac[m] = P.add("v", lambda v, _m=m: v.tensor_scalar(
            out=hbf[:, _m % 2, :], in0=htmp_f[:, :], scalar1=cu_eff[:, _m:_m + 1],
            scalar2=None, op0=ALU.mult), s_ve)
        emit_loads()
    emit_z(63)

    if stop_after == "ph5":
        return finish()

    dz64 = dz.rearrange("t (b v) n -> (t b) v n", v=64)
    brd = {}
    bh = {}
    bh2 = {}
    bev = {}
    bev_s = {}
    bufA4 = scrA[:, 0:4096].rearrange("p (a n) -> p a n", a=4)
    bufB4 = scrB[:, 0:4096].rearrange("p (a n) -> p a n", a=4)
    for j in range(16):
        buf = bufA4 if j % 2 == 0 else bufB4
        if j == 0:
            P.wait("y", s_z[0], zout[30])
            P.wait("y", s_z[1], zout[31])
        if j == 1:
            P.wait("y", s_ve, hevac[63])
        if j >= 2:
            P.wait("y", s_pe, bh2[j - 2])
        brd[j] = P.add("y", lambda y, _j=j, _b=buf: y.dma_start(
            out=_b[:, :, :], in_=dz64[:, 4 * _j:4 * _j + 4, :]), s_bd[j % 2], 16)
        P.wait("t", s_bd[j % 2], brd[j])
        P.wait("t", s_pre, PRE_T)
        if j >= 1:
            P.wait("t", s_ve, bev[4 * (j - 1) + 1])
        def h64a(t, _b=buf):
            i = None
            for vv, ps in ((0, pA), (1, pC)):
                for (c0, c1) in [(0, 512), (512, 1024)]:
                    i = t.matmul(ps[:, c0:c1], hadB, _b[:, vv, c0:c1],
                                 start=True, stop=True)
            return i
        bh[j] = P.add("t", h64a, s_pe)
        if j >= 1:
            P.wait("t", s_sc, bev_s[4 * (j - 1) + 3])
        def h64b(t, _b=buf):
            i = None
            for vv, ps in ((2, pB), (3, pD)):
                for (c0, c1) in [(0, 512), (512, 1024)]:
                    i = t.matmul(ps[:, c0:c1], hadB, _b[:, vv, c0:c1],
                                 start=True, stop=True)
            return i
        bh2[j] = P.add("t", h64b, s_pe)
        P.wait("v", s_pe, bh[j])
        for vv, ps in ((0, pA), (1, pC)):
            bev[4 * j + vv] = P.add("v", lambda v, _j=j, _vv=vv, _ps=ps: v.tensor_copy(
                out=h2B(4 * _j + _vv), in_=_ps[:]), s_ve)
        P.wait("s", s_pe, bh2[j])
        for vv, ps in ((2, pB), (3, pD)):
            bev_s[4 * j + vv] = P.add("s", lambda s, _j=j, _vv=vv, _ps=ps: s.copy(
                out=h2B(4 * _j + _vv), in_=_ps[:]), s_sc)
        emit_loads()

    if stop_after == "ph6":
        return finish()

    rowA2 = scrB[0:1, 0:2048].bitcast(F32)
    rowB2 = scrB[0:1, 2048:4096].bitcast(F32)
    rsq2 = scrB[:, 0:2048]
    dn_done = {}
    r2_done = {}
    sq2 = {}
    st2 = {}
    for m in range(17):
        if m < 16:
            psDn = pA if m % 2 == 0 else pB
            if m == 0:
                P.wait("t", s_ve, bev[61])
                P.wait("t", s_sc, bev_s[63])
            if m >= 2:
                P.wait("t", s_ve, r2_done[m - 2])
            for kq in range(4):
                lab = f"wd{m}_{kq}"
                P.wait("t", *g_target[lab])

                def dnmm(t, _lab=lab, _ps=psDn, _kq=kq):
                    w3 = ring_slot(_lab).rearrange("p (k j) -> p k j", k=16)
                    i = None
                    for k in range(16):
                        vv = _kq * 16 + k
                        for nh in range(2):
                            i = t.matmul(_ps[:, nh * 512:(nh + 1) * 512], w3[:, k, :],
                                         h2B(vv)[:, nh * 512:(nh + 1) * 512],
                                         start=(_kq == 0 and k == 0),
                                         stop=(_kq == 3 and k == 15))
                    return i
                dn_done[m] = P.add("t", dnmm, s_pe)
                ring_free[ring_idx[lab]] = s_pe.n
            P.wait("s", s_pe, dn_done[m])
            scl = P.add("s", lambda s, _ps=psDn, _m=m: s.activation(
                out=_ps[:], in_=_ps[:], func=AF.Copy, scale=wsd64[:, _m:_m + 1]), s_sc)
            P.wait("v", s_sc, scl)
            r2_done[m] = P.add("v", lambda v, _ps=psDn, _m=m: v.tensor_tensor(
                out=b3[:, _m, :], in0=_ps[:], in1=b3[:, _m, :], op=ALU.add), s_ve)
            P.wait("s", s_ve, r2_done[m])
            if m >= 2:
                P.wait("s", s_pe, st2[m - 2])
            sq2[m] = P.add("s", lambda s, _m=m: s.activation(
                out=rsq2[:, (_m % 2) * 1024:(_m % 2 + 1) * 1024], in_=b3[:, _m, :],
                func=AF.Square), s_sc)
        if m >= 1:
            mm_ = m - 1
            P.wait("t", s_ve, r2_done[mm_])
            P.wait("t", s_sc, sq2[mm_])

            def st2mm(t, _m=mm_):
                i = None
                for (c0, c1) in [(0, 512), (512, 1024)]:
                    t.matmul(pC[0:1, c0:c1], ones_bf, b3[:, _m, c0:c1],
                             start=(_m == 0), stop=(_m == 15))
                    i = t.matmul(pD[0:1, c0:c1], ones_bf,
                                 rsq2[:, (_m % 2) * 1024 + c0:(_m % 2) * 1024 + c1],
                                 start=(_m == 0), stop=(_m == 15))
                return i
            st2[mm_] = P.add("t", st2mm, s_pe)
        emit_loads()

    P.wait("v", s_pe, st2[15])
    P.add("v", lambda v: v.tensor_scalar(out=rowA2, in0=pC[0:1, :],
                                         scalar1=float(1.0 / D), scalar2=None,
                                         op0=ALU.mult), s_ve)
    P.add("v", lambda v: v.tensor_tensor(out=rowB2, in0=rowA2, in1=rowA2,
                                         op=ALU.mult), s_ve)
    P.add("v", lambda v: v.tensor_scalar(out=pD[0:1, :], in0=pD[0:1, :],
                                         scalar1=float(1.0 / D), scalar2=None,
                                         op0=ALU.mult), s_ve)
    vv2 = P.add("v", lambda v: v.tensor_tensor(out=rowB2, in0=pD[0:1, :],
                                               in1=rowB2, op=ALU.subtract), s_ve)
    vv2 = P.add("v", lambda v: v.tensor_scalar(
        out=rowB2, in0=rowB2, scalar1=float(EPS), scalar2=None, op0=ALU.add), s_ve)
    P.wait("s", s_ve, vv2)
    P.add("s", lambda s: s.activation(out=rowB2, in_=rowB2, func=AF.Ln), s_sc)
    a2_done = P.add("s", lambda s: s.activation(out=rowB2, in_=rowB2,
                                                func=AF.Exp, scale=-0.5), s_sc)
    P.wait("v", s_sc, a2_done)
    P.add("v", lambda v: v.tensor_tensor(out=rowA2, in0=rowA2, in1=rowB2,
                                         op=ALU.mult), s_ve)
    b2_done = P.add("v", lambda v: v.tensor_scalar(out=rowA2, in0=rowA2,
                                                   scalar1=-1.0, scalar2=None,
                                                   op0=ALU.mult), s_ve)
    P.wait("t", s_ve, b2_done)

    def bcln2(t):
        i = None
        for (c0, c1) in [(0, 512), (512, 1024)]:
            t.matmul(pC[:, c0:c1], ones_f[0:1, 0:128], rowB2[0:1, c0:c1],
                     start=True, stop=True)
            i = t.matmul(pD[:, c0:c1], ones_f[0:1, 0:128], rowA2[0:1, c0:c1],
                         start=True, stop=True)
        return i
    bc2 = P.add("t", bcln2, s_pe)

    outst = scrB.bitcast(F32)
    outg = big1.bitcast(F32)
    a_sb = vtbuf[:, 0:2048].bitcast(F32)
    b_sb = vtbuf[:, 2048:4096].bitcast(F32)
    out_dma = {}
    P.wait("v", s_pe, bc2)
    P.add("v", lambda v: v.tensor_copy(out=a_sb, in_=pC[:]), s_ve)
    ab_done = P.add("v", lambda v: v.tensor_copy(out=b_sb, in_=pD[:]), s_ve)
    P.wait("g", s_ve, ab_done)
    pool_set = (1, 4, 7, 10, 13, 15)
    prev_v, prev_g = {}, {}
    nv = ng = 0
    for m in range(16):
        if m not in pool_set:
            sl = slice((nv % 2) * 1024, (nv % 2 + 1) * 1024)
            if nv >= 2:
                P.wait("v", s_od[nv % 2], out_dma[prev_v[nv % 2]])
            P.add("v", lambda v, _m=m, _s=sl: v.tensor_tensor(
                out=outst[:, _s], in0=b3[:, _m, :], in1=pC[:], op=ALU.mult), s_ve)
            od = P.add("v", lambda v, _m=m, _s=sl: v.tensor_tensor(
                out=outst[:, _s], in0=outst[:, _s], in1=pD[:], op=ALU.add), s_ve)
            P.wait("y", s_ve, od)
            out_dma[m] = P.add("y", lambda y, _m=m, _s=sl: y.dma_start(
                out=dy[_m], in_=outst[:, _s]), s_od[nv % 2], 16)
            prev_v[nv % 2] = m
            nv += 1
        else:
            sl = slice((ng % 2) * 1024, (ng % 2 + 1) * 1024)
            if ng >= 2:
                P.wait("g", s_op[ng % 2], out_dma[prev_g[ng % 2]])
            gm = P.add("g", lambda g, _m=m, _s=sl: g.tensor_tensor(
                out=outg[:, _s], in0=b3[:, _m, :], in1=a_sb, op=ALU.mult), s_gp)
            P.wait("g", s_gp, gm)
            od = P.add("g", lambda g, _m=m, _s=sl: g.tensor_tensor(
                out=outg[:, _s], in0=outg[:, _s], in1=b_sb, op=ALU.add), s_gp)
            P.wait("s", s_gp, od)
            out_dma[m] = P.add("s", lambda s, _m=m, _s=sl: s.dma_start(
                out=dy[_m], in_=outg[:, _s]), s_op[ng % 2], 16)
            prev_g[ng % 2] = m
            ng += 1
    return finish()



def kernel(**inputs):
    g, shards = host_prepare(inputs)
    nc = build_program()
    shared = {k: g[k] for k in ("wq", "wk", "wv", "wo", "wu", "wg", "wd",
                                "cstall", "fall")}
    in_maps = [dict(shared, x=shards[c]) for c in range(NCORES)]
    res = run_bass_kernel_spmd(nc, in_maps, core_ids=list(range(NCORES)))
    out = np.empty((8 * T, D), dtype=np.float32)
    for c in range(NCORES):
        yc = np.asarray(res.results[c]["y"])
        out[c * T:(c + 1) * T, :] = yc.reshape(D, T).T
    return out.reshape(4, 2048, 2048)


# version 38
# speedup vs baseline: 1.2145x; 1.2145x over previous
import numpy as np
import concourse.bass as bass
import concourse.mybir as mybir
from concourse.bass_utils import run_bass_kernel_spmd
from contextlib import ExitStack

F32 = mybir.dt.float32
BF16 = mybir.dt.bfloat16
AF = mybir.ActivationFunctionType
ALU = mybir.AluOpType

NCORES = 8
T = 1024
D = 2048
INTER = 8192
NH = 32
EPS = 1e-5
RING_N = 4
RING_W = 2048


def _hadamard(n):
    h = np.array([[1.0]], dtype=np.float32)
    while h.shape[0] < n:
        h = np.block([[h, h], [h, -h]])
    return np.ascontiguousarray(h, dtype=np.float32)


def _perm_q():
    p = np.empty(D, dtype=np.int64)
    for h in range(NH):
        g = h // 4
        tile = (g // 2) * 4 + (h % 4)
        half = g % 2
        r0 = tile * 128 + half * 64
        p[r0:r0 + 64] = np.arange(h * 64, h * 64 + 64)
    return p


def _mk_blocks(W, out_perm=None, in_perm=None):
    W2 = np.asarray(W, dtype=np.float32)
    if out_perm is not None:
        W2 = W2[out_perm]
    if in_perm is not None:
        W2 = W2[:, in_perm]
    O, I = W2.shape
    Mt, Kt = O // 128, I // 128
    a = W2.reshape(Mt, 128, Kt, 128).transpose(0, 3, 2, 1)
    return np.ascontiguousarray(a.reshape(Mt, 128, Kt * 128), dtype=np.float32)


def _vec_tiles(v, nt):
    return np.ascontiguousarray(np.asarray(v, dtype=np.float32).reshape(nt, 128).T)


def host_prepare(inputs):
    g = {}
    cu = np.asarray(inputs["L0_attn_cu_seqlens_q"]).ravel().tolist()
    seg_lens = [int(cu[i + 1] - cu[i]) for i in range(len(cu) - 1)]
    if cu[0] != 0 or any(sl != T for sl in seg_lens):
        raise NotImplementedError(f"kernel built for segments of {T}; got cu_seqlens={cu}")
    x = np.ascontiguousarray(
        np.asarray(inputs["L0_attn_hidden_states"], dtype=np.float32).reshape(8 * T, D))
    pq = _perm_q()
    idx = np.arange(D)
    pu = (idx & 127) * 16 + (idx >> 7)
    idx8 = np.arange(INTER)
    pd = (idx8 & 127) * 64 + (idx8 >> 7)

    g["wq"] = _mk_blocks(inputs["L0_attn_Wq"], out_perm=pq)
    g["wk"] = _mk_blocks(inputs["L0_attn_Wk"])
    g["wv"] = _mk_blocks(inputs["L0_attn_Wv"])
    g["wo"] = _mk_blocks(inputs["L0_attn_Wo"], in_perm=pq)
    g["wu"] = _mk_blocks(inputs["L0_ffn_W_up"], in_perm=pu)
    g["wg"] = _mk_blocks(inputs["L0_ffn_W_gate"], in_perm=pu)
    wd = _mk_blocks(inputs["L0_ffn_W_down"], in_perm=pd)
    g["wd"] = np.ascontiguousarray(wd.reshape(16, 128, 4, 2048).transpose(0, 2, 1, 3))

    wsug = np.asarray(inputs["L0_ffn_Wscale_ug"], np.float32)
    g["fall"] = np.ascontiguousarray(np.concatenate([
        _vec_tiles(inputs["L0_ffn_SU_ug"], 16),
        _vec_tiles(inputs["L0_ffn_SU_dp"], 64),
        _vec_tiles(wsug[:INTER], 64),
        _vec_tiles(wsug[INTER:], 64),
        _vec_tiles(inputs["L0_ffn_Wscale_dp"], 16)], axis=1))

    kk, qq = np.meshgrid(np.arange(128), np.arange(128), indexing="ij")
    g["cstall"] = np.ascontiguousarray(np.concatenate([
        _hadamard(128),
        np.kron(_hadamard(16), np.eye(8, dtype=np.float32)),
        np.kron(_hadamard(64), np.eye(2, dtype=np.float32)),
        (kk <= qq).astype(np.float32),
        np.eye(128, dtype=np.float32)], axis=1))

    shards = [np.ascontiguousarray(x[c * T:(c + 1) * T, :].T.reshape(16, 128, T))
              for c in range(NCORES)]
    return g, shards



class Sem:
    def __init__(self, es, nc, name):
        self.h = es.enter_context(nc.semaphore(name))
        self.n = 0


class Prog:
    KEYS = ("t", "s", "v", "g", "y")
    SELF_ORDER = ("s", "v")

    def __init__(self):
        self.ops = {k: [] for k in self.KEYS}
        self.last = {}

    def add(self, eng, fn, sem=None, k=1):
        if eng in self.SELF_ORDER and eng in self.last:
            lsem, lval = self.last[eng]
            self.ops[eng].append(
                (lambda e, _s=lsem, _v=lval: e.wait_ge(_s, _v), None, 0))
        self.ops[eng].append((fn, sem.h if sem is not None else None, k))
        if sem is not None:
            sem.n += k
            if eng in self.SELF_ORDER:
                self.last[eng] = (sem.h, sem.n)
            return sem.n

    def wait(self, eng, sem, val):
        if val is None:
            raise RuntimeError("wait target is None")
        self.ops[eng].append(
            (lambda e, _s=sem.h, _v=val: e.wait_ge(_s, _v), None, 0))

    def run(self, handle, key):
        for fn, semh, k in self.ops[key]:
            inst = fn(handle)
            if semh is not None:
                inst.then_inc(semh, k)


def build_program(stop_after="all", taps=()):
    nc = bass.Bass("TRN2", target_bir_lowering=False, debug=False)
    es = ExitStack()

    dx = nc.dram_tensor("x", [16, 128, T], F32, kind="ExternalInput").ap()
    dwq = nc.dram_tensor("wq", [16, 128, 2048], F32, kind="ExternalInput").ap()
    dwk = nc.dram_tensor("wk", [4, 128, 2048], F32, kind="ExternalInput").ap()
    dwv = nc.dram_tensor("wv", [4, 128, 2048], F32, kind="ExternalInput").ap()
    dwo = nc.dram_tensor("wo", [16, 128, 2048], F32, kind="ExternalInput").ap()
    dwu = nc.dram_tensor("wu", [64, 128, 2048], F32, kind="ExternalInput").ap()
    dwg = nc.dram_tensor("wg", [64, 128, 2048], F32, kind="ExternalInput").ap()
    dwd = nc.dram_tensor("wd", [16, 4, 128, 2048], F32, kind="ExternalInput").ap()
    dcst = nc.dram_tensor("cstall", [128, 640], F32, kind="ExternalInput").ap()
    dfm = nc.dram_tensor("fall", [128, 224], F32, kind="ExternalInput").ap()
    dy = nc.dram_tensor("y", [16, 128, T], F32, kind="ExternalOutput").ap()
    dz = nc.dram_tensor("zscratch", [64, 128, T], BF16, kind="Internal").ap()

    big0 = es.enter_context(nc.sbuf_tensor("big0", [128, 16 * T], BF16))
    big1 = es.enter_context(nc.sbuf_tensor("big1", [128, 16 * T], BF16))
    big2 = es.enter_context(nc.sbuf_tensor("big2", [128, 16 * T], BF16))
    big3 = es.enter_context(nc.sbuf_tensor("big3", [128, 16 * T], BF16))
    big4 = es.enter_context(nc.sbuf_tensor("big4", [128, 16 * T], BF16))
    scrA = es.enter_context(nc.sbuf_tensor("scrA", [128, 4096], BF16))
    scrB = es.enter_context(nc.sbuf_tensor("scrB", [128, 4096], BF16))
    vtbuf = es.enter_context(nc.sbuf_tensor("vtbuf", [128, 64 * 65], BF16))
    esb = es.enter_context(nc.sbuf_tensor("esb", [128, 2048], BF16))
    wring = es.enter_context(nc.sbuf_tensor("wring", [128, RING_N * RING_W], BF16))
    cst = es.enter_context(nc.sbuf_tensor("cst", [128, 712], BF16))
    fmisc = es.enter_context(nc.sbuf_tensor("fmisc", [128, 352], F32))

    hadU, hadA, hadB = cst[:, 0:128], cst[:, 128:256], cst[:, 256:384]
    tri, iden2 = cst[:, 384:512], cst[:, 512:640]
    ones_bf = cst[:, 640:641]
    onesrow_bf = cst[0:1, 644:708]
    su_eff = fmisc[:, 0:16]
    cu_eff = fmisc[:, 16:80]
    wsu_raw = fmisc[:, 80:144]
    wsg64 = fmisc[:, 144:208]
    wsd64 = fmisc[:, 208:224]
    ones_f = fmisc[:, 224:352]

    pA = es.enter_context(nc.psum_tensor("pA", [128, 1024], F32))
    pB = es.enter_context(nc.psum_tensor("pB", [128, 1024], F32))
    pS = es.enter_context(nc.psum_tensor("pS", [128, 2048], F32))
    pC = pS[:, 0:1024]
    pD = pS[:, 1024:2048]
    pT = pS[:, 1024:2048].bitcast(BF16)

    s_s = Sem(es, nc, "s_s")
    s_pe = Sem(es, nc, "s_pe")
    s_sc = Sem(es, nc, "s_sc")
    s_ve = Sem(es, nc, "s_ve")
    s_gp = Sem(es, nc, "s_gp")
    s_pre = Sem(es, nc, "s_pre")
    rs = [Sem(es, nc, f"rs{k}") for k in range(RING_N)]
    s_z = [Sem(es, nc, "s_z0"), Sem(es, nc, "s_z1")]
    s_x = [Sem(es, nc, "s_x0"), Sem(es, nc, "s_x1")]
    s_bd = [Sem(es, nc, "s_bd0"), Sem(es, nc, "s_bd1")]
    s_od = [Sem(es, nc, "s_od0"), Sem(es, nc, "s_od1")]
    s_op = [Sem(es, nc, "s_op0"), Sem(es, nc, "s_op1")]

    P = Prog()

    b0 = big0.rearrange("p (t n) -> p t n", t=16)
    b1 = big1.rearrange("p (t n) -> p t n", t=16)
    b2 = big2.rearrange("p (t n) -> p t n", t=16)
    b3 = big3.rearrange("p (t n) -> p t n", t=16)
    b4 = big4.rearrange("p (t n) -> p t n", t=16)
    kbf = scrA.rearrange("p (t n) -> p t n", t=4)
    zb6 = scrA.rearrange("p (t n) -> p t n", t=4)
    vtmp2 = scrB[:, 0:2048].rearrange("p (t n) -> p t n", t=2)
    vt3 = vtbuf.rearrange("p (t n) -> p t n", t=64)
    r_f32 = big1.bitcast(F32).rearrange("p (t n) -> p t n", t=16)

    def h2B(v):
        r, tt = divmod(v, 16)
        return [b0, b1, b2, b4][r][:, tt, :]

    P.add("g", lambda g: g.dma_start(out=cst[:, 0:640], in_=dcst), s_pre, 16)
    P.add("g", lambda g: g.dma_start(out=fmisc[:, 0:224], in_=dfm), s_pre, 16)
    P.add("g", lambda g: g.dma_start(
        out=b0[:, :, :], in_=dx.rearrange("t p n -> p t n")), s_pre, 16)
    PRE_T = 48

    ringl = []
    ringl += [(f"wk{m}", dwk[m]) for m in range(4)]
    ringl += [(f"wv{m}", dwv[m]) for m in range(4)]
    ringl += [(f"wq{m}", dwq[m]) for m in range(16)]
    for half in range(2):
        ringl += [(f"wo{half}_{m}", dwo[m]) for m in range(16)]
    for m in range(64):
        ringl += [(f"wu{m}", dwu[m]), (f"wg{m}", dwg[m])]
    for m in range(16):
        ringl += [(f"wd{m}_{kq}", dwd[m, kq]) for kq in range(4)]

    g_target = {lab: (rs[j % RING_N], 16 * (j // RING_N + 1))
                for j, (lab, _s) in enumerate(ringl)}
    ring_idx = {lab: j for j, (lab, _s) in enumerate(ringl)}
    ring_free = {}
    emitted = [0]

    def ring_slot(lab):
        j = ring_idx[lab] % RING_N
        return wring[:, j * RING_W:(j + 1) * RING_W]

    def emit_loads():
        i = emitted[0]
        while i < len(ringl):
            lab, src = ringl[i]
            if i >= RING_N:
                if (i - RING_N) not in ring_free:
                    break
                P.wait("g", s_pe, ring_free[i - RING_N])
            P.add("g", lambda g, _d=ring_slot(lab), _s=src:
                  g.dma_start(out=_d, in_=_s), rs[i % RING_N], 16)
            i += 1
        emitted[0] = i

    tapreg = {
        "xbf": (big0, [128, 16 * T], BF16), "q": (big1, [128, 16 * T], BF16),
        "kv": (scrA, [128, 6144], BF16), "vt": (vtbuf, [128, 64 * 65], BF16),
        "attn": (big2, [128, 16 * T], BF16), "x2": (big3, [128, 16 * T], BF16),
        "x1": (big4, [128, 16 * T], BF16),
        "h0": (big0, [128, 16 * T], BF16), "h1": (big1, [128, 16 * T], BF16),
        "h2": (big2, [128, 16 * T], BF16), "h4": (big4, [128, 16 * T], BF16),
    }
    dtaps = {name: nc.dram_tensor("tap_" + name, tapreg[name][1], tapreg[name][2],
                                  kind="ExternalOutput").ap() for name in taps}

    def finish():
        emit_loads()
        for name in taps:
            P.wait("y", s_ve, s_ve.n)
            P.wait("y", s_sc, s_sc.n)
            P.wait("y", s_pe, s_pe.n)
            P.add("y", lambda y, _d=dtaps[name], _a=tapreg[name][0][:, :]:
                  y.dma_start(out=_d, in_=_a), s_s, 16)
        if s_s.n:
            P.wait("y", s_s, s_s.n)
        for _sm in (s_od[0], s_od[1], s_op[0], s_op[1]):
            if _sm.n:
                P.wait("y", _sm, _sm.n)
        with nc.Block() as block:
            block.tensor(lambda t: P.run(t, "t"))
            block.scalar(lambda s: P.run(s, "s"))
            block.vector(lambda v: P.run(v, "v"))
            block.gpsimd(lambda g: P.run(g, "g"))
            block.sync(lambda y: P.run(y, "y"))
        es.close()
        return nc

    emit_loads()
    P.add("g", lambda g: g.memset(ones_bf, 1.0), s_gp)
    P.add("g", lambda g: g.memset(cst[:, 644:708], 1.0), s_gp)
    P.add("g", lambda g: g.memset(ones_f, 1.0), s_gp)
    P.add("g", lambda g: g.memset(vt3[:, :, 64:65], 1.0), s_gp)
    GP_MEMSETS = s_gp.n

    P.add("v", lambda v: v.memset(pS[:, :], 0.0), s_ve)
    P.wait("v", s_pre, PRE_T)
    c1 = float(1.0 / (64.0 * np.sqrt(2048.0)))
    c2 = float(1.0 / np.sqrt(8192.0))
    P.add("v", lambda v: v.tensor_scalar(out=su_eff, in0=su_eff, scalar1=c1,
                                         scalar2=None, op0=ALU.mult), s_ve)
    P.add("v", lambda v: v.tensor_tensor(out=cu_eff, in0=cu_eff, in1=wsu_raw,
                                         op=ALU.mult), s_ve)
    P.add("v", lambda v: v.tensor_scalar(out=cu_eff, in0=cu_eff, scalar1=c2,
                                         scalar2=None, op0=ALU.mult), s_ve)
    P.add("v", lambda v: v.tensor_scalar(out=wsg64, in0=wsg64, scalar1=64.0,
                                         scalar2=None, op0=ALU.mult), s_ve)
    VE_SCALES = P.add("v", lambda v: v.tensor_scalar(out=wsd64, in0=wsd64, scalar1=64.0,
                                                     scalar2=None, op0=ALU.mult), s_ve)

    evac_of = {}
    kv_plan = ([(f"wk{m}", m) for m in range(4)] +
               [(f"wv{m}", m) for m in range(4)])
    tb_evac = {}
    kbf_evac = {}

    P.wait("t", s_pre, PRE_T)
    for nb, (lab, m) in enumerate(kv_plan):
        ps = pA if nb % 2 == 0 else pB
        P.wait("t", *g_target[lab])
        if nb >= 2:
            sm, tv = evac_of[kv_plan[nb - 2][0]]
            P.wait("t", sm, tv)
        w3 = ring_slot(lab).rearrange("p (k j) -> p k j", k=16)

        def mmqkv(t, _ps=ps, _w3=w3):
            i = None
            for nh in range(2):
                for k in range(16):
                    i = t.matmul(_ps[:, nh * 512:(nh + 1) * 512], _w3[:, k, :],
                                 b0[:, k, nh * 512:(nh + 1) * 512],
                                 start=(k == 0), stop=(k == 15))
            return i
        done = P.add("t", mmqkv, s_pe)
        ring_free[ring_idx[lab]] = s_pe.n

        if lab.startswith("wk"):
            P.wait("v", s_pe, done)
            ev = P.add("v", lambda v, _m=m, _ps=ps: v.tensor_copy(
                out=kbf[:, _m, :], in_=_ps[:]), s_ve)
            evac_of[lab] = (s_ve, ev)
            kbf_evac[m] = ev
        else:
            P.wait("v", s_pe, done)
            ev = P.add("v", lambda v, _m=m, _ps=ps: v.tensor_copy(
                out=vtmp2[:, _m % 2, :], in_=_ps[:]), s_ve)
            evac_of[lab] = (s_ve, ev)
            P.wait("t", s_ve, ev)
            P.wait("t", s_pre, PRE_T)
            if m >= 1:
                P.wait("t", s_ve, tb_evac[m - 1])

            def trans(t, _m=m):
                vt = vtmp2[:, _m % 2, :]
                i = None
                for kt in range(8):
                    i = t.transpose(pT[:, kt * 128:(kt + 1) * 128],
                                    vt[:, kt * 128:(kt + 1) * 128], iden2)
                return i
            tb_done = P.add("t", trans, s_pe)
            P.wait("v", s_pe, tb_done)
            P.wait("v", s_gp, GP_MEMSETS)
            pv8 = pT[:, 0:1024].rearrange("p (a b) -> p a b", a=8)
            P.add("v", lambda v, _m=m: v.tensor_copy(
                out=vt3[:, 16 * _m:16 * _m + 8, 0:64], in_=pv8[:, :, 0:64]), s_ve)
            tb_evac[m] = P.add("v", lambda v, _m=m: v.tensor_copy(
                out=vt3[:, 16 * _m + 8:16 * _m + 16, 0:64],
                in_=pv8[:, :, 64:128]), s_ve)
        emit_loads()

    if stop_after == "ph1":
        return finish()

    q_evac = {}
    wq_state = {"m": 0, "c": 0}

    def emit_wq_chunk():
        m, c = wq_state["m"], wq_state["c"]
        if m >= 16:
            return False
        lab = f"wq{m}"
        if c == 0:
            P.wait("t", *g_target[lab])
            if m >= 1:
                P.wait("t", s_sc, q_evac[m - 1])
            else:
                P.wait("t", evac_of["wv2"][0], evac_of["wv2"][1])
        nh, k0 = c // 4, (c % 4) * 4
        w3 = ring_slot(lab).rearrange("p (k j) -> p k j", k=16)

        def chunk(t, _w3=w3, _nh=nh, _k0=k0):
            i = None
            for k in range(_k0, _k0 + 4):
                i = t.matmul(pA[:, _nh * 512:(_nh + 1) * 512], _w3[:, k, :],
                             b0[:, k, _nh * 512:(_nh + 1) * 512],
                             start=(k == 0), stop=(k == 15))
            return i
        d = P.add("t", chunk, s_pe)
        if c == 7:
            ring_free[ring_idx[lab]] = s_pe.n
            P.wait("s", s_pe, d)
            q_evac[m] = P.add("s", lambda s, _m=m: s.copy(
                out=b1[:, _m, :], in_=pA[:, :]), s_sc)
            wq_state["m"], wq_state["c"] = m + 1, 0
            emit_loads()
        else:
            wq_state["c"] = c + 1
        return True

    def ensure_wq(m):
        while wq_state["m"] <= m and (wq_state["m"] < 16):
            emit_wq_chunk()

    ehAB = [esb[:, 0:2048], scrB[:, 0:2048]]
    rbsb = scrB[:, 2048:3072]
    row_f = [big3[:, 0:2048].bitcast(F32)[64:65, 0:1024],
             big3[:, 4096:6144].bitcast(F32)[64:65, 0:1024]]
    recbf = [big3[64:65, 2048:3072], big3[64:65, 3072:4096]]
    onesblk = cst[:, 644:708]

    st = {"expA": None, "expB": None,
          "jS": 0,
          "eh": [None, None],
          "tl": {1: {"mult": None, "copyA": None, "copyB": None}},
          "pend": {1: None}}

    def c0f(qc, kt):
        return max(0, 128 * kt - 512 * qc)

    r0_f32 = big4.bitcast(F32).rearrange("p (t n) -> p t n", t=16)
    rsq0 = big3[:, 6144:10240].bitcast(F32)
    wo0_state = {"m": 0, "c": 0, "on": False, "gate": None}
    r0_done = {}
    st0_done = {}
    st0_q = []
    gsq0 = {}

    def emit_st0(upto):
        while st0_q and st0_q[0] <= upto:
            m_ = st0_q.pop(0)
            P.wait("t", s_ve, r0_done[m_])
            P.wait("t", s_gp, gsq0[m_])

            def st0mm(t, _m=m_):
                t.matmul(pA[0:1, 512:1024], ones_f[:, 0:1], r0_f32[:, _m, :],
                         start=(_m == 0), stop=(_m == 15))
                return t.matmul(pA[32:33, 512:1024], ones_f[:, 0:1],
                                rsq0[:, (_m % 4) * 512:(_m % 4) * 512 + 512],
                                start=(_m == 0), stop=(_m == 15),
                                tile_position=(0, 32))
            st0_done[m_] = P.add("t", st0mm, s_pe)

    def emit_wo0_chunk():
        m, c = wo0_state["m"], wo0_state["c"]
        if not wo0_state["on"] or m >= 16:
            return False
        lab = f"wo0_{m}"
        if c == 0:
            emit_st0(m - 3)
            P.wait("t", *g_target[lab])
            if m >= 1:
                P.wait("t", s_ve, r0_done[m - 1])
            else:
                P.wait("t", s_ve, wo0_state["gate"])
                P.wait("t", s_sc, q_evac[15])
        w3 = ring_slot(lab).rearrange("p (k j) -> p k j", k=16)

        def chunk(t, _w3=w3, _k0=c * 4):
            i_ = None
            for k in range(_k0, _k0 + 4):
                i_ = t.matmul(pA[:, 0:512], _w3[:, k, :], b2[:, k, 0:512],
                              start=(k == 0), stop=(k == 15))
            return i_
        d = P.add("t", chunk, s_pe)
        if c < 3:
            wo0_state["c"] = c + 1
            return True
        ring_free[ring_idx[lab]] = s_pe.n
        P.wait("v", s_pe, d)
        r0_done[m] = P.add("v", lambda v, _m=m: v.tensor_tensor(
            out=r0_f32[:, _m, :], in0=pA[:, 0:512], in1=b0[:, _m, 0:512],
            op=ALU.add), s_ve)
        P.wait("g", s_ve, r0_done[m])
        if m >= 4:
            P.wait("g", s_pe, st0_done[m - 4])
        gsq0[m] = P.add("g", lambda g, _m=m: g.tensor_mul(
            out=rsq0[:, (_m % 4) * 512:(_m % 4) * 512 + 512],
            in0=r0_f32[:, _m, :], in1=r0_f32[:, _m, :]), s_gp)
        st0_q.append(m)
        wo0_state["m"], wo0_state["c"] = m + 1, 0
        emit_loads()
        return True

    def emit_filler():
        if not emit_wq_chunk():
            emit_wo0_chunk()

    def emit_norm_recips(pV, rpar):
        P.add("s", lambda s, _r=rpar, _p=pV: s.activation(
            out=row_f[_r], in_=_p[64:65, 0:1024], func=AF.Ln), s_sc)
        return P.add("s", lambda s, _r=rpar: s.activation(
            out=recbf[_r], in_=row_f[_r], func=AF.Exp, scale=-1.0), s_sc)

    def emit_norm_finish(qt, qc, cast, tk, pV, rpar):
        tl = st["tl"][tk]
        qcc = slice(qc * 512, (qc + 1) * 512)
        P.wait("t", s_sc, cast)
        if tl["copyA"] is not None:
            P.wait("t", s_ve, tl["copyA"])

        def bcA(t, _p=pV, _r=rpar):
            return t.matmul(_p[64:128, 0:512], onesblk[64:65, 0:64],
                            recbf[_r][0:1, 0:512],
                            start=True, stop=True, tile_position=(64, 64))
        bA = P.add("t", bcA, s_pe)
        if tl["copyB"] is not None:
            P.wait("t", s_ve, tl["copyB"])

        def bcB(t, _p=pV, _r=rpar):
            return t.matmul(_p[64:128, 512:1024], onesblk[64:65, 0:64],
                            recbf[_r][0:1, 512:1024],
                            start=True, stop=True, tile_position=(64, 64))
        bB = P.add("t", bcB, s_pe)
        P.wait("v", s_pe, bA)
        tl["copyA"] = P.add("v", lambda v, _p=pV: v.tensor_copy(
            out=rbsb[0:64, 0:512], in_=_p[64:128, 0:512]), s_ve)
        P.add("v", lambda v, _q=qt, _c=qcc, _p=pV: v.tensor_tensor(
            out=b2[0:64, _q, _c], in0=_p[0:64, 0:512], in1=rbsb[0:64, 0:512],
            op=ALU.mult), s_ve)
        P.wait("v", s_pe, bB)
        tl["copyB"] = P.add("v", lambda v, _p=pV: v.tensor_copy(
            out=rbsb[64:128, 512:1024], in_=_p[64:128, 512:1024]), s_ve)
        tl["mult"] = P.add("v", lambda v, _q=qt, _c=qcc, _p=pV: v.tensor_tensor(
            out=b2[64:128, _q, _c], in0=_p[0:64, 512:1024],
            in1=rbsb[64:128, 512:1024], op=ALU.mult), s_ve)
        if qt == 15 and qc == 0:
            wo0_state["on"] = True
            wo0_state["gate"] = tl["mult"]

    def emit_qc_iter(p, gp, i, qc):
        qt = 4 * gp + i
        if qc == 0:
            ensure_wq(qt)
        exp_ctr = {}
        mask_ctr = {}
        exp_par = {}
        q_hat = 16 * qc + p

        def emit_pv_quad(j, last_kt):
            kt0, kt1 = 2 * j, 2 * j + 1
            for side in ("A", "B"):
                for kt in (kt0, kt1):
                    if (side, kt) in mask_ctr:
                        P.wait("t", s_ve, mask_ctr[(side, kt)])
            P.wait("t", s_sc, exp_ctr[j])
            if kt0 == 0 and st["tl"][1]["mult"] is not None:
                P.wait("t", s_ve, st["tl"][1]["mult"])
            par = exp_par[j]

            def pvq(t, _gp=gp, _qc=qc, _k0=kt0, _k1=kt1, _last=last_kt,
                    _par=par):
                i_ = None
                for kt in (_k0, _k1):
                    c0 = c0f(_qc, kt)
                    for g, eoff, off in ((2 * _gp, 0, 0), (2 * _gp + 1, 1024, 512)):
                        i_ = t.matmul(
                            pB[0:65, off + c0:off + 512],
                            vt3[:, g * 8 + kt, 0:65],
                            ehAB[_par][:, eoff + (kt % 2) * 512 + c0:
                                       eoff + (kt % 2) * 512 + 512],
                            start=(kt == 0), stop=(kt == _last))
                return i_
            d = P.add("t", pvq, s_pe)
            st["eh"][par] = d
            return d

        kts = [0, 1, 2, 3] if qc == 0 else list(range(8))
        njs = len(kts) // 2
        for j in range(njs):
            kt0, kt1 = 2 * j, 2 * j + 1
            if p == 0 and qc == 0 and j == 0:
                P.wait("t", s_sc, q_evac[qt])
                P.wait("t", s_ve, kbf_evac[gp])
                P.wait("t", s_ve, tb_evac[3])
                P.wait("t", s_gp, GP_MEMSETS)
            elif qc == 0 and j == 0:
                P.wait("t", s_sc, q_evac[qt])

            def spair(t, _gp=gp, _qt=qt, _qc=qc, _k0=kt0, _k1=kt1,
                      _qb=0, _soff=0):
                i_ = None
                for kt in (_k0, _k1):
                    c0 = c0f(_qc, kt)
                    i_ = t.matmul(
                        pS[:, _soff + (kt % 2) * 512 + c0:
                           _soff + (kt % 2) * 512 + 512],
                        kbf[_qb:_qb + 64, _gp, kt * 128:(kt + 1) * 128],
                        b1[_qb:_qb + 64, _qt, _qc * 512 + c0:(_qc + 1) * 512],
                        start=True, stop=True)
                return i_
            if st["expA"] is not None:
                P.wait("t", s_sc, st["expA"])
            sqA = P.add("t", lambda t, _f=spair: _f(t), s_pe)
            if st["expB"] is not None:
                P.wait("t", s_sc, st["expB"])
            sqB = P.add("t", lambda t, _f=spair: _f(t, _qb=64, _soff=1024), s_pe)

            if j == min(1, njs - 1) and st["pend"][1] is not None:
                emit_filler()
                emit_filler()
                emit_norm_finish(*st["pend"][1])
                st["pend"][1] = None

            emit_filler()

            c0e = c0f(qc, kt0)
            par = st["jS"] % 2
            st["jS"] += 1
            exp_par[j] = par
            if p == 0 and qc == 0 and j <= 1:
                P.wait("s", s_ve, tb_evac[3])
            if st["eh"][par] is not None:
                P.wait("s", s_pe, st["eh"][par])
            P.wait("s", s_pe, sqA)
            ecA = P.add(
                "s", lambda s, _eh=ehAB[par], _c0=c0e: s.activation(
                    out=_eh[:, _c0:1024], in_=pS[:, _c0:1024], func=AF.Exp,
                    scale=0.125), s_sc)
            P.wait("s", s_pe, sqB)
            ecB = P.add(
                "s", lambda s, _eh=ehAB[par], _c0=c0e: s.activation(
                    out=_eh[:, 1024 + _c0:2048], in_=pS[:, 1024 + _c0:2048],
                    func=AF.Exp, scale=0.125), s_sc)
            exp_ctr[j] = ecB
            exp_ctr[(j, "A")] = ecA
            st["expA"] = ecA
            st["expB"] = ecB
            for kt in (kt0, kt1):
                if 128 * kt >= 512 * qc:
                    c0 = c0f(qc, kt)
                    for side, eoff, ec in (("A", 0, ecA), ("B", 1024, ecB)):
                        P.wait("v", s_sc, ec)
                        mask_ctr[(side, kt)] = P.add(
                            "v", lambda v, _eh=ehAB[par],
                            _o=eoff + (kt % 2) * 512 + c0:
                            v.tensor_tensor(out=_eh[:, _o:_o + 128],
                                            in0=_eh[:, _o:_o + 128], in1=tri,
                                            op=ALU.mult), s_ve)

            if j >= 1:
                emit_pv_quad(j - 1, kts[-1])
        pv_last = emit_pv_quad(njs - 1, kts[-1])
        P.wait("s", s_pe, pv_last)
        st["pend"][1] = (qt, qc, emit_norm_recips(pB, q_hat % 2), 1, pB,
                         q_hat % 2)
        emit_filler()
        emit_loads()

    pairs = [(g_, i_) for g_ in range(4) for i_ in range(4)]
    for p, (gp, i) in enumerate(pairs):
        emit_qc_iter(p, gp, i, 0)
    ensure_wq(15)
    for p, (gp, i) in enumerate(pairs):
        emit_qc_iter(p, gp, i, 1)
    emit_norm_finish(*st["pend"][1])
    st["pend"][1] = None
    b2_ready = st["tl"][1]["mult"]
    while emit_wo0_chunk():
        pass
    emit_st0(15)
    SC_AB = s_sc.n

    if stop_after == "ph2":
        return finish()

    rowA = scrB[0:1, 0:1024].bitcast(F32)
    rowB = scrB[0:1, 1024:2048].bitcast(F32)
    rsq = scrB[:, 2048:4096].bitcast(F32)
    msq0 = esb[:, 0:1024].bitcast(F32)[0:1, 0:512]
    x2_last = None

    def emit_ln_tail(sum_ap, sq_ap, rX, half, cb=None):
        P.add("v", lambda v: v.tensor_scalar(out=rowA, in0=sum_ap,
                                             scalar1=float(1.0 / D), scalar2=None,
                                             op0=ALU.mult), s_ve)
        P.add("v", lambda v: v.tensor_tensor(out=rowB, in0=rowA, in1=rowA,
                                             op=ALU.mult), s_ve)
        stats_read = P.add("v", lambda v: v.tensor_scalar(
            out=msq0, in0=sq_ap, scalar1=float(1.0 / D), scalar2=None,
            op0=ALU.mult), s_ve)
        vv_done = P.add("v", lambda v: v.tensor_tensor(
            out=rowB, in0=msq0, in1=rowB, op=ALU.subtract), s_ve)
        vv_done = P.add("v", lambda v: v.tensor_scalar(
            out=rowB, in0=rowB, scalar1=float(EPS), scalar2=None, op0=ALU.add), s_ve)
        P.wait("s", s_ve, vv_done)
        P.add("s", lambda s: s.activation(out=rowB, in_=rowB, func=AF.Ln), s_sc)
        a_done = P.add("s", lambda s: s.activation(out=rowB, in_=rowB,
                                                   func=AF.Exp, scale=-0.5), s_sc)
        P.wait("v", s_sc, a_done)
        P.add("v", lambda v: v.tensor_tensor(out=rowA, in0=rowA, in1=rowB,
                                             op=ALU.mult), s_ve)
        b_done = P.add("v", lambda v: v.tensor_scalar(
            out=rowA, in0=rowA, scalar1=-1.0, scalar2=None, op0=ALU.mult), s_ve)
        P.wait("t", s_ve, b_done)

        def bcln(t):
            t.matmul(pD[:, 0:512], ones_f[0:1, 0:128], rowB,
                     start=True, stop=True)
            return t.matmul(pD[:, 512:1024], ones_f[0:1, 0:128], rowA,
                            start=True, stop=True)
        bc1 = P.add("t", bcln, s_pe)

        x2t = rsq[:, 512:1024]
        P.wait("v", s_pe, bc1)
        last = None
        for m in range(16):
            P.add("v", lambda v, _m=m, _r=rX: v.tensor_tensor(
                out=x2t, in0=_r[:, _m, :], in1=pD[:, 0:512],
                op=ALU.mult), s_ve)
            last = P.add("v", lambda v, _m=m, _h=half: v.tensor_tensor(
                out=b3[:, _m, _h * 512:(_h + 1) * 512], in0=x2t,
                in1=pD[:, 512:1024], op=ALU.add), s_ve)
            if cb is not None:
                cb(m, last)
        return stats_read, last

    P.wait("v", s_pe, st0_done[15])
    P.wait("t", s_sc, SC_AB)
    LN0_PA_READ, x2_h0_last = emit_ln_tail(
        pA[0:1, 512:1024], pA[32:33, 512:1024], r0_f32, 0)

    xp0 = esb.rearrange("p (a n) -> p a n", a=4)
    ph40 = {"t": 0, "j": 0, "on": False}
    z1h0 = {}
    z1ev0 = {}
    z1out0 = {}
    xpd0 = {}
    x1rd0 = {}
    h16x0 = {}
    x1ev0 = {}

    def emit_ph40_chunk():
        if not ph40["on"]:
            return False
        t_i, j = ph40["t"], ph40["j"]
        if t_i < 16:
            if t_i >= 2:
                P.wait("v", s_pe, z1h0[t_i - 2])
            xpd0[t_i] = P.add("v", lambda v, _t=t_i: v.tensor_scalar(
                out=xp0[:, _t % 2, :], in0=b3[:, _t, 0:512],
                scalar1=su_eff[:, _t:_t + 1], scalar2=None, op0=ALU.mult), s_ve)
            P.wait("t", s_ve, xpd0[t_i])
            if t_i == 0:
                P.wait("t", s_ve, b2_ready)
                P.wait("t", s_pre, PRE_T)
            if t_i >= 2:
                P.wait("t", s_sc, z1ev0[t_i - 2])
            z1h0[t_i] = P.add("t", lambda t, _t=t_i: t.matmul(
                pB[:, (_t % 2) * 512:(_t % 2) * 512 + 512], hadU,
                xp0[:, _t % 2, :], start=True, stop=True), s_pe)
            P.wait("s", s_pe, z1h0[t_i])
            jj = t_i // 2
            if jj >= 2 and t_i % 2 == 0:
                P.wait("s", s_z[jj % 2], z1out0[jj - 2])
            z1ev0[t_i] = P.add("s", lambda s, _t=t_i: s.copy(
                out=zb6[:, (_t // 2 % 2) * 2 + (_t % 2), 0:512],
                in_=pB[:, (_t % 2) * 512:(_t % 2) * 512 + 512]), s_sc)
            if t_i % 2 == 1:
                P.wait("y", s_sc, z1ev0[t_i])
                z1out0[jj] = P.add("y", lambda y, _j=jj: y.dma_start(
                    out=dz[2 * _j:2 * _j + 2].rearrange("t p n -> p t n")[:, :, 0:512],
                    in_=zb6[:, (_j % 2) * 2:(_j % 2) * 2 + 2, 0:512]), s_z[jj % 2], 16)
            ph40["t"] = t_i + 1
            return True
        if j < 8:
            if j == 0:
                P.wait("y", s_z[0], z1out0[6])
                P.wait("y", s_z[1], z1out0[7])
            if j >= 2:
                P.wait("y", s_pe, h16x0[j - 2])
            x1rd0[j] = P.add("y", lambda y, _j=j: y.dma_start(
                out=zb6[:, (_j % 2) * 2:(_j % 2) * 2 + 2, 0:512],
                in_=dz16[:, 2 * _j:2 * _j + 2, 0:512]), s_x[j % 2], 16)
            P.wait("t", s_x[j % 2], x1rd0[j])
            if j == 0:
                P.wait("t", s_pre, PRE_T)
                P.wait("t", s_sc, z1ev0[15])
            else:
                P.wait("t", s_sc, x1ev0[2 * (j - 1) + 1])

            def h16(t, _j=j):
                i_ = None
                for uu in range(2):
                    i_ = t.matmul(pB[:, uu * 512:(uu + 1) * 512], hadA,
                                  zb6[:, (_j % 2) * 2 + uu, 0:512],
                                  start=True, stop=True)
                return i_
            h16x0[j] = P.add("t", h16, s_pe)
            for uu in range(2):
                P.wait("s", s_pe, h16x0[j])
                x1ev0[2 * j + uu] = P.add("s", lambda s, _j=j, _uu=uu: s.copy(
                    out=b4[:, 2 * _j + _uu, 0:512],
                    in_=pB[:, _uu * 512:(_uu + 1) * 512]), s_sc)
            ph40["j"] = j + 1
            return True
        return False

    dz16 = dz[0:16].rearrange("t (b u) n -> (t b) u n", u=16)
    ph40["on"] = True

    o_done = {}
    r_done = {}
    sq_done = {}
    st_done = {}
    for m in range(17):
        if m < 16:
            lab = f"wo1_{m}"
            P.wait("t", *g_target[lab])
            if m >= 1:
                P.wait("t", s_ve, r_done[m - 1])
            else:
                P.wait("t", s_ve, LN0_PA_READ)
            w3 = ring_slot(lab).rearrange("p (k j) -> p k j", k=16)

            def omm(t, _w3=w3):
                i = None
                for k in range(16):
                    i = t.matmul(pA[:, 0:512], _w3[:, k, :],
                                 b2[:, k, 512:1024],
                                 start=(k == 0), stop=(k == 15))
                return i
            o_done[m] = P.add("t", omm, s_pe)
            ring_free[ring_idx[lab]] = s_pe.n

            P.wait("v", s_pe, o_done[m])
            r_done[m] = P.add("v", lambda v, _m=m: v.tensor_tensor(
                out=r_f32[:, _m, :], in0=pA[:, 0:512],
                in1=b0[:, _m, 512:1024], op=ALU.add), s_ve)

            P.wait("s", s_ve, r_done[m])
            if m >= 2:
                P.wait("s", s_pe, st_done[m - 2])
            sq_done[m] = P.add("s", lambda s, _m=m: s.activation(
                out=rsq[:, (_m % 2) * 512:(_m % 2) * 512 + 512],
                in_=r_f32[:, _m, :], func=AF.Square), s_sc)
        if m >= 1:
            mm_ = m - 1
            P.wait("t", s_ve, r_done[mm_])
            P.wait("t", s_sc, sq_done[mm_])

            def stmm(t, _m=mm_):
                t.matmul(pC[0:1, 0:512], ones_f[:, 0:1], r_f32[:, _m, :],
                         start=(_m == 0), stop=(_m == 15))
                return t.matmul(pC[0:1, 512:1024], ones_f[:, 0:1],
                                rsq[:, (_m % 2) * 512:(_m % 2) * 512 + 512],
                                start=(_m == 0), stop=(_m == 15))
            st_done[mm_] = P.add("t", stmm, s_pe)
        emit_ph40_chunk()
        emit_ph40_chunk()
        emit_loads()
    while emit_ph40_chunk():
        pass

    xp1 = scrB.rearrange("p (a n) -> p a n", a=8)
    z1h1 = {}
    z1ev1 = {}
    z1out1 = {}

    def ph41_cb(t_i, x2_ctr):
        if t_i >= 2:
            P.wait("v", s_pe, z1h1[t_i - 2])
        xpd = P.add("v", lambda v, _t=t_i: v.tensor_scalar(
            out=xp1[:, _t % 2, :], in0=b3[:, _t, 512:1024],
            scalar1=su_eff[:, _t:_t + 1], scalar2=None, op0=ALU.mult), s_ve)
        P.wait("t", s_ve, xpd)
        if t_i >= 2:
            P.wait("t", s_sc, z1ev1[t_i - 2])
        z1h1[t_i] = P.add("t", lambda t, _t=t_i: t.matmul(
            pA[:, (_t % 2) * 512:(_t % 2) * 512 + 512], hadU,
            xp1[:, _t % 2, :], start=True, stop=True), s_pe)
        P.wait("s", s_pe, z1h1[t_i])
        jj = t_i // 2
        if jj >= 2 and t_i % 2 == 0:
            P.wait("s", s_z[jj % 2], z1out1[jj - 2])
        z1ev1[t_i] = P.add("s", lambda s, _t=t_i: s.copy(
            out=zb6[:, (_t // 2 % 2) * 2 + (_t % 2), 512:1024],
            in_=pA[:, (_t % 2) * 512:(_t % 2) * 512 + 512]), s_sc)
        if t_i % 2 == 1:
            P.wait("y", s_sc, z1ev1[t_i])
            z1out1[jj] = P.add("y", lambda y, _j=jj: y.dma_start(
                out=dz[2 * _j:2 * _j + 2].rearrange("t p n -> p t n")[:, :, 512:1024],
                in_=zb6[:, (_j % 2) * 2:(_j % 2) * 2 + 2, 512:1024]), s_z[jj % 2], 16)

    P.wait("v", s_pe, st_done[15])
    P.wait("t", s_ve, x2_h0_last)
    _, x2_last = emit_ln_tail(pC[0:1, 0:512], pC[0:1, 512:1024], r_f32, 1,
                              cb=ph41_cb)

    x1rd1 = {}
    h16x1 = {}
    x1ev1 = {}
    for j in range(8):
        if j == 0:
            P.wait("y", s_z[0], z1out1[6])
            P.wait("y", s_z[1], z1out1[7])
        if j >= 2:
            P.wait("y", s_pe, h16x1[j - 2])
        x1rd1[j] = P.add("y", lambda y, _j=j: y.dma_start(
            out=zb6[:, (_j % 2) * 2:(_j % 2) * 2 + 2, 512:1024],
            in_=dz16[:, 2 * _j:2 * _j + 2, 512:1024]), s_x[j % 2], 16)
        P.wait("t", s_x[j % 2], x1rd1[j])
        if j == 0:
            P.wait("t", s_sc, z1ev1[15])
        else:
            P.wait("t", s_sc, x1ev1[2 * (j - 1) + 1])

        def h16b(t, _j=j):
            i_ = None
            for uu in range(2):
                i_ = t.matmul(pA[:, uu * 512:(uu + 1) * 512], hadA,
                              zb6[:, (_j % 2) * 2 + uu, 512:1024],
                              start=True, stop=True)
            return i_
        h16x1[j] = P.add("t", h16b, s_pe)
        for uu in range(2):
            P.wait("s", s_pe, h16x1[j])
            x1ev1[2 * j + uu] = P.add("s", lambda s, _j=j, _uu=uu: s.copy(
                out=b4[:, 2 * _j + _uu, 512:1024],
                in_=pA[:, _uu * 512:(_uu + 1) * 512]), s_sc)
        emit_loads()
    X1_READY = x1ev1[15]

    if stop_after == "ph3":
        return finish()

    silu_f = scrB[:, 0:2048].bitcast(F32)
    htmp_f = scrB[:, 2048:4096].bitcast(F32)
    hbf = vtbuf[:, 0:4096].rearrange("p (a n) -> p a n", a=4)
    zev = {}
    zout = {}
    hevac = {}
    zh = {}
    silu_done = {}

    def emit_z(m):
        psGz = pC if m % 2 == 0 else pD
        P.wait("t", s_ve, hevac[m])
        P.wait("t", s_sc, silu_done[m])

        def zmm(t, _ps=psGz, _m=m):
            i = None
            for (c0, c1) in [(0, 512), (512, 1024)]:
                i = t.matmul(_ps[:, c0:c1], hadU, hbf[:, _m % 2, c0:c1],
                             start=True, stop=True)
            return i
        zh[m] = P.add("t", zmm, s_pe)
        P.wait("s", s_pe, zh[m])
        j = m // 2
        if j >= 2 and m % 2 == 0:
            P.wait("s", s_z[j % 2], zout[j - 2])
        zev[m] = P.add("s", lambda s, _ps=psGz, _m=m: s.copy(
            out=zb6[:, (_m // 2 % 2) * 2 + (_m % 2), :], in_=_ps[:]), s_sc)
        if m % 2 == 1:
            P.wait("y", s_sc, zev[m])
            zout[j] = P.add("y", lambda y, _j=j: y.dma_start(
                out=dz[2 * _j:2 * _j + 2].rearrange("t p n -> p t n"),
                in_=zb6[:, (_j % 2) * 2:(_j % 2) * 2 + 2, :]), s_z[j % 2], 16)

    for m in range(64):
        upl, gal = f"wu{m}", f"wg{m}"
        psU = pA if m % 2 == 0 else pB
        psG = pC if m % 2 == 0 else pD
        P.wait("t", *g_target[upl])
        if m == 0:
            P.wait("t", s_sc, X1_READY)
        if m >= 2:
            P.wait("t", s_ve, hevac[m - 2])

        def upmm(t, _ps=psU, _lab=upl):
            w3 = ring_slot(_lab).rearrange("p (k j) -> p k j", k=16)
            i = None
            for nh in range(2):
                for k in range(16):
                    i = t.matmul(_ps[:, nh * 512:(nh + 1) * 512], w3[:, k, :],
                                 b4[:, k, nh * 512:(nh + 1) * 512],
                                 start=(k == 0), stop=(k == 15))
            return i
        up_done = P.add("t", upmm, s_pe)
        ring_free[ring_idx[upl]] = s_pe.n

        P.wait("t", *g_target[gal])
        if m >= 2:
            P.wait("t", s_sc, zev[m - 2])

        def gamm(t, _ps=psG, _lab=gal):
            w3 = ring_slot(_lab).rearrange("p (k j) -> p k j", k=16)
            i = None
            for nh in range(2):
                for k in range(16):
                    i = t.matmul(_ps[:, nh * 512:(nh + 1) * 512], w3[:, k, :],
                                 b4[:, k, nh * 512:(nh + 1) * 512],
                                 start=(k == 0), stop=(k == 15))
            return i
        ga_done = P.add("t", gamm, s_pe)
        ring_free[ring_idx[gal]] = s_pe.n

        if m >= 1:
            emit_z(m - 1)

        P.wait("s", s_pe, ga_done)
        if m >= 1:
            P.wait("s", s_ve, hevac[m - 1])
        silu_done[m] = P.add("s", lambda s, _ps=psG, _m=m: s.activation(
            out=silu_f[:, :], in_=_ps[:], func=AF.Silu, scale=wsg64[:, _m:_m + 1]),
            s_sc)

        P.wait("v", s_sc, silu_done[m])
        P.wait("v", s_pe, up_done)
        if m >= 2:
            P.wait("v", s_pe, zh[m - 2])
        P.add("v", lambda v, _ps=psU: v.tensor_tensor(
            out=htmp_f[:, :], in0=silu_f[:, :], in1=_ps[:], op=ALU.mult), s_ve)
        hevac[m] = P.add("v", lambda v, _m=m: v.tensor_scalar(
            out=hbf[:, _m % 2, :], in0=htmp_f[:, :], scalar1=cu_eff[:, _m:_m + 1],
            scalar2=None, op0=ALU.mult), s_ve)
        emit_loads()
    emit_z(63)

    if stop_after == "ph5":
        return finish()

    dz64 = dz.rearrange("t (b v) n -> (t b) v n", v=64)
    brd = {}
    bh = {}
    bh2 = {}
    bev = {}
    bev_s = {}
    bufA4 = scrA[:, 0:4096].rearrange("p (a n) -> p a n", a=4)
    bufB4 = scrB[:, 0:4096].rearrange("p (a n) -> p a n", a=4)
    for j in range(16):
        buf = bufA4 if j % 2 == 0 else bufB4
        if j == 0:
            P.wait("y", s_z[0], zout[30])
            P.wait("y", s_z[1], zout[31])
        if j == 1:
            P.wait("y", s_ve, hevac[63])
        if j >= 2:
            P.wait("y", s_pe, bh2[j - 2])
        brd[j] = P.add("y", lambda y, _j=j, _b=buf: y.dma_start(
            out=_b[:, :, :], in_=dz64[:, 4 * _j:4 * _j + 4, :]), s_bd[j % 2], 16)
        P.wait("t", s_bd[j % 2], brd[j])
        P.wait("t", s_pre, PRE_T)
        if j >= 1:
            P.wait("t", s_ve, bev[4 * (j - 1) + 1])
        def h64a(t, _b=buf):
            i = None
            for vv, ps in ((0, pA), (1, pC)):
                for (c0, c1) in [(0, 512), (512, 1024)]:
                    i = t.matmul(ps[:, c0:c1], hadB, _b[:, vv, c0:c1],
                                 start=True, stop=True)
            return i
        bh[j] = P.add("t", h64a, s_pe)
        if j >= 1:
            P.wait("t", s_sc, bev_s[4 * (j - 1) + 3])
        def h64b(t, _b=buf):
            i = None
            for vv, ps in ((2, pB), (3, pD)):
                for (c0, c1) in [(0, 512), (512, 1024)]:
                    i = t.matmul(ps[:, c0:c1], hadB, _b[:, vv, c0:c1],
                                 start=True, stop=True)
            return i
        bh2[j] = P.add("t", h64b, s_pe)
        P.wait("v", s_pe, bh[j])
        for vv, ps in ((0, pA), (1, pC)):
            bev[4 * j + vv] = P.add("v", lambda v, _j=j, _vv=vv, _ps=ps: v.tensor_copy(
                out=h2B(4 * _j + _vv), in_=_ps[:]), s_ve)
        P.wait("s", s_pe, bh2[j])
        for vv, ps in ((2, pB), (3, pD)):
            bev_s[4 * j + vv] = P.add("s", lambda s, _j=j, _vv=vv, _ps=ps: s.copy(
                out=h2B(4 * _j + _vv), in_=_ps[:]), s_sc)
        emit_loads()

    if stop_after == "ph6":
        return finish()

    rowA2 = scrB[0:1, 0:2048].bitcast(F32)
    rowB2 = scrB[0:1, 2048:4096].bitcast(F32)
    rsq2 = scrB[:, 0:2048]
    dn_done = {}
    r2_done = {}
    sq2 = {}
    st2 = {}
    for m in range(17):
        if m < 16:
            psDn = pA if m % 2 == 0 else pB
            if m == 0:
                P.wait("t", s_ve, bev[61])
                P.wait("t", s_sc, bev_s[63])
            if m >= 2:
                P.wait("t", s_ve, r2_done[m - 2])
            for kq in range(4):
                lab = f"wd{m}_{kq}"
                P.wait("t", *g_target[lab])

                def dnmm(t, _lab=lab, _ps=psDn, _kq=kq):
                    w3 = ring_slot(_lab).rearrange("p (k j) -> p k j", k=16)
                    i = None
                    for k in range(16):
                        vv = _kq * 16 + k
                        for nh in range(2):
                            i = t.matmul(_ps[:, nh * 512:(nh + 1) * 512], w3[:, k, :],
                                         h2B(vv)[:, nh * 512:(nh + 1) * 512],
                                         start=(_kq == 0 and k == 0),
                                         stop=(_kq == 3 and k == 15))
                    return i
                dn_done[m] = P.add("t", dnmm, s_pe)
                ring_free[ring_idx[lab]] = s_pe.n
            P.wait("s", s_pe, dn_done[m])
            scl = P.add("s", lambda s, _ps=psDn, _m=m: s.activation(
                out=_ps[:], in_=_ps[:], func=AF.Copy, scale=wsd64[:, _m:_m + 1]), s_sc)
            P.wait("v", s_sc, scl)
            r2_done[m] = P.add("v", lambda v, _ps=psDn, _m=m: v.tensor_tensor(
                out=b3[:, _m, :], in0=_ps[:], in1=b3[:, _m, :], op=ALU.add), s_ve)
            P.wait("s", s_ve, r2_done[m])
            if m >= 2:
                P.wait("s", s_pe, st2[m - 2])
            sq2[m] = P.add("s", lambda s, _m=m: s.activation(
                out=rsq2[:, (_m % 2) * 1024:(_m % 2 + 1) * 1024], in_=b3[:, _m, :],
                func=AF.Square), s_sc)
        if m >= 1:
            mm_ = m - 1
            P.wait("t", s_ve, r2_done[mm_])
            P.wait("t", s_sc, sq2[mm_])

            def st2mm(t, _m=mm_):
                i = None
                for (c0, c1) in [(0, 512), (512, 1024)]:
                    t.matmul(pC[0:1, c0:c1], ones_bf, b3[:, _m, c0:c1],
                             start=(_m == 0), stop=(_m == 15))
                    i = t.matmul(pD[0:1, c0:c1], ones_bf,
                                 rsq2[:, (_m % 2) * 1024 + c0:(_m % 2) * 1024 + c1],
                                 start=(_m == 0), stop=(_m == 15))
                return i
            st2[mm_] = P.add("t", st2mm, s_pe)
        emit_loads()

    P.wait("v", s_pe, st2[15])
    P.add("v", lambda v: v.tensor_scalar(out=rowA2, in0=pC[0:1, :],
                                         scalar1=float(1.0 / D), scalar2=None,
                                         op0=ALU.mult), s_ve)
    P.add("v", lambda v: v.tensor_tensor(out=rowB2, in0=rowA2, in1=rowA2,
                                         op=ALU.mult), s_ve)
    P.add("v", lambda v: v.tensor_scalar(out=pD[0:1, :], in0=pD[0:1, :],
                                         scalar1=float(1.0 / D), scalar2=None,
                                         op0=ALU.mult), s_ve)
    vv2 = P.add("v", lambda v: v.tensor_tensor(out=rowB2, in0=pD[0:1, :],
                                               in1=rowB2, op=ALU.subtract), s_ve)
    vv2 = P.add("v", lambda v: v.tensor_scalar(
        out=rowB2, in0=rowB2, scalar1=float(EPS), scalar2=None, op0=ALU.add), s_ve)
    P.wait("s", s_ve, vv2)
    P.add("s", lambda s: s.activation(out=rowB2, in_=rowB2, func=AF.Ln), s_sc)
    a2_done = P.add("s", lambda s: s.activation(out=rowB2, in_=rowB2,
                                                func=AF.Exp, scale=-0.5), s_sc)
    P.wait("v", s_sc, a2_done)
    P.add("v", lambda v: v.tensor_tensor(out=rowA2, in0=rowA2, in1=rowB2,
                                         op=ALU.mult), s_ve)
    b2_done = P.add("v", lambda v: v.tensor_scalar(out=rowA2, in0=rowA2,
                                                   scalar1=-1.0, scalar2=None,
                                                   op0=ALU.mult), s_ve)
    P.wait("t", s_ve, b2_done)

    def bcln2(t):
        i = None
        for (c0, c1) in [(0, 512), (512, 1024)]:
            t.matmul(pC[:, c0:c1], ones_f[0:1, 0:128], rowB2[0:1, c0:c1],
                     start=True, stop=True)
            i = t.matmul(pD[:, c0:c1], ones_f[0:1, 0:128], rowA2[0:1, c0:c1],
                         start=True, stop=True)
        return i
    bc2 = P.add("t", bcln2, s_pe)

    outst = scrB.bitcast(F32)
    outg = big1.bitcast(F32)
    a_sb = vtbuf[:, 0:2048].bitcast(F32)
    b_sb = vtbuf[:, 2048:4096].bitcast(F32)
    out_dma = {}
    P.wait("v", s_pe, bc2)
    P.add("v", lambda v: v.tensor_copy(out=a_sb, in_=pC[:]), s_ve)
    ab_done = P.add("v", lambda v: v.tensor_copy(out=b_sb, in_=pD[:]), s_ve)
    P.wait("g", s_ve, ab_done)
    pool_set = (1, 4, 7, 10, 13, 15)
    prev_v, prev_g = {}, {}
    nv = ng = 0
    for m in range(16):
        if m not in pool_set:
            sl = slice((nv % 2) * 1024, (nv % 2 + 1) * 1024)
            if nv >= 2:
                P.wait("v", s_od[nv % 2], out_dma[prev_v[nv % 2]])
            P.add("v", lambda v, _m=m, _s=sl: v.tensor_tensor(
                out=outst[:, _s], in0=b3[:, _m, :], in1=pC[:], op=ALU.mult), s_ve)
            od = P.add("v", lambda v, _m=m, _s=sl: v.tensor_tensor(
                out=outst[:, _s], in0=outst[:, _s], in1=pD[:], op=ALU.add), s_ve)
            P.wait("y", s_ve, od)
            out_dma[m] = P.add("y", lambda y, _m=m, _s=sl: y.dma_start(
                out=dy[_m], in_=outst[:, _s]), s_od[nv % 2], 16)
            prev_v[nv % 2] = m
            nv += 1
        else:
            sl = slice((ng % 2) * 1024, (ng % 2 + 1) * 1024)
            if ng >= 2:
                P.wait("g", s_op[ng % 2], out_dma[prev_g[ng % 2]])
            gm = P.add("g", lambda g, _m=m, _s=sl: g.tensor_tensor(
                out=outg[:, _s], in0=b3[:, _m, :], in1=a_sb, op=ALU.mult), s_gp)
            P.wait("g", s_gp, gm)
            od = P.add("g", lambda g, _m=m, _s=sl: g.tensor_tensor(
                out=outg[:, _s], in0=outg[:, _s], in1=b_sb, op=ALU.add), s_gp)
            P.wait("s", s_gp, od)
            out_dma[m] = P.add("s", lambda s, _m=m, _s=sl: s.dma_start(
                out=dy[_m], in_=outg[:, _s]), s_op[ng % 2], 16)
            prev_g[ng % 2] = m
            ng += 1
    return finish()



def kernel(**inputs):
    g, shards = host_prepare(inputs)
    nc = build_program()
    shared = {k: g[k] for k in ("wq", "wk", "wv", "wo", "wu", "wg", "wd",
                                "cstall", "fall")}
    in_maps = [dict(shared, x=shards[c]) for c in range(NCORES)]
    res = run_bass_kernel_spmd(nc, in_maps, core_ids=list(range(NCORES)))
    out = np.empty((8 * T, D), dtype=np.float32)
    for c in range(NCORES):
        yc = np.asarray(res.results[c]["y"])
        out[c * T:(c + 1) * T, :] = yc.reshape(D, T).T
    return out.reshape(4, 2048, 2048)


# version 39
# speedup vs baseline: 1.2195x; 1.0041x over previous
import numpy as np
import concourse.bass as bass
import concourse.mybir as mybir
from concourse.bass_utils import run_bass_kernel_spmd
from contextlib import ExitStack

F32 = mybir.dt.float32
BF16 = mybir.dt.bfloat16
AF = mybir.ActivationFunctionType
ALU = mybir.AluOpType

NCORES = 8
T = 1024
D = 2048
INTER = 8192
NH = 32
EPS = 1e-5
RING_N = 4
RING_W = 2048


def _hadamard(n):
    h = np.array([[1.0]], dtype=np.float32)
    while h.shape[0] < n:
        h = np.block([[h, h], [h, -h]])
    return np.ascontiguousarray(h, dtype=np.float32)


def _perm_q():
    p = np.empty(D, dtype=np.int64)
    for h in range(NH):
        g = h // 4
        tile = (g // 2) * 4 + (h % 4)
        half = g % 2
        r0 = tile * 128 + half * 64
        p[r0:r0 + 64] = np.arange(h * 64, h * 64 + 64)
    return p


def _mk_blocks(W, out_perm=None, in_perm=None):
    W2 = np.asarray(W, dtype=np.float32)
    if out_perm is not None:
        W2 = W2[out_perm]
    if in_perm is not None:
        W2 = W2[:, in_perm]
    O, I = W2.shape
    Mt, Kt = O // 128, I // 128
    a = W2.reshape(Mt, 128, Kt, 128).transpose(0, 3, 2, 1)
    return np.ascontiguousarray(a.reshape(Mt, 128, Kt * 128), dtype=np.float32)


def _vec_tiles(v, nt):
    return np.ascontiguousarray(np.asarray(v, dtype=np.float32).reshape(nt, 128).T)


def host_prepare(inputs):
    g = {}
    cu = np.asarray(inputs["L0_attn_cu_seqlens_q"]).ravel().tolist()
    seg_lens = [int(cu[i + 1] - cu[i]) for i in range(len(cu) - 1)]
    if cu[0] != 0 or any(sl != T for sl in seg_lens):
        raise NotImplementedError(f"kernel built for segments of {T}; got cu_seqlens={cu}")
    x = np.ascontiguousarray(
        np.asarray(inputs["L0_attn_hidden_states"], dtype=np.float32).reshape(8 * T, D))
    pq = _perm_q()
    idx = np.arange(D)
    pu = (idx & 127) * 16 + (idx >> 7)
    idx8 = np.arange(INTER)
    pd = (idx8 & 127) * 64 + (idx8 >> 7)

    g["wq"] = _mk_blocks(inputs["L0_attn_Wq"], out_perm=pq)
    g["wk"] = _mk_blocks(inputs["L0_attn_Wk"])
    g["wv"] = _mk_blocks(inputs["L0_attn_Wv"])
    g["wo"] = _mk_blocks(inputs["L0_attn_Wo"], in_perm=pq)
    g["wu"] = _mk_blocks(inputs["L0_ffn_W_up"], in_perm=pu)
    g["wg"] = _mk_blocks(inputs["L0_ffn_W_gate"], in_perm=pu)
    wd = _mk_blocks(inputs["L0_ffn_W_down"], in_perm=pd)
    g["wd"] = np.ascontiguousarray(wd.reshape(16, 128, 4, 2048).transpose(0, 2, 1, 3))

    wsug = np.asarray(inputs["L0_ffn_Wscale_ug"], np.float32)
    g["fall"] = np.ascontiguousarray(np.concatenate([
        _vec_tiles(inputs["L0_ffn_SU_ug"], 16),
        _vec_tiles(inputs["L0_ffn_SU_dp"], 64),
        _vec_tiles(wsug[:INTER], 64),
        _vec_tiles(wsug[INTER:], 64),
        _vec_tiles(inputs["L0_ffn_Wscale_dp"], 16)], axis=1))

    kk, qq = np.meshgrid(np.arange(128), np.arange(128), indexing="ij")
    g["cstall"] = np.ascontiguousarray(np.concatenate([
        _hadamard(128),
        np.kron(_hadamard(16), np.eye(8, dtype=np.float32)),
        np.kron(_hadamard(64), np.eye(2, dtype=np.float32)),
        (kk <= qq).astype(np.float32),
        np.eye(128, dtype=np.float32)], axis=1))

    shards = [np.ascontiguousarray(x[c * T:(c + 1) * T, :].T.reshape(16, 128, T))
              for c in range(NCORES)]
    return g, shards



class Sem:
    def __init__(self, es, nc, name):
        self.h = es.enter_context(nc.semaphore(name))
        self.n = 0


class Prog:
    KEYS = ("t", "s", "v", "g", "y")
    SELF_ORDER = ("s", "v")

    def __init__(self):
        self.ops = {k: [] for k in self.KEYS}
        self.last = {}

    def add(self, eng, fn, sem=None, k=1):
        if eng in self.SELF_ORDER and eng in self.last:
            lsem, lval = self.last[eng]
            self.ops[eng].append(
                (lambda e, _s=lsem, _v=lval: e.wait_ge(_s, _v), None, 0))
        self.ops[eng].append((fn, sem.h if sem is not None else None, k))
        if sem is not None:
            sem.n += k
            if eng in self.SELF_ORDER:
                self.last[eng] = (sem.h, sem.n)
            return sem.n

    def wait(self, eng, sem, val):
        if val is None:
            raise RuntimeError("wait target is None")
        self.ops[eng].append(
            (lambda e, _s=sem.h, _v=val: e.wait_ge(_s, _v), None, 0))

    def run(self, handle, key):
        for fn, semh, k in self.ops[key]:
            inst = fn(handle)
            if semh is not None:
                inst.then_inc(semh, k)


def build_program(stop_after="all", taps=()):
    nc = bass.Bass("TRN2", target_bir_lowering=False, debug=False)
    es = ExitStack()

    dx = nc.dram_tensor("x", [16, 128, T], F32, kind="ExternalInput").ap()
    dwq = nc.dram_tensor("wq", [16, 128, 2048], F32, kind="ExternalInput").ap()
    dwk = nc.dram_tensor("wk", [4, 128, 2048], F32, kind="ExternalInput").ap()
    dwv = nc.dram_tensor("wv", [4, 128, 2048], F32, kind="ExternalInput").ap()
    dwo = nc.dram_tensor("wo", [16, 128, 2048], F32, kind="ExternalInput").ap()
    dwu = nc.dram_tensor("wu", [64, 128, 2048], F32, kind="ExternalInput").ap()
    dwg = nc.dram_tensor("wg", [64, 128, 2048], F32, kind="ExternalInput").ap()
    dwd = nc.dram_tensor("wd", [16, 4, 128, 2048], F32, kind="ExternalInput").ap()
    dcst = nc.dram_tensor("cstall", [128, 640], F32, kind="ExternalInput").ap()
    dfm = nc.dram_tensor("fall", [128, 224], F32, kind="ExternalInput").ap()
    dy = nc.dram_tensor("y", [16, 128, T], F32, kind="ExternalOutput").ap()
    dz = nc.dram_tensor("zscratch", [64, 128, T], BF16, kind="Internal").ap()

    big0 = es.enter_context(nc.sbuf_tensor("big0", [128, 16 * T], BF16))
    big1 = es.enter_context(nc.sbuf_tensor("big1", [128, 16 * T], BF16))
    big2 = es.enter_context(nc.sbuf_tensor("big2", [128, 16 * T], BF16))
    big3 = es.enter_context(nc.sbuf_tensor("big3", [128, 16 * T], BF16))
    big4 = es.enter_context(nc.sbuf_tensor("big4", [128, 16 * T], BF16))
    scrA = es.enter_context(nc.sbuf_tensor("scrA", [128, 4096], BF16))
    scrB = es.enter_context(nc.sbuf_tensor("scrB", [128, 4096], BF16))
    vtbuf = es.enter_context(nc.sbuf_tensor("vtbuf", [128, 64 * 65], BF16))
    esb = es.enter_context(nc.sbuf_tensor("esb", [128, 2048], BF16))
    wring = es.enter_context(nc.sbuf_tensor("wring", [128, RING_N * RING_W], BF16))
    cst = es.enter_context(nc.sbuf_tensor("cst", [128, 712], BF16))
    fmisc = es.enter_context(nc.sbuf_tensor("fmisc", [128, 352], F32))

    hadU, hadA, hadB = cst[:, 0:128], cst[:, 128:256], cst[:, 256:384]
    tri, iden2 = cst[:, 384:512], cst[:, 512:640]
    ones_bf = cst[:, 640:641]
    onesrow_bf = cst[0:1, 644:708]
    su_eff = fmisc[:, 0:16]
    cu_eff = fmisc[:, 16:80]
    wsu_raw = fmisc[:, 80:144]
    wsg64 = fmisc[:, 144:208]
    wsd64 = fmisc[:, 208:224]
    ones_f = fmisc[:, 224:352]

    pA = es.enter_context(nc.psum_tensor("pA", [128, 1024], F32))
    pB = es.enter_context(nc.psum_tensor("pB", [128, 1024], F32))
    pS = es.enter_context(nc.psum_tensor("pS", [128, 2048], F32))
    pC = pS[:, 0:1024]
    pD = pS[:, 1024:2048]
    pT = pS[:, 1024:2048].bitcast(BF16)

    s_s = Sem(es, nc, "s_s")
    s_pe = Sem(es, nc, "s_pe")
    s_sc = Sem(es, nc, "s_sc")
    s_ve = Sem(es, nc, "s_ve")
    s_gp = Sem(es, nc, "s_gp")
    s_pre = Sem(es, nc, "s_pre")
    rs = [Sem(es, nc, f"rs{k}") for k in range(RING_N)]
    s_z = [Sem(es, nc, "s_z0"), Sem(es, nc, "s_z1")]
    s_x = [Sem(es, nc, "s_x0"), Sem(es, nc, "s_x1")]
    s_bd = [Sem(es, nc, "s_bd0"), Sem(es, nc, "s_bd1")]
    s_od = [Sem(es, nc, "s_od0"), Sem(es, nc, "s_od1")]
    s_op = [Sem(es, nc, "s_op0"), Sem(es, nc, "s_op1")]

    P = Prog()

    b0 = big0.rearrange("p (t n) -> p t n", t=16)
    b1 = big1.rearrange("p (t n) -> p t n", t=16)
    b2 = big2.rearrange("p (t n) -> p t n", t=16)
    b3 = big3.rearrange("p (t n) -> p t n", t=16)
    b4 = big4.rearrange("p (t n) -> p t n", t=16)
    kbf = scrA.rearrange("p (t n) -> p t n", t=4)
    zb6 = scrA.rearrange("p (t n) -> p t n", t=4)
    vtmp2 = scrB[:, 0:2048].rearrange("p (t n) -> p t n", t=2)
    vt3 = vtbuf.rearrange("p (t n) -> p t n", t=64)
    r_f32 = big1.bitcast(F32).rearrange("p (t n) -> p t n", t=16)

    def h2B(v):
        r, tt = divmod(v, 16)
        return [b0, b1, b2, b4][r][:, tt, :]

    P.add("g", lambda g: g.dma_start(out=cst[:, 0:640], in_=dcst), s_pre, 16)
    P.add("g", lambda g: g.dma_start(out=fmisc[:, 0:224], in_=dfm), s_pre, 16)
    P.add("g", lambda g: g.dma_start(
        out=b0[:, :, :], in_=dx.rearrange("t p n -> p t n")), s_pre, 16)
    PRE_T = 48

    ringl = []
    ringl += [(f"wk{m}", dwk[m]) for m in range(4)]
    ringl += [(f"wv{m}", dwv[m]) for m in range(4)]
    ringl += [(f"wq{m}", dwq[m]) for m in range(16)]
    for half in range(2):
        ringl += [(f"wo{half}_{m}", dwo[m]) for m in range(16)]
    for m in range(64):
        ringl += [(f"wu{m}", dwu[m]), (f"wg{m}", dwg[m])]
    for m in range(16):
        ringl += [(f"wd{m}_{kq}", dwd[m, kq]) for kq in range(4)]

    g_target = {lab: (rs[j % RING_N], 16 * (j // RING_N + 1))
                for j, (lab, _s) in enumerate(ringl)}
    ring_idx = {lab: j for j, (lab, _s) in enumerate(ringl)}
    ring_free = {}
    emitted = [0]

    def ring_slot(lab):
        j = ring_idx[lab] % RING_N
        return wring[:, j * RING_W:(j + 1) * RING_W]

    def emit_loads():
        i = emitted[0]
        while i < len(ringl):
            lab, src = ringl[i]
            if i >= RING_N:
                if (i - RING_N) not in ring_free:
                    break
                P.wait("g", s_pe, ring_free[i - RING_N])
            P.add("g", lambda g, _d=ring_slot(lab), _s=src:
                  g.dma_start(out=_d, in_=_s), rs[i % RING_N], 16)
            i += 1
        emitted[0] = i

    tapreg = {
        "xbf": (big0, [128, 16 * T], BF16), "q": (big1, [128, 16 * T], BF16),
        "kv": (scrA, [128, 6144], BF16), "vt": (vtbuf, [128, 64 * 65], BF16),
        "attn": (big2, [128, 16 * T], BF16), "x2": (big3, [128, 16 * T], BF16),
        "x1": (big4, [128, 16 * T], BF16),
        "h0": (big0, [128, 16 * T], BF16), "h1": (big1, [128, 16 * T], BF16),
        "h2": (big2, [128, 16 * T], BF16), "h4": (big4, [128, 16 * T], BF16),
    }
    dtaps = {name: nc.dram_tensor("tap_" + name, tapreg[name][1], tapreg[name][2],
                                  kind="ExternalOutput").ap() for name in taps}

    def finish():
        emit_loads()
        for name in taps:
            P.wait("y", s_ve, s_ve.n)
            P.wait("y", s_sc, s_sc.n)
            P.wait("y", s_pe, s_pe.n)
            P.add("y", lambda y, _d=dtaps[name], _a=tapreg[name][0][:, :]:
                  y.dma_start(out=_d, in_=_a), s_s, 16)
        if s_s.n:
            P.wait("y", s_s, s_s.n)
        for _sm in (s_od[0], s_od[1], s_op[0], s_op[1]):
            if _sm.n:
                P.wait("y", _sm, _sm.n)
        with nc.Block() as block:
            block.tensor(lambda t: P.run(t, "t"))
            block.scalar(lambda s: P.run(s, "s"))
            block.vector(lambda v: P.run(v, "v"))
            block.gpsimd(lambda g: P.run(g, "g"))
            block.sync(lambda y: P.run(y, "y"))
        es.close()
        return nc

    emit_loads()
    P.add("g", lambda g: g.memset(ones_bf, 1.0), s_gp)
    P.add("g", lambda g: g.memset(cst[:, 644:708], 1.0), s_gp)
    P.add("g", lambda g: g.memset(ones_f, 1.0), s_gp)
    P.add("g", lambda g: g.memset(vt3[:, :, 64:65], 1.0), s_gp)
    GP_MEMSETS = s_gp.n

    P.add("v", lambda v: v.memset(pS[:, :], 0.0), s_ve)
    P.wait("v", s_pre, PRE_T)
    c1 = float(1.0 / (64.0 * np.sqrt(2048.0)))
    c2 = float(1.0 / np.sqrt(8192.0))
    P.add("v", lambda v: v.tensor_scalar(out=su_eff, in0=su_eff, scalar1=c1,
                                         scalar2=None, op0=ALU.mult), s_ve)
    P.add("v", lambda v: v.tensor_tensor(out=cu_eff, in0=cu_eff, in1=wsu_raw,
                                         op=ALU.mult), s_ve)
    P.add("v", lambda v: v.tensor_scalar(out=cu_eff, in0=cu_eff, scalar1=c2,
                                         scalar2=None, op0=ALU.mult), s_ve)
    P.add("v", lambda v: v.tensor_scalar(out=wsg64, in0=wsg64, scalar1=64.0,
                                         scalar2=None, op0=ALU.mult), s_ve)
    VE_SCALES = P.add("v", lambda v: v.tensor_scalar(out=wsd64, in0=wsd64, scalar1=64.0,
                                                     scalar2=None, op0=ALU.mult), s_ve)

    evac_of = {}
    kv_plan = ([(f"wk{m}", m) for m in range(4)] +
               [(f"wv{m}", m) for m in range(4)])
    tb_evac = {}
    kbf_evac = {}

    P.wait("t", s_pre, PRE_T)
    for nb, (lab, m) in enumerate(kv_plan):
        ps = pA if nb % 2 == 0 else pB
        P.wait("t", *g_target[lab])
        if nb >= 2:
            sm, tv = evac_of[kv_plan[nb - 2][0]]
            P.wait("t", sm, tv)
        w3 = ring_slot(lab).rearrange("p (k j) -> p k j", k=16)

        def mmqkv(t, _ps=ps, _w3=w3):
            i = None
            for nh in range(2):
                for k in range(16):
                    i = t.matmul(_ps[:, nh * 512:(nh + 1) * 512], _w3[:, k, :],
                                 b0[:, k, nh * 512:(nh + 1) * 512],
                                 start=(k == 0), stop=(k == 15))
            return i
        done = P.add("t", mmqkv, s_pe)
        ring_free[ring_idx[lab]] = s_pe.n

        if lab.startswith("wk"):
            P.wait("v", s_pe, done)
            ev = P.add("v", lambda v, _m=m, _ps=ps: v.tensor_copy(
                out=kbf[:, _m, :], in_=_ps[:]), s_ve)
            evac_of[lab] = (s_ve, ev)
            kbf_evac[m] = ev
        else:
            P.wait("v", s_pe, done)
            ev = P.add("v", lambda v, _m=m, _ps=ps: v.tensor_copy(
                out=vtmp2[:, _m % 2, :], in_=_ps[:]), s_ve)
            evac_of[lab] = (s_ve, ev)
            P.wait("t", s_ve, ev)
            P.wait("t", s_pre, PRE_T)
            if m >= 1:
                P.wait("t", s_ve, tb_evac[m - 1])

            def trans(t, _m=m):
                vt = vtmp2[:, _m % 2, :]
                i = None
                for kt in range(8):
                    i = t.transpose(pT[:, kt * 128:(kt + 1) * 128],
                                    vt[:, kt * 128:(kt + 1) * 128], iden2)
                return i
            tb_done = P.add("t", trans, s_pe)
            P.wait("v", s_pe, tb_done)
            P.wait("v", s_gp, GP_MEMSETS)
            pv8 = pT[:, 0:1024].rearrange("p (a b) -> p a b", a=8)
            P.add("v", lambda v, _m=m: v.tensor_copy(
                out=vt3[:, 16 * _m:16 * _m + 8, 0:64], in_=pv8[:, :, 0:64]), s_ve)
            tb_evac[m] = P.add("v", lambda v, _m=m: v.tensor_copy(
                out=vt3[:, 16 * _m + 8:16 * _m + 16, 0:64],
                in_=pv8[:, :, 64:128]), s_ve)
        emit_loads()

    if stop_after == "ph1":
        return finish()

    q_evac = {}
    wq_state = {"m": 0, "c": 0}

    def emit_wq_chunk():
        m, c = wq_state["m"], wq_state["c"]
        if m >= 16:
            return False
        lab = f"wq{m}"
        if c == 0:
            P.wait("t", *g_target[lab])
            if m >= 1:
                P.wait("t", s_sc, q_evac[m - 1])
            else:
                P.wait("t", evac_of["wv2"][0], evac_of["wv2"][1])
        nh, k0 = c // 4, (c % 4) * 4
        w3 = ring_slot(lab).rearrange("p (k j) -> p k j", k=16)

        def chunk(t, _w3=w3, _nh=nh, _k0=k0):
            i = None
            for k in range(_k0, _k0 + 4):
                i = t.matmul(pA[:, _nh * 512:(_nh + 1) * 512], _w3[:, k, :],
                             b0[:, k, _nh * 512:(_nh + 1) * 512],
                             start=(k == 0), stop=(k == 15))
            return i
        d = P.add("t", chunk, s_pe)
        if c == 7:
            ring_free[ring_idx[lab]] = s_pe.n
            P.wait("s", s_pe, d)
            q_evac[m] = P.add("s", lambda s, _m=m: s.copy(
                out=b1[:, _m, :], in_=pA[:, :]), s_sc)
            wq_state["m"], wq_state["c"] = m + 1, 0
            emit_loads()
        else:
            wq_state["c"] = c + 1
        return True

    def ensure_wq(m):
        while wq_state["m"] <= m and (wq_state["m"] < 16):
            emit_wq_chunk()

    ehAB = [esb[:, 0:2048], scrB[:, 0:2048]]
    rbsb = scrB[:, 2048:3072]
    row_f = [big3[:, 0:2048].bitcast(F32)[64:65, 0:1024],
             big3[:, 4096:6144].bitcast(F32)[64:65, 0:1024]]
    recbf = [big3[64:65, 2048:3072], big3[64:65, 3072:4096]]
    onesblk = cst[:, 644:708]

    st = {"expA": None, "expB": None,
          "jS": 0,
          "eh": [None, None],
          "tl": {1: {"mult": None, "copyA": None, "copyB": None}},
          "pend": {1: None}}

    def c0f(qc, kt):
        return max(0, 128 * kt - 512 * qc)

    r0_f32 = big4.bitcast(F32).rearrange("p (t n) -> p t n", t=16)
    rsq0 = big3[:, 6144:10240].bitcast(F32)
    wo0_state = {"m": 0, "c": 0, "on": False, "gate": None}
    r0_done = {}
    st0_done = {}
    st0_q = []
    gsq0 = {}

    def emit_st0(upto):
        while st0_q and st0_q[0] <= upto:
            m_ = st0_q.pop(0)
            P.wait("t", s_ve, r0_done[m_])
            P.wait("t", s_gp, gsq0[m_])

            def st0mm(t, _m=m_):
                t.matmul(pA[0:1, 512:1024], ones_f[:, 0:1], r0_f32[:, _m, :],
                         start=(_m == 0), stop=(_m == 15))
                return t.matmul(pA[32:33, 512:1024], ones_f[:, 0:1],
                                rsq0[:, (_m % 4) * 512:(_m % 4) * 512 + 512],
                                start=(_m == 0), stop=(_m == 15),
                                tile_position=(0, 32))
            st0_done[m_] = P.add("t", st0mm, s_pe)

    def emit_wo0_chunk():
        m, c = wo0_state["m"], wo0_state["c"]
        if not wo0_state["on"] or m >= 16:
            return False
        lab = f"wo0_{m}"
        if c == 0:
            emit_st0(m - 3)
            P.wait("t", *g_target[lab])
            if m >= 1:
                P.wait("t", s_ve, r0_done[m - 1])
            else:
                P.wait("t", s_ve, wo0_state["gate"])
                P.wait("t", s_sc, q_evac[15])
        w3 = ring_slot(lab).rearrange("p (k j) -> p k j", k=16)

        def chunk(t, _w3=w3, _k0=c * 4):
            i_ = None
            for k in range(_k0, _k0 + 4):
                i_ = t.matmul(pA[:, 0:512], _w3[:, k, :], b2[:, k, 0:512],
                              start=(k == 0), stop=(k == 15))
            return i_
        d = P.add("t", chunk, s_pe)
        if c < 3:
            wo0_state["c"] = c + 1
            return True
        ring_free[ring_idx[lab]] = s_pe.n
        P.wait("v", s_pe, d)
        r0_done[m] = P.add("v", lambda v, _m=m: v.tensor_tensor(
            out=r0_f32[:, _m, :], in0=pA[:, 0:512], in1=b0[:, _m, 0:512],
            op=ALU.add), s_ve)
        P.wait("g", s_ve, r0_done[m])
        if m >= 4:
            P.wait("g", s_pe, st0_done[m - 4])
        gsq0[m] = P.add("g", lambda g, _m=m: g.tensor_mul(
            out=rsq0[:, (_m % 4) * 512:(_m % 4) * 512 + 512],
            in0=r0_f32[:, _m, :], in1=r0_f32[:, _m, :]), s_gp)
        st0_q.append(m)
        wo0_state["m"], wo0_state["c"] = m + 1, 0
        emit_loads()
        return True

    def emit_filler():
        if not emit_wq_chunk():
            emit_wo0_chunk()

    def emit_norm_recips(pV, rpar):
        P.add("s", lambda s, _r=rpar, _p=pV: s.activation(
            out=row_f[_r], in_=_p[64:65, 0:1024], func=AF.Ln), s_sc)
        return P.add("s", lambda s, _r=rpar: s.activation(
            out=recbf[_r], in_=row_f[_r], func=AF.Exp, scale=-1.0), s_sc)

    def emit_norm_finish(qt, qc, cast, tk, pV, rpar):
        tl = st["tl"][tk]
        qcc = slice(qc * 512, (qc + 1) * 512)
        P.wait("t", s_sc, cast)
        if tl["copyA"] is not None:
            P.wait("t", s_ve, tl["copyA"])

        def bcA(t, _p=pV, _r=rpar):
            return t.matmul(_p[64:128, 0:512], onesblk[64:65, 0:64],
                            recbf[_r][0:1, 0:512],
                            start=True, stop=True, tile_position=(64, 64))
        bA = P.add("t", bcA, s_pe)
        if tl["copyB"] is not None:
            P.wait("t", s_ve, tl["copyB"])

        def bcB(t, _p=pV, _r=rpar):
            return t.matmul(_p[64:128, 512:1024], onesblk[64:65, 0:64],
                            recbf[_r][0:1, 512:1024],
                            start=True, stop=True, tile_position=(64, 64))
        bB = P.add("t", bcB, s_pe)
        P.wait("v", s_pe, bA)
        tl["copyA"] = P.add("v", lambda v, _p=pV: v.tensor_copy(
            out=rbsb[0:64, 0:512], in_=_p[64:128, 0:512]), s_ve)
        P.add("v", lambda v, _q=qt, _c=qcc, _p=pV: v.tensor_tensor(
            out=b2[0:64, _q, _c], in0=_p[0:64, 0:512], in1=rbsb[0:64, 0:512],
            op=ALU.mult), s_ve)
        P.wait("v", s_pe, bB)
        tl["copyB"] = P.add("v", lambda v, _p=pV: v.tensor_copy(
            out=rbsb[64:128, 512:1024], in_=_p[64:128, 512:1024]), s_ve)
        tl["mult"] = P.add("v", lambda v, _q=qt, _c=qcc, _p=pV: v.tensor_tensor(
            out=b2[64:128, _q, _c], in0=_p[0:64, 512:1024],
            in1=rbsb[64:128, 512:1024], op=ALU.mult), s_ve)
        if qt == 15 and qc == 0:
            wo0_state["on"] = True
            wo0_state["gate"] = tl["mult"]

    def emit_qc_iter(p, gp, i, qc):
        qt = 4 * gp + i
        if qc == 0:
            ensure_wq(qt)
        exp_ctr = {}
        mask_ctr = {}
        exp_par = {}
        q_hat = 16 * qc + p

        def emit_pv_quad(j, last_kt):
            kt0, kt1 = 2 * j, 2 * j + 1
            for side in ("A", "B"):
                for kt in (kt0, kt1):
                    if (side, kt) in mask_ctr:
                        P.wait("t", s_ve, mask_ctr[(side, kt)])
            P.wait("t", s_sc, exp_ctr[j])
            if kt0 == 0 and st["tl"][1]["mult"] is not None:
                P.wait("t", s_ve, st["tl"][1]["mult"])
            par = exp_par[j]

            def pvq(t, _gp=gp, _qc=qc, _k0=kt0, _k1=kt1, _last=last_kt,
                    _par=par):
                i_ = None
                for kt in (_k0, _k1):
                    c0 = c0f(_qc, kt)
                    for g, eoff, off in ((2 * _gp, 0, 0), (2 * _gp + 1, 1024, 512)):
                        i_ = t.matmul(
                            pB[0:65, off + c0:off + 512],
                            vt3[:, g * 8 + kt, 0:65],
                            ehAB[_par][:, eoff + (kt % 2) * 512 + c0:
                                       eoff + (kt % 2) * 512 + 512],
                            start=(kt == 0), stop=(kt == _last))
                return i_
            d = P.add("t", pvq, s_pe)
            st["eh"][par] = d
            return d

        kts = [0, 1, 2, 3] if qc == 0 else list(range(8))
        njs = len(kts) // 2
        for j in range(njs):
            kt0, kt1 = 2 * j, 2 * j + 1
            if p == 0 and qc == 0 and j == 0:
                P.wait("t", s_sc, q_evac[qt])
                P.wait("t", s_ve, kbf_evac[gp])
                P.wait("t", s_ve, tb_evac[3])
                P.wait("t", s_gp, GP_MEMSETS)
            elif qc == 0 and j == 0:
                P.wait("t", s_sc, q_evac[qt])

            def spair(t, _gp=gp, _qt=qt, _qc=qc, _k0=kt0, _k1=kt1,
                      _qb=0, _soff=0):
                i_ = None
                for kt in (_k0, _k1):
                    c0 = c0f(_qc, kt)
                    i_ = t.matmul(
                        pS[:, _soff + (kt % 2) * 512 + c0:
                           _soff + (kt % 2) * 512 + 512],
                        kbf[_qb:_qb + 64, _gp, kt * 128:(kt + 1) * 128],
                        b1[_qb:_qb + 64, _qt, _qc * 512 + c0:(_qc + 1) * 512],
                        start=True, stop=True)
                return i_
            if st["expA"] is not None:
                P.wait("t", s_sc, st["expA"])
            sqA = P.add("t", lambda t, _f=spair: _f(t), s_pe)
            if st["expB"] is not None:
                P.wait("t", s_sc, st["expB"])
            sqB = P.add("t", lambda t, _f=spair: _f(t, _qb=64, _soff=1024), s_pe)

            if j == min(1, njs - 1) and st["pend"][1] is not None:
                emit_filler()
                emit_filler()
                emit_norm_finish(*st["pend"][1])
                st["pend"][1] = None

            emit_filler()

            c0e = c0f(qc, kt0)
            par = st["jS"] % 2
            st["jS"] += 1
            exp_par[j] = par
            if p == 0 and qc == 0 and j <= 1:
                P.wait("s", s_ve, tb_evac[3])
            if st["eh"][par] is not None:
                P.wait("s", s_pe, st["eh"][par])
            P.wait("s", s_pe, sqA)
            ecA = P.add(
                "s", lambda s, _eh=ehAB[par], _c0=c0e: s.activation(
                    out=_eh[:, _c0:1024], in_=pS[:, _c0:1024], func=AF.Exp,
                    scale=0.125), s_sc)
            P.wait("s", s_pe, sqB)
            ecB = P.add(
                "s", lambda s, _eh=ehAB[par], _c0=c0e: s.activation(
                    out=_eh[:, 1024 + _c0:2048], in_=pS[:, 1024 + _c0:2048],
                    func=AF.Exp, scale=0.125), s_sc)
            exp_ctr[j] = ecB
            exp_ctr[(j, "A")] = ecA
            st["expA"] = ecA
            st["expB"] = ecB
            for kt in (kt0, kt1):
                if 128 * kt >= 512 * qc:
                    c0 = c0f(qc, kt)
                    for side, eoff, ec in (("A", 0, ecA), ("B", 1024, ecB)):
                        P.wait("v", s_sc, ec)
                        mask_ctr[(side, kt)] = P.add(
                            "v", lambda v, _eh=ehAB[par],
                            _o=eoff + (kt % 2) * 512 + c0:
                            v.tensor_tensor(out=_eh[:, _o:_o + 128],
                                            in0=_eh[:, _o:_o + 128], in1=tri,
                                            op=ALU.mult), s_ve)

            if j >= 1:
                emit_pv_quad(j - 1, kts[-1])
                emit_filler()
        pv_last = emit_pv_quad(njs - 1, kts[-1])
        P.wait("s", s_pe, pv_last)
        st["pend"][1] = (qt, qc, emit_norm_recips(pB, q_hat % 2), 1, pB,
                         q_hat % 2)
        emit_filler()
        emit_loads()

    pairs = [(g_, i_) for g_ in range(4) for i_ in range(4)]
    for p, (gp, i) in enumerate(pairs):
        emit_qc_iter(p, gp, i, 0)
    ensure_wq(15)
    for p, (gp, i) in enumerate(pairs):
        emit_qc_iter(p, gp, i, 1)
    emit_norm_finish(*st["pend"][1])
    st["pend"][1] = None
    b2_ready = st["tl"][1]["mult"]
    while emit_wo0_chunk():
        pass
    emit_st0(15)
    SC_AB = s_sc.n

    if stop_after == "ph2":
        return finish()

    rowA = scrB[0:1, 0:1024].bitcast(F32)
    rowB = scrB[0:1, 1024:2048].bitcast(F32)
    rsq = scrB[:, 2048:4096].bitcast(F32)
    msq0 = esb[:, 0:1024].bitcast(F32)[0:1, 0:512]
    x2_last = None

    def emit_ln_tail(sum_ap, sq_ap, rX, half, cb=None):
        P.add("v", lambda v: v.tensor_scalar(out=rowA, in0=sum_ap,
                                             scalar1=float(1.0 / D), scalar2=None,
                                             op0=ALU.mult), s_ve)
        P.add("v", lambda v: v.tensor_tensor(out=rowB, in0=rowA, in1=rowA,
                                             op=ALU.mult), s_ve)
        stats_read = P.add("v", lambda v: v.tensor_scalar(
            out=msq0, in0=sq_ap, scalar1=float(1.0 / D), scalar2=None,
            op0=ALU.mult), s_ve)
        vv_done = P.add("v", lambda v: v.tensor_tensor(
            out=rowB, in0=msq0, in1=rowB, op=ALU.subtract), s_ve)
        vv_done = P.add("v", lambda v: v.tensor_scalar(
            out=rowB, in0=rowB, scalar1=float(EPS), scalar2=None, op0=ALU.add), s_ve)
        P.wait("s", s_ve, vv_done)
        P.add("s", lambda s: s.activation(out=rowB, in_=rowB, func=AF.Ln), s_sc)
        a_done = P.add("s", lambda s: s.activation(out=rowB, in_=rowB,
                                                   func=AF.Exp, scale=-0.5), s_sc)
        P.wait("v", s_sc, a_done)
        P.add("v", lambda v: v.tensor_tensor(out=rowA, in0=rowA, in1=rowB,
                                             op=ALU.mult), s_ve)
        b_done = P.add("v", lambda v: v.tensor_scalar(
            out=rowA, in0=rowA, scalar1=-1.0, scalar2=None, op0=ALU.mult), s_ve)
        P.wait("t", s_ve, b_done)

        def bcln(t):
            t.matmul(pD[:, 0:512], ones_f[0:1, 0:128], rowB,
                     start=True, stop=True)
            return t.matmul(pD[:, 512:1024], ones_f[0:1, 0:128], rowA,
                            start=True, stop=True)
        bc1 = P.add("t", bcln, s_pe)

        x2t = rsq[:, 512:1024]
        P.wait("v", s_pe, bc1)
        last = None
        for m in range(16):
            P.add("v", lambda v, _m=m, _r=rX: v.tensor_tensor(
                out=x2t, in0=_r[:, _m, :], in1=pD[:, 0:512],
                op=ALU.mult), s_ve)
            last = P.add("v", lambda v, _m=m, _h=half: v.tensor_tensor(
                out=b3[:, _m, _h * 512:(_h + 1) * 512], in0=x2t,
                in1=pD[:, 512:1024], op=ALU.add), s_ve)
            if cb is not None:
                cb(m, last)
        return stats_read, last

    P.wait("v", s_pe, st0_done[15])
    P.wait("t", s_sc, SC_AB)
    LN0_PA_READ, x2_h0_last = emit_ln_tail(
        pA[0:1, 512:1024], pA[32:33, 512:1024], r0_f32, 0)

    xp0 = esb.rearrange("p (a n) -> p a n", a=4)
    ph40 = {"t": 0, "j": 0, "on": False}
    z1h0 = {}
    z1ev0 = {}
    z1out0 = {}
    xpd0 = {}
    x1rd0 = {}
    h16x0 = {}
    x1ev0 = {}

    def emit_ph40_chunk():
        if not ph40["on"]:
            return False
        t_i, j = ph40["t"], ph40["j"]
        if t_i < 16:
            if t_i >= 2:
                P.wait("v", s_pe, z1h0[t_i - 2])
            xpd0[t_i] = P.add("v", lambda v, _t=t_i: v.tensor_scalar(
                out=xp0[:, _t % 2, :], in0=b3[:, _t, 0:512],
                scalar1=su_eff[:, _t:_t + 1], scalar2=None, op0=ALU.mult), s_ve)
            P.wait("t", s_ve, xpd0[t_i])
            if t_i == 0:
                P.wait("t", s_ve, b2_ready)
                P.wait("t", s_pre, PRE_T)
            if t_i >= 2:
                P.wait("t", s_sc, z1ev0[t_i - 2])
            z1h0[t_i] = P.add("t", lambda t, _t=t_i: t.matmul(
                pB[:, (_t % 2) * 512:(_t % 2) * 512 + 512], hadU,
                xp0[:, _t % 2, :], start=True, stop=True), s_pe)
            P.wait("s", s_pe, z1h0[t_i])
            jj = t_i // 2
            if jj >= 2 and t_i % 2 == 0:
                P.wait("s", s_z[jj % 2], z1out0[jj - 2])
            z1ev0[t_i] = P.add("s", lambda s, _t=t_i: s.copy(
                out=zb6[:, (_t // 2 % 2) * 2 + (_t % 2), 0:512],
                in_=pB[:, (_t % 2) * 512:(_t % 2) * 512 + 512]), s_sc)
            if t_i % 2 == 1:
                P.wait("y", s_sc, z1ev0[t_i])
                z1out0[jj] = P.add("y", lambda y, _j=jj: y.dma_start(
                    out=dz[2 * _j:2 * _j + 2].rearrange("t p n -> p t n")[:, :, 0:512],
                    in_=zb6[:, (_j % 2) * 2:(_j % 2) * 2 + 2, 0:512]), s_z[jj % 2], 16)
            ph40["t"] = t_i + 1
            return True
        if j < 8:
            if j == 0:
                P.wait("y", s_z[0], z1out0[6])
                P.wait("y", s_z[1], z1out0[7])
            if j >= 2:
                P.wait("y", s_pe, h16x0[j - 2])
            x1rd0[j] = P.add("y", lambda y, _j=j: y.dma_start(
                out=zb6[:, (_j % 2) * 2:(_j % 2) * 2 + 2, 0:512],
                in_=dz16[:, 2 * _j:2 * _j + 2, 0:512]), s_x[j % 2], 16)
            P.wait("t", s_x[j % 2], x1rd0[j])
            if j == 0:
                P.wait("t", s_pre, PRE_T)
                P.wait("t", s_sc, z1ev0[15])
            else:
                P.wait("t", s_sc, x1ev0[2 * (j - 1) + 1])

            def h16(t, _j=j):
                i_ = None
                for uu in range(2):
                    i_ = t.matmul(pB[:, uu * 512:(uu + 1) * 512], hadA,
                                  zb6[:, (_j % 2) * 2 + uu, 0:512],
                                  start=True, stop=True)
                return i_
            h16x0[j] = P.add("t", h16, s_pe)
            for uu in range(2):
                P.wait("s", s_pe, h16x0[j])
                x1ev0[2 * j + uu] = P.add("s", lambda s, _j=j, _uu=uu: s.copy(
                    out=b4[:, 2 * _j + _uu, 0:512],
                    in_=pB[:, _uu * 512:(_uu + 1) * 512]), s_sc)
            ph40["j"] = j + 1
            return True
        return False

    dz16 = dz[0:16].rearrange("t (b u) n -> (t b) u n", u=16)
    ph40["on"] = True

    o_done = {}
    r_done = {}
    sq_done = {}
    st_done = {}
    for m in range(17):
        if m < 16:
            lab = f"wo1_{m}"
            P.wait("t", *g_target[lab])
            if m >= 1:
                P.wait("t", s_ve, r_done[m - 1])
            else:
                P.wait("t", s_ve, LN0_PA_READ)
            w3 = ring_slot(lab).rearrange("p (k j) -> p k j", k=16)

            def omm(t, _w3=w3):
                i = None
                for k in range(16):
                    i = t.matmul(pA[:, 0:512], _w3[:, k, :],
                                 b2[:, k, 512:1024],
                                 start=(k == 0), stop=(k == 15))
                return i
            o_done[m] = P.add("t", omm, s_pe)
            ring_free[ring_idx[lab]] = s_pe.n

            P.wait("v", s_pe, o_done[m])
            r_done[m] = P.add("v", lambda v, _m=m: v.tensor_tensor(
                out=r_f32[:, _m, :], in0=pA[:, 0:512],
                in1=b0[:, _m, 512:1024], op=ALU.add), s_ve)

            P.wait("s", s_ve, r_done[m])
            if m >= 2:
                P.wait("s", s_pe, st_done[m - 2])
            sq_done[m] = P.add("s", lambda s, _m=m: s.activation(
                out=rsq[:, (_m % 2) * 512:(_m % 2) * 512 + 512],
                in_=r_f32[:, _m, :], func=AF.Square), s_sc)
        if m >= 1:
            mm_ = m - 1
            P.wait("t", s_ve, r_done[mm_])
            P.wait("t", s_sc, sq_done[mm_])

            def stmm(t, _m=mm_):
                t.matmul(pC[0:1, 0:512], ones_f[:, 0:1], r_f32[:, _m, :],
                         start=(_m == 0), stop=(_m == 15))
                return t.matmul(pC[0:1, 512:1024], ones_f[:, 0:1],
                                rsq[:, (_m % 2) * 512:(_m % 2) * 512 + 512],
                                start=(_m == 0), stop=(_m == 15))
            st_done[mm_] = P.add("t", stmm, s_pe)
        emit_ph40_chunk()
        emit_ph40_chunk()
        emit_loads()
    while emit_ph40_chunk():
        pass

    xp1 = scrB.rearrange("p (a n) -> p a n", a=8)
    z1h1 = {}
    z1ev1 = {}
    z1out1 = {}

    def ph41_cb(t_i, x2_ctr):
        if t_i >= 2:
            P.wait("v", s_pe, z1h1[t_i - 2])
        xpd = P.add("v", lambda v, _t=t_i: v.tensor_scalar(
            out=xp1[:, _t % 2, :], in0=b3[:, _t, 512:1024],
            scalar1=su_eff[:, _t:_t + 1], scalar2=None, op0=ALU.mult), s_ve)
        P.wait("t", s_ve, xpd)
        if t_i >= 2:
            P.wait("t", s_sc, z1ev1[t_i - 2])
        z1h1[t_i] = P.add("t", lambda t, _t=t_i: t.matmul(
            pA[:, (_t % 2) * 512:(_t % 2) * 512 + 512], hadU,
            xp1[:, _t % 2, :], start=True, stop=True), s_pe)
        P.wait("s", s_pe, z1h1[t_i])
        jj = t_i // 2
        if jj >= 2 and t_i % 2 == 0:
            P.wait("s", s_z[jj % 2], z1out1[jj - 2])
        z1ev1[t_i] = P.add("s", lambda s, _t=t_i: s.copy(
            out=zb6[:, (_t // 2 % 2) * 2 + (_t % 2), 512:1024],
            in_=pA[:, (_t % 2) * 512:(_t % 2) * 512 + 512]), s_sc)
        if t_i % 2 == 1:
            P.wait("y", s_sc, z1ev1[t_i])
            z1out1[jj] = P.add("y", lambda y, _j=jj: y.dma_start(
                out=dz[2 * _j:2 * _j + 2].rearrange("t p n -> p t n")[:, :, 512:1024],
                in_=zb6[:, (_j % 2) * 2:(_j % 2) * 2 + 2, 512:1024]), s_z[jj % 2], 16)

    P.wait("v", s_pe, st_done[15])
    P.wait("t", s_ve, x2_h0_last)
    _, x2_last = emit_ln_tail(pC[0:1, 0:512], pC[0:1, 512:1024], r_f32, 1,
                              cb=ph41_cb)

    x1rd1 = {}
    h16x1 = {}
    x1ev1 = {}
    for j in range(8):
        eng1 = "y" if j % 2 == 0 else "g"
        if j <= 1:
            P.wait(eng1, s_z[0], z1out1[6])
            P.wait(eng1, s_z[1], z1out1[7])
        if j >= 2:
            P.wait(eng1, s_pe, h16x1[j - 2])
        x1rd1[j] = P.add(eng1, lambda q, _j=j: q.dma_start(
            out=zb6[:, (_j % 2) * 2:(_j % 2) * 2 + 2, 512:1024],
            in_=dz16[:, 2 * _j:2 * _j + 2, 512:1024]), s_x[j % 2], 16)
        P.wait("t", s_x[j % 2], x1rd1[j])
        if j == 0:
            P.wait("t", s_sc, z1ev1[15])
        else:
            P.wait("t", s_sc, x1ev1[2 * (j - 1) + 1])

        def h16b(t, _j=j):
            i_ = None
            for uu in range(2):
                i_ = t.matmul(pA[:, uu * 512:(uu + 1) * 512], hadA,
                              zb6[:, (_j % 2) * 2 + uu, 512:1024],
                              start=True, stop=True)
            return i_
        h16x1[j] = P.add("t", h16b, s_pe)
        for uu in range(2):
            P.wait("s", s_pe, h16x1[j])
            x1ev1[2 * j + uu] = P.add("s", lambda s, _j=j, _uu=uu: s.copy(
                out=b4[:, 2 * _j + _uu, 512:1024],
                in_=pA[:, _uu * 512:(_uu + 1) * 512]), s_sc)
        emit_loads()
    X1_READY = x1ev1[15]

    if stop_after == "ph3":
        return finish()

    silu_f = scrB[:, 0:2048].bitcast(F32)
    htmp_f = scrB[:, 2048:4096].bitcast(F32)
    hbf = vtbuf[:, 0:4096].rearrange("p (a n) -> p a n", a=4)
    zev = {}
    zout = {}
    hevac = {}
    zh = {}
    silu_done = {}

    def emit_z(m):
        psGz = pC if m % 2 == 0 else pD
        P.wait("t", s_ve, hevac[m])
        P.wait("t", s_sc, silu_done[m])

        def zmm(t, _ps=psGz, _m=m):
            i = None
            for (c0, c1) in [(0, 512), (512, 1024)]:
                i = t.matmul(_ps[:, c0:c1], hadU, hbf[:, _m % 2, c0:c1],
                             start=True, stop=True)
            return i
        zh[m] = P.add("t", zmm, s_pe)
        P.wait("s", s_pe, zh[m])
        j = m // 2
        if j >= 2 and m % 2 == 0:
            P.wait("s", s_z[j % 2], zout[j - 2])
        zev[m] = P.add("s", lambda s, _ps=psGz, _m=m: s.copy(
            out=zb6[:, (_m // 2 % 2) * 2 + (_m % 2), :], in_=_ps[:]), s_sc)
        if m % 2 == 1:
            P.wait("y", s_sc, zev[m])
            zout[j] = P.add("y", lambda y, _j=j: y.dma_start(
                out=dz[2 * _j:2 * _j + 2].rearrange("t p n -> p t n"),
                in_=zb6[:, (_j % 2) * 2:(_j % 2) * 2 + 2, :]), s_z[j % 2], 16)

    for m in range(64):
        upl, gal = f"wu{m}", f"wg{m}"
        psU = pA if m % 2 == 0 else pB
        psG = pC if m % 2 == 0 else pD
        P.wait("t", *g_target[upl])
        if m == 0:
            P.wait("t", s_sc, X1_READY)
        if m >= 2:
            P.wait("t", s_ve, hevac[m - 2])

        def upmm(t, _ps=psU, _lab=upl):
            w3 = ring_slot(_lab).rearrange("p (k j) -> p k j", k=16)
            i = None
            for nh in range(2):
                for k in range(16):
                    i = t.matmul(_ps[:, nh * 512:(nh + 1) * 512], w3[:, k, :],
                                 b4[:, k, nh * 512:(nh + 1) * 512],
                                 start=(k == 0), stop=(k == 15))
            return i
        up_done = P.add("t", upmm, s_pe)
        ring_free[ring_idx[upl]] = s_pe.n

        P.wait("t", *g_target[gal])
        if m >= 2:
            P.wait("t", s_sc, zev[m - 2])

        def gamm(t, _ps=psG, _lab=gal):
            w3 = ring_slot(_lab).rearrange("p (k j) -> p k j", k=16)
            i = None
            for nh in range(2):
                for k in range(16):
                    i = t.matmul(_ps[:, nh * 512:(nh + 1) * 512], w3[:, k, :],
                                 b4[:, k, nh * 512:(nh + 1) * 512],
                                 start=(k == 0), stop=(k == 15))
            return i
        ga_done = P.add("t", gamm, s_pe)
        ring_free[ring_idx[gal]] = s_pe.n

        if m >= 1:
            emit_z(m - 1)

        P.wait("s", s_pe, ga_done)
        if m >= 1:
            P.wait("s", s_ve, hevac[m - 1])
        silu_done[m] = P.add("s", lambda s, _ps=psG, _m=m: s.activation(
            out=silu_f[:, :], in_=_ps[:], func=AF.Silu, scale=wsg64[:, _m:_m + 1]),
            s_sc)

        P.wait("v", s_sc, silu_done[m])
        P.wait("v", s_pe, up_done)
        if m >= 2:
            P.wait("v", s_pe, zh[m - 2])
        P.add("v", lambda v, _ps=psU: v.tensor_tensor(
            out=htmp_f[:, :], in0=silu_f[:, :], in1=_ps[:], op=ALU.mult), s_ve)
        hevac[m] = P.add("v", lambda v, _m=m: v.tensor_scalar(
            out=hbf[:, _m % 2, :], in0=htmp_f[:, :], scalar1=cu_eff[:, _m:_m + 1],
            scalar2=None, op0=ALU.mult), s_ve)
        emit_loads()
    emit_z(63)

    if stop_after == "ph5":
        return finish()

    dz64 = dz.rearrange("t (b v) n -> (t b) v n", v=64)
    brd = {}
    bh = {}
    bh2 = {}
    bev = {}
    bev_s = {}
    bufA4 = scrA[:, 0:4096].rearrange("p (a n) -> p a n", a=4)
    bufB4 = scrB[:, 0:4096].rearrange("p (a n) -> p a n", a=4)
    for j in range(16):
        buf = bufA4 if j % 2 == 0 else bufB4
        if j == 0:
            P.wait("y", s_z[0], zout[30])
            P.wait("y", s_z[1], zout[31])
        if j == 1:
            P.wait("y", s_ve, hevac[63])
        if j >= 2:
            P.wait("y", s_pe, bh2[j - 2])
        brd[j] = P.add("y", lambda y, _j=j, _b=buf: y.dma_start(
            out=_b[:, :, :], in_=dz64[:, 4 * _j:4 * _j + 4, :]), s_bd[j % 2], 16)
        P.wait("t", s_bd[j % 2], brd[j])
        P.wait("t", s_pre, PRE_T)
        if j >= 1:
            P.wait("t", s_ve, bev[4 * (j - 1) + 1])
        def h64a(t, _b=buf):
            i = None
            for vv, ps in ((0, pA), (1, pC)):
                for (c0, c1) in [(0, 512), (512, 1024)]:
                    i = t.matmul(ps[:, c0:c1], hadB, _b[:, vv, c0:c1],
                                 start=True, stop=True)
            return i
        bh[j] = P.add("t", h64a, s_pe)
        if j >= 1:
            P.wait("t", s_sc, bev_s[4 * (j - 1) + 3])
        def h64b(t, _b=buf):
            i = None
            for vv, ps in ((2, pB), (3, pD)):
                for (c0, c1) in [(0, 512), (512, 1024)]:
                    i = t.matmul(ps[:, c0:c1], hadB, _b[:, vv, c0:c1],
                                 start=True, stop=True)
            return i
        bh2[j] = P.add("t", h64b, s_pe)
        P.wait("v", s_pe, bh[j])
        for vv, ps in ((0, pA), (1, pC)):
            bev[4 * j + vv] = P.add("v", lambda v, _j=j, _vv=vv, _ps=ps: v.tensor_copy(
                out=h2B(4 * _j + _vv), in_=_ps[:]), s_ve)
        P.wait("s", s_pe, bh2[j])
        for vv, ps in ((2, pB), (3, pD)):
            bev_s[4 * j + vv] = P.add("s", lambda s, _j=j, _vv=vv, _ps=ps: s.copy(
                out=h2B(4 * _j + _vv), in_=_ps[:]), s_sc)
        emit_loads()

    if stop_after == "ph6":
        return finish()

    rowA2 = scrB[0:1, 0:2048].bitcast(F32)
    rowB2 = scrB[0:1, 2048:4096].bitcast(F32)
    rsq2 = scrB[:, 0:2048]
    dn_done = {}
    r2_done = {}
    sq2 = {}
    st2 = {}
    for m in range(17):
        if m < 16:
            psDn = pA if m % 2 == 0 else pB
            if m == 0:
                P.wait("t", s_ve, bev[61])
                P.wait("t", s_sc, bev_s[63])
            if m >= 2:
                P.wait("t", s_ve, r2_done[m - 2])
            for kq in range(4):
                lab = f"wd{m}_{kq}"
                P.wait("t", *g_target[lab])

                def dnmm(t, _lab=lab, _ps=psDn, _kq=kq):
                    w3 = ring_slot(_lab).rearrange("p (k j) -> p k j", k=16)
                    i = None
                    for k in range(16):
                        vv = _kq * 16 + k
                        for nh in range(2):
                            i = t.matmul(_ps[:, nh * 512:(nh + 1) * 512], w3[:, k, :],
                                         h2B(vv)[:, nh * 512:(nh + 1) * 512],
                                         start=(_kq == 0 and k == 0),
                                         stop=(_kq == 3 and k == 15))
                    return i
                dn_done[m] = P.add("t", dnmm, s_pe)
                ring_free[ring_idx[lab]] = s_pe.n
            P.wait("s", s_pe, dn_done[m])
            scl = P.add("s", lambda s, _ps=psDn, _m=m: s.activation(
                out=_ps[:], in_=_ps[:], func=AF.Copy, scale=wsd64[:, _m:_m + 1]), s_sc)
            P.wait("v", s_sc, scl)
            r2_done[m] = P.add("v", lambda v, _ps=psDn, _m=m: v.tensor_tensor(
                out=b3[:, _m, :], in0=_ps[:], in1=b3[:, _m, :], op=ALU.add), s_ve)
            P.wait("s", s_ve, r2_done[m])
            if m >= 2:
                P.wait("s", s_pe, st2[m - 2])
            sq2[m] = P.add("s", lambda s, _m=m: s.activation(
                out=rsq2[:, (_m % 2) * 1024:(_m % 2 + 1) * 1024], in_=b3[:, _m, :],
                func=AF.Square), s_sc)
        if m >= 1:
            mm_ = m - 1
            P.wait("t", s_ve, r2_done[mm_])
            P.wait("t", s_sc, sq2[mm_])

            def st2mm(t, _m=mm_):
                i = None
                for (c0, c1) in [(0, 512), (512, 1024)]:
                    t.matmul(pC[0:1, c0:c1], ones_bf, b3[:, _m, c0:c1],
                             start=(_m == 0), stop=(_m == 15))
                    i = t.matmul(pD[0:1, c0:c1], ones_bf,
                                 rsq2[:, (_m % 2) * 1024 + c0:(_m % 2) * 1024 + c1],
                                 start=(_m == 0), stop=(_m == 15))
                return i
            st2[mm_] = P.add("t", st2mm, s_pe)
        emit_loads()

    P.wait("v", s_pe, st2[15])
    P.add("v", lambda v: v.tensor_scalar(out=rowA2, in0=pC[0:1, :],
                                         scalar1=float(1.0 / D), scalar2=None,
                                         op0=ALU.mult), s_ve)
    P.add("v", lambda v: v.tensor_tensor(out=rowB2, in0=rowA2, in1=rowA2,
                                         op=ALU.mult), s_ve)
    P.add("v", lambda v: v.tensor_scalar(out=pD[0:1, :], in0=pD[0:1, :],
                                         scalar1=float(1.0 / D), scalar2=None,
                                         op0=ALU.mult), s_ve)
    vv2 = P.add("v", lambda v: v.tensor_tensor(out=rowB2, in0=pD[0:1, :],
                                               in1=rowB2, op=ALU.subtract), s_ve)
    vv2 = P.add("v", lambda v: v.tensor_scalar(
        out=rowB2, in0=rowB2, scalar1=float(EPS), scalar2=None, op0=ALU.add), s_ve)
    P.wait("s", s_ve, vv2)
    P.add("s", lambda s: s.activation(out=rowB2, in_=rowB2, func=AF.Ln), s_sc)
    a2_done = P.add("s", lambda s: s.activation(out=rowB2, in_=rowB2,
                                                func=AF.Exp, scale=-0.5), s_sc)
    P.wait("v", s_sc, a2_done)
    P.add("v", lambda v: v.tensor_tensor(out=rowA2, in0=rowA2, in1=rowB2,
                                         op=ALU.mult), s_ve)
    b2_done = P.add("v", lambda v: v.tensor_scalar(out=rowA2, in0=rowA2,
                                                   scalar1=-1.0, scalar2=None,
                                                   op0=ALU.mult), s_ve)
    P.wait("t", s_ve, b2_done)

    def bcln2(t):
        i = None
        for (c0, c1) in [(0, 512), (512, 1024)]:
            t.matmul(pC[:, c0:c1], ones_f[0:1, 0:128], rowB2[0:1, c0:c1],
                     start=True, stop=True)
            i = t.matmul(pD[:, c0:c1], ones_f[0:1, 0:128], rowA2[0:1, c0:c1],
                         start=True, stop=True)
        return i
    bc2 = P.add("t", bcln2, s_pe)

    outst = scrB.bitcast(F32)
    outg = big1.bitcast(F32)
    a_sb = vtbuf[:, 0:2048].bitcast(F32)
    b_sb = vtbuf[:, 2048:4096].bitcast(F32)
    out_dma = {}
    P.wait("v", s_pe, bc2)
    P.add("v", lambda v: v.tensor_copy(out=a_sb, in_=pC[:]), s_ve)
    ab_done = P.add("v", lambda v: v.tensor_copy(out=b_sb, in_=pD[:]), s_ve)
    P.wait("g", s_ve, ab_done)
    pool_set = (1, 4, 7, 10, 13, 15)
    prev_v, prev_g = {}, {}
    nv = ng = 0
    for m in range(16):
        if m not in pool_set:
            sl = slice((nv % 2) * 1024, (nv % 2 + 1) * 1024)
            if nv >= 2:
                P.wait("v", s_od[nv % 2], out_dma[prev_v[nv % 2]])
            P.add("v", lambda v, _m=m, _s=sl: v.tensor_tensor(
                out=outst[:, _s], in0=b3[:, _m, :], in1=pC[:], op=ALU.mult), s_ve)
            od = P.add("v", lambda v, _m=m, _s=sl: v.tensor_tensor(
                out=outst[:, _s], in0=outst[:, _s], in1=pD[:], op=ALU.add), s_ve)
            P.wait("y", s_ve, od)
            out_dma[m] = P.add("y", lambda y, _m=m, _s=sl: y.dma_start(
                out=dy[_m], in_=outst[:, _s]), s_od[nv % 2], 16)
            prev_v[nv % 2] = m
            nv += 1
        else:
            sl = slice((ng % 2) * 1024, (ng % 2 + 1) * 1024)
            if ng >= 2:
                P.wait("g", s_op[ng % 2], out_dma[prev_g[ng % 2]])
            gm = P.add("g", lambda g, _m=m, _s=sl: g.tensor_tensor(
                out=outg[:, _s], in0=b3[:, _m, :], in1=a_sb, op=ALU.mult), s_gp)
            P.wait("g", s_gp, gm)
            od = P.add("g", lambda g, _m=m, _s=sl: g.tensor_tensor(
                out=outg[:, _s], in0=outg[:, _s], in1=b_sb, op=ALU.add), s_gp)
            P.wait("s", s_gp, od)
            out_dma[m] = P.add("s", lambda s, _m=m, _s=sl: s.dma_start(
                out=dy[_m], in_=outg[:, _s]), s_op[ng % 2], 16)
            prev_g[ng % 2] = m
            ng += 1
    return finish()



def kernel(**inputs):
    g, shards = host_prepare(inputs)
    nc = build_program()
    shared = {k: g[k] for k in ("wq", "wk", "wv", "wo", "wu", "wg", "wd",
                                "cstall", "fall")}
    in_maps = [dict(shared, x=shards[c]) for c in range(NCORES)]
    res = run_bass_kernel_spmd(nc, in_maps, core_ids=list(range(NCORES)))
    out = np.empty((8 * T, D), dtype=np.float32)
    for c in range(NCORES):
        yc = np.asarray(res.results[c]["y"])
        out[c * T:(c + 1) * T, :] = yc.reshape(D, T).T
    return out.reshape(4, 2048, 2048)


# version 41
# speedup vs baseline: 1.2209x; 1.0011x over previous
import numpy as np
import concourse.bass as bass
import concourse.mybir as mybir
from concourse.bass_utils import run_bass_kernel_spmd
from contextlib import ExitStack

F32 = mybir.dt.float32
BF16 = mybir.dt.bfloat16
AF = mybir.ActivationFunctionType
ALU = mybir.AluOpType

NCORES = 8
T = 1024
D = 2048
INTER = 8192
NH = 32
EPS = 1e-5
RING_N = 4
RING_W = 2048


def _hadamard(n):
    h = np.array([[1.0]], dtype=np.float32)
    while h.shape[0] < n:
        h = np.block([[h, h], [h, -h]])
    return np.ascontiguousarray(h, dtype=np.float32)


def _perm_q():
    p = np.empty(D, dtype=np.int64)
    for h in range(NH):
        g = h // 4
        tile = (g // 2) * 4 + (h % 4)
        half = g % 2
        r0 = tile * 128 + half * 64
        p[r0:r0 + 64] = np.arange(h * 64, h * 64 + 64)
    return p


def _mk_blocks(W, out_perm=None, in_perm=None):
    W2 = np.asarray(W, dtype=np.float32)
    if out_perm is not None:
        W2 = W2[out_perm]
    if in_perm is not None:
        W2 = W2[:, in_perm]
    O, I = W2.shape
    Mt, Kt = O // 128, I // 128
    a = W2.reshape(Mt, 128, Kt, 128).transpose(0, 3, 2, 1)
    return np.ascontiguousarray(a.reshape(Mt, 128, Kt * 128), dtype=np.float32)


def _vec_tiles(v, nt):
    return np.ascontiguousarray(np.asarray(v, dtype=np.float32).reshape(nt, 128).T)


def host_prepare(inputs):
    g = {}
    cu = np.asarray(inputs["L0_attn_cu_seqlens_q"]).ravel().tolist()
    seg_lens = [int(cu[i + 1] - cu[i]) for i in range(len(cu) - 1)]
    if cu[0] != 0 or any(sl != T for sl in seg_lens):
        raise NotImplementedError(f"kernel built for segments of {T}; got cu_seqlens={cu}")
    x = np.ascontiguousarray(
        np.asarray(inputs["L0_attn_hidden_states"], dtype=np.float32).reshape(8 * T, D))
    pq = _perm_q()
    idx = np.arange(D)
    pu = (idx & 127) * 16 + (idx >> 7)
    idx8 = np.arange(INTER)
    pd = (idx8 & 127) * 64 + (idx8 >> 7)

    g["wq"] = _mk_blocks(inputs["L0_attn_Wq"], out_perm=pq)
    g["wk"] = _mk_blocks(inputs["L0_attn_Wk"])
    g["wv"] = _mk_blocks(inputs["L0_attn_Wv"])
    g["wo"] = _mk_blocks(inputs["L0_attn_Wo"], in_perm=pq)
    g["wu"] = _mk_blocks(inputs["L0_ffn_W_up"], in_perm=pu)
    g["wg"] = _mk_blocks(inputs["L0_ffn_W_gate"], in_perm=pu)
    wd = _mk_blocks(inputs["L0_ffn_W_down"], in_perm=pd)
    g["wd"] = np.ascontiguousarray(wd.reshape(16, 128, 4, 2048).transpose(0, 2, 1, 3))

    wsug = np.asarray(inputs["L0_ffn_Wscale_ug"], np.float32)
    g["fall"] = np.ascontiguousarray(np.concatenate([
        _vec_tiles(inputs["L0_ffn_SU_ug"], 16),
        _vec_tiles(inputs["L0_ffn_SU_dp"], 64),
        _vec_tiles(wsug[:INTER], 64),
        _vec_tiles(wsug[INTER:], 64),
        _vec_tiles(inputs["L0_ffn_Wscale_dp"], 16)], axis=1))

    kk, qq = np.meshgrid(np.arange(128), np.arange(128), indexing="ij")
    g["cstall"] = np.ascontiguousarray(np.concatenate([
        _hadamard(128),
        np.kron(_hadamard(16), np.eye(8, dtype=np.float32)),
        np.kron(_hadamard(64), np.eye(2, dtype=np.float32)),
        (kk <= qq).astype(np.float32),
        np.eye(128, dtype=np.float32)], axis=1))

    shards = [np.ascontiguousarray(x[c * T:(c + 1) * T, :].T.reshape(16, 128, T))
              for c in range(NCORES)]
    return g, shards



class Sem:
    def __init__(self, es, nc, name):
        self.h = es.enter_context(nc.semaphore(name))
        self.n = 0


class Prog:
    KEYS = ("t", "s", "v", "g", "y")
    SELF_ORDER = ("s", "v")

    def __init__(self):
        self.ops = {k: [] for k in self.KEYS}
        self.last = {}
        self.hi = {k: {} for k in self.KEYS}

    def _wait_h(self, eng, semh, val):
        key = id(semh)
        if self.hi[eng].get(key, -1) >= val:
            return
        self.hi[eng][key] = val
        self.ops[eng].append(
            (lambda e, _s=semh, _v=val: e.wait_ge(_s, _v), None, 0))

    def add(self, eng, fn, sem=None, k=1):
        if eng in self.SELF_ORDER and eng in self.last:
            lsem, lval = self.last[eng]
            self._wait_h(eng, lsem, lval)
        self.ops[eng].append((fn, sem.h if sem is not None else None, k))
        if sem is not None:
            sem.n += k
            if eng in self.SELF_ORDER:
                self.last[eng] = (sem.h, sem.n)
            return sem.n

    def wait(self, eng, sem, val):
        if val is None:
            raise RuntimeError("wait target is None")
        self._wait_h(eng, sem.h, val)

    def run(self, handle, key):
        for fn, semh, k in self.ops[key]:
            inst = fn(handle)
            if semh is not None:
                inst.then_inc(semh, k)


def build_program(stop_after="all", taps=()):
    nc = bass.Bass("TRN2", target_bir_lowering=False, debug=False)
    es = ExitStack()

    dx = nc.dram_tensor("x", [16, 128, T], F32, kind="ExternalInput").ap()
    dwq = nc.dram_tensor("wq", [16, 128, 2048], F32, kind="ExternalInput").ap()
    dwk = nc.dram_tensor("wk", [4, 128, 2048], F32, kind="ExternalInput").ap()
    dwv = nc.dram_tensor("wv", [4, 128, 2048], F32, kind="ExternalInput").ap()
    dwo = nc.dram_tensor("wo", [16, 128, 2048], F32, kind="ExternalInput").ap()
    dwu = nc.dram_tensor("wu", [64, 128, 2048], F32, kind="ExternalInput").ap()
    dwg = nc.dram_tensor("wg", [64, 128, 2048], F32, kind="ExternalInput").ap()
    dwd = nc.dram_tensor("wd", [16, 4, 128, 2048], F32, kind="ExternalInput").ap()
    dcst = nc.dram_tensor("cstall", [128, 640], F32, kind="ExternalInput").ap()
    dfm = nc.dram_tensor("fall", [128, 224], F32, kind="ExternalInput").ap()
    dy = nc.dram_tensor("y", [16, 128, T], F32, kind="ExternalOutput").ap()
    dz = nc.dram_tensor("zscratch", [64, 128, T], BF16, kind="Internal").ap()

    big0 = es.enter_context(nc.sbuf_tensor("big0", [128, 16 * T], BF16))
    big1 = es.enter_context(nc.sbuf_tensor("big1", [128, 16 * T], BF16))
    big2 = es.enter_context(nc.sbuf_tensor("big2", [128, 16 * T], BF16))
    big3 = es.enter_context(nc.sbuf_tensor("big3", [128, 16 * T], BF16))
    big4 = es.enter_context(nc.sbuf_tensor("big4", [128, 16 * T], BF16))
    scrA = es.enter_context(nc.sbuf_tensor("scrA", [128, 4096], BF16))
    scrB = es.enter_context(nc.sbuf_tensor("scrB", [128, 4096], BF16))
    vtbuf = es.enter_context(nc.sbuf_tensor("vtbuf", [128, 64 * 65], BF16))
    esb = es.enter_context(nc.sbuf_tensor("esb", [128, 2048], BF16))
    wring = es.enter_context(nc.sbuf_tensor("wring", [128, RING_N * RING_W], BF16))
    cst = es.enter_context(nc.sbuf_tensor("cst", [128, 712], BF16))
    fmisc = es.enter_context(nc.sbuf_tensor("fmisc", [128, 352], F32))

    hadU, hadA, hadB = cst[:, 0:128], cst[:, 128:256], cst[:, 256:384]
    tri, iden2 = cst[:, 384:512], cst[:, 512:640]
    ones_bf = cst[:, 640:641]
    onesrow_bf = cst[0:1, 644:708]
    su_eff = fmisc[:, 0:16]
    cu_eff = fmisc[:, 16:80]
    wsu_raw = fmisc[:, 80:144]
    wsg64 = fmisc[:, 144:208]
    wsd64 = fmisc[:, 208:224]
    ones_f = fmisc[:, 224:352]

    pA = es.enter_context(nc.psum_tensor("pA", [128, 1024], F32))
    pB = es.enter_context(nc.psum_tensor("pB", [128, 1024], F32))
    pS = es.enter_context(nc.psum_tensor("pS", [128, 2048], F32))
    pC = pS[:, 0:1024]
    pD = pS[:, 1024:2048]
    pT = pS[:, 1024:2048].bitcast(BF16)

    s_s = Sem(es, nc, "s_s")
    s_pe = Sem(es, nc, "s_pe")
    s_sc = Sem(es, nc, "s_sc")
    s_ve = Sem(es, nc, "s_ve")
    s_gp = Sem(es, nc, "s_gp")
    s_pre = Sem(es, nc, "s_pre")
    rs = [Sem(es, nc, f"rs{k}") for k in range(RING_N)]
    s_z = [Sem(es, nc, "s_z0"), Sem(es, nc, "s_z1")]
    s_x = [Sem(es, nc, "s_x0"), Sem(es, nc, "s_x1")]
    s_bd = [Sem(es, nc, "s_bd0"), Sem(es, nc, "s_bd1")]
    s_od = [Sem(es, nc, "s_od0"), Sem(es, nc, "s_od1")]
    s_op = [Sem(es, nc, "s_op0"), Sem(es, nc, "s_op1")]

    P = Prog()

    b0 = big0.rearrange("p (t n) -> p t n", t=16)
    b1 = big1.rearrange("p (t n) -> p t n", t=16)
    b2 = big2.rearrange("p (t n) -> p t n", t=16)
    b3 = big3.rearrange("p (t n) -> p t n", t=16)
    b4 = big4.rearrange("p (t n) -> p t n", t=16)
    kbf = scrA.rearrange("p (t n) -> p t n", t=4)
    zb6 = scrA.rearrange("p (t n) -> p t n", t=4)
    vtmp2 = scrB[:, 0:2048].rearrange("p (t n) -> p t n", t=2)
    vt3 = vtbuf.rearrange("p (t n) -> p t n", t=64)
    r_f32 = big1.bitcast(F32).rearrange("p (t n) -> p t n", t=16)

    def h2B(v):
        r, tt = divmod(v, 16)
        return [b0, b1, b2, b4][r][:, tt, :]

    P.add("g", lambda g: g.dma_start(out=cst[:, 0:640], in_=dcst), s_pre, 16)
    P.add("g", lambda g: g.dma_start(out=fmisc[:, 0:224], in_=dfm), s_pre, 16)
    P.add("g", lambda g: g.dma_start(
        out=b0[:, :, :], in_=dx.rearrange("t p n -> p t n")), s_pre, 16)
    PRE_T = 48

    ringl = []
    ringl += [(f"wk{m}", dwk[m]) for m in range(4)]
    ringl += [(f"wv{m}", dwv[m]) for m in range(4)]
    ringl += [(f"wq{m}", dwq[m]) for m in range(16)]
    for half in range(2):
        ringl += [(f"wo{half}_{m}", dwo[m]) for m in range(16)]
    for m in range(64):
        ringl += [(f"wu{m}", dwu[m]), (f"wg{m}", dwg[m])]
    for m in range(16):
        ringl += [(f"wd{m}_{kq}", dwd[m, kq]) for kq in range(4)]

    g_target = {lab: (rs[j % RING_N], 16 * (j // RING_N + 1))
                for j, (lab, _s) in enumerate(ringl)}
    ring_idx = {lab: j for j, (lab, _s) in enumerate(ringl)}
    ring_free = {}
    emitted = [0]

    def ring_slot(lab):
        j = ring_idx[lab] % RING_N
        return wring[:, j * RING_W:(j + 1) * RING_W]

    def emit_loads():
        i = emitted[0]
        while i < len(ringl):
            lab, src = ringl[i]
            if i >= RING_N:
                if (i - RING_N) not in ring_free:
                    break
                P.wait("g", s_pe, ring_free[i - RING_N])
            P.add("g", lambda g, _d=ring_slot(lab), _s=src:
                  g.dma_start(out=_d, in_=_s), rs[i % RING_N], 16)
            i += 1
        emitted[0] = i

    tapreg = {
        "xbf": (big0, [128, 16 * T], BF16), "q": (big1, [128, 16 * T], BF16),
        "kv": (scrA, [128, 6144], BF16), "vt": (vtbuf, [128, 64 * 65], BF16),
        "attn": (big2, [128, 16 * T], BF16), "x2": (big3, [128, 16 * T], BF16),
        "x1": (big4, [128, 16 * T], BF16),
        "h0": (big0, [128, 16 * T], BF16), "h1": (big1, [128, 16 * T], BF16),
        "h2": (big2, [128, 16 * T], BF16), "h4": (big4, [128, 16 * T], BF16),
    }
    dtaps = {name: nc.dram_tensor("tap_" + name, tapreg[name][1], tapreg[name][2],
                                  kind="ExternalOutput").ap() for name in taps}

    def finish():
        emit_loads()
        for name in taps:
            P.wait("y", s_ve, s_ve.n)
            P.wait("y", s_sc, s_sc.n)
            P.wait("y", s_pe, s_pe.n)
            P.add("y", lambda y, _d=dtaps[name], _a=tapreg[name][0][:, :]:
                  y.dma_start(out=_d, in_=_a), s_s, 16)
        if s_s.n:
            P.wait("y", s_s, s_s.n)
        for _sm in (s_od[0], s_od[1], s_op[0], s_op[1]):
            if _sm.n:
                P.wait("y", _sm, _sm.n)
        with nc.Block() as block:
            block.tensor(lambda t: P.run(t, "t"))
            block.scalar(lambda s: P.run(s, "s"))
            block.vector(lambda v: P.run(v, "v"))
            block.gpsimd(lambda g: P.run(g, "g"))
            block.sync(lambda y: P.run(y, "y"))
        es.close()
        return nc

    emit_loads()
    P.add("g", lambda g: g.memset(ones_bf, 1.0), s_gp)
    P.add("g", lambda g: g.memset(cst[:, 644:708], 1.0), s_gp)
    P.add("g", lambda g: g.memset(ones_f, 1.0), s_gp)
    P.add("g", lambda g: g.memset(vt3[:, :, 64:65], 1.0), s_gp)
    GP_MEMSETS = s_gp.n

    P.add("v", lambda v: v.memset(pS[:, :], 0.0), s_ve)
    P.wait("v", s_pre, PRE_T)
    c1 = float(1.0 / (64.0 * np.sqrt(2048.0)))
    c2 = float(1.0 / np.sqrt(8192.0))
    P.add("v", lambda v: v.tensor_scalar(out=su_eff, in0=su_eff, scalar1=c1,
                                         scalar2=None, op0=ALU.mult), s_ve)
    P.add("v", lambda v: v.tensor_tensor(out=cu_eff, in0=cu_eff, in1=wsu_raw,
                                         op=ALU.mult), s_ve)
    P.add("v", lambda v: v.tensor_scalar(out=cu_eff, in0=cu_eff, scalar1=c2,
                                         scalar2=None, op0=ALU.mult), s_ve)
    P.add("v", lambda v: v.tensor_scalar(out=wsg64, in0=wsg64, scalar1=64.0,
                                         scalar2=None, op0=ALU.mult), s_ve)
    VE_SCALES = P.add("v", lambda v: v.tensor_scalar(out=wsd64, in0=wsd64, scalar1=64.0,
                                                     scalar2=None, op0=ALU.mult), s_ve)

    evac_of = {}
    kv_plan = ([(f"wk{m}", m) for m in range(4)] +
               [(f"wv{m}", m) for m in range(4)])
    tb_evac = {}
    kbf_evac = {}

    P.wait("t", s_pre, PRE_T)
    for nb, (lab, m) in enumerate(kv_plan):
        ps = pA if nb % 2 == 0 else pB
        P.wait("t", *g_target[lab])
        if nb >= 2:
            sm, tv = evac_of[kv_plan[nb - 2][0]]
            P.wait("t", sm, tv)
        w3 = ring_slot(lab).rearrange("p (k j) -> p k j", k=16)

        def mmqkv(t, _ps=ps, _w3=w3):
            i = None
            for nh in range(2):
                for k in range(16):
                    i = t.matmul(_ps[:, nh * 512:(nh + 1) * 512], _w3[:, k, :],
                                 b0[:, k, nh * 512:(nh + 1) * 512],
                                 start=(k == 0), stop=(k == 15))
            return i
        done = P.add("t", mmqkv, s_pe)
        ring_free[ring_idx[lab]] = s_pe.n

        if lab.startswith("wk"):
            P.wait("v", s_pe, done)
            ev = P.add("v", lambda v, _m=m, _ps=ps: v.tensor_copy(
                out=kbf[:, _m, :], in_=_ps[:]), s_ve)
            evac_of[lab] = (s_ve, ev)
            kbf_evac[m] = ev
        else:
            P.wait("v", s_pe, done)
            ev = P.add("v", lambda v, _m=m, _ps=ps: v.tensor_copy(
                out=vtmp2[:, _m % 2, :], in_=_ps[:]), s_ve)
            evac_of[lab] = (s_ve, ev)
            P.wait("t", s_ve, ev)
            P.wait("t", s_pre, PRE_T)
            if m >= 1:
                P.wait("t", s_ve, tb_evac[m - 1])

            def trans(t, _m=m):
                vt = vtmp2[:, _m % 2, :]
                i = None
                for kt in range(8):
                    i = t.transpose(pT[:, kt * 128:(kt + 1) * 128],
                                    vt[:, kt * 128:(kt + 1) * 128], iden2)
                return i
            tb_done = P.add("t", trans, s_pe)
            P.wait("v", s_pe, tb_done)
            P.wait("v", s_gp, GP_MEMSETS)
            pv8 = pT[:, 0:1024].rearrange("p (a b) -> p a b", a=8)
            P.add("v", lambda v, _m=m: v.tensor_copy(
                out=vt3[:, 16 * _m:16 * _m + 8, 0:64], in_=pv8[:, :, 0:64]), s_ve)
            tb_evac[m] = P.add("v", lambda v, _m=m: v.tensor_copy(
                out=vt3[:, 16 * _m + 8:16 * _m + 16, 0:64],
                in_=pv8[:, :, 64:128]), s_ve)
        emit_loads()

    if stop_after == "ph1":
        return finish()

    q_evac = {}
    wq_state = {"m": 0, "c": 0}

    def emit_wq_chunk():
        m, c = wq_state["m"], wq_state["c"]
        if m >= 16:
            return False
        lab = f"wq{m}"
        if c == 0:
            P.wait("t", *g_target[lab])
            if m >= 1:
                P.wait("t", s_sc, q_evac[m - 1])
            else:
                P.wait("t", evac_of["wv2"][0], evac_of["wv2"][1])
        nh, k0 = c // 4, (c % 4) * 4
        w3 = ring_slot(lab).rearrange("p (k j) -> p k j", k=16)

        def chunk(t, _w3=w3, _nh=nh, _k0=k0):
            i = None
            for k in range(_k0, _k0 + 4):
                i = t.matmul(pA[:, _nh * 512:(_nh + 1) * 512], _w3[:, k, :],
                             b0[:, k, _nh * 512:(_nh + 1) * 512],
                             start=(k == 0), stop=(k == 15))
            return i
        d = P.add("t", chunk, s_pe)
        if c == 7:
            ring_free[ring_idx[lab]] = s_pe.n
            P.wait("s", s_pe, d)
            q_evac[m] = P.add("s", lambda s, _m=m: s.copy(
                out=b1[:, _m, :], in_=pA[:, :]), s_sc)
            wq_state["m"], wq_state["c"] = m + 1, 0
            emit_loads()
        else:
            wq_state["c"] = c + 1
        return True

    def ensure_wq(m):
        while wq_state["m"] <= m and (wq_state["m"] < 16):
            emit_wq_chunk()

    ehAB = [esb[:, 0:2048], scrB[:, 0:2048]]
    rbsb = scrB[:, 2048:3072]
    row_f = [big3[:, 0:2048].bitcast(F32)[64:65, 0:1024],
             big3[:, 4096:6144].bitcast(F32)[64:65, 0:1024]]
    recbf = [big3[64:65, 2048:3072], big3[64:65, 3072:4096]]
    onesblk = cst[:, 644:708]

    st = {"expA": None, "expB": None,
          "jS": 0,
          "eh": [None, None],
          "tl": {1: {"mult": None, "copyA": None, "copyB": None}},
          "pend": {1: None}}

    def c0f(qc, kt):
        return max(0, 128 * kt - 512 * qc)

    r0_f32 = big4.bitcast(F32).rearrange("p (t n) -> p t n", t=16)
    rsq0 = big3[:, 6144:10240].bitcast(F32)
    wo0_state = {"m": 0, "c": 0, "on": False, "gate": None}
    r0_done = {}
    st0_done = {}
    st0_q = []
    gsq0 = {}

    def emit_st0(upto):
        while st0_q and st0_q[0] <= upto:
            m_ = st0_q.pop(0)
            P.wait("t", s_ve, r0_done[m_])
            P.wait("t", s_gp, gsq0[m_])

            def st0mm(t, _m=m_):
                t.matmul(pA[0:1, 512:1024], ones_f[:, 0:1], r0_f32[:, _m, :],
                         start=(_m == 0), stop=(_m == 15))
                return t.matmul(pA[32:33, 512:1024], ones_f[:, 0:1],
                                rsq0[:, (_m % 4) * 512:(_m % 4) * 512 + 512],
                                start=(_m == 0), stop=(_m == 15),
                                tile_position=(0, 32))
            st0_done[m_] = P.add("t", st0mm, s_pe)

    def emit_wo0_chunk():
        m, c = wo0_state["m"], wo0_state["c"]
        if not wo0_state["on"] or m >= 16:
            return False
        lab = f"wo0_{m}"
        if c == 0:
            emit_st0(m - 3)
            P.wait("t", *g_target[lab])
            if m >= 1:
                P.wait("t", s_ve, r0_done[m - 1])
            else:
                P.wait("t", s_ve, wo0_state["gate"])
                P.wait("t", s_sc, q_evac[15])
        w3 = ring_slot(lab).rearrange("p (k j) -> p k j", k=16)

        def chunk(t, _w3=w3, _k0=c * 4):
            i_ = None
            for k in range(_k0, _k0 + 4):
                i_ = t.matmul(pA[:, 0:512], _w3[:, k, :], b2[:, k, 0:512],
                              start=(k == 0), stop=(k == 15))
            return i_
        d = P.add("t", chunk, s_pe)
        if c < 3:
            wo0_state["c"] = c + 1
            return True
        ring_free[ring_idx[lab]] = s_pe.n
        P.wait("v", s_pe, d)
        r0_done[m] = P.add("v", lambda v, _m=m: v.tensor_tensor(
            out=r0_f32[:, _m, :], in0=pA[:, 0:512], in1=b0[:, _m, 0:512],
            op=ALU.add), s_ve)
        P.wait("g", s_ve, r0_done[m])
        if m >= 4:
            P.wait("g", s_pe, st0_done[m - 4])
        gsq0[m] = P.add("g", lambda g, _m=m: g.tensor_mul(
            out=rsq0[:, (_m % 4) * 512:(_m % 4) * 512 + 512],
            in0=r0_f32[:, _m, :], in1=r0_f32[:, _m, :]), s_gp)
        st0_q.append(m)
        wo0_state["m"], wo0_state["c"] = m + 1, 0
        emit_loads()
        return True

    def emit_filler():
        if not emit_wq_chunk():
            emit_wo0_chunk()

    def emit_norm_recips(pV, rpar):
        P.add("s", lambda s, _r=rpar, _p=pV: s.activation(
            out=row_f[_r], in_=_p[64:65, 0:1024], func=AF.Ln), s_sc)
        return P.add("s", lambda s, _r=rpar: s.activation(
            out=recbf[_r], in_=row_f[_r], func=AF.Exp, scale=-1.0), s_sc)

    def emit_norm_finish(qt, qc, cast, tk, pV, rpar):
        tl = st["tl"][tk]
        qcc = slice(qc * 512, (qc + 1) * 512)
        P.wait("t", s_sc, cast)
        if tl["copyA"] is not None:
            P.wait("t", s_ve, tl["copyA"])

        def bcA(t, _p=pV, _r=rpar):
            return t.matmul(_p[64:128, 0:512], onesblk[64:65, 0:64],
                            recbf[_r][0:1, 0:512],
                            start=True, stop=True, tile_position=(64, 64))
        bA = P.add("t", bcA, s_pe)
        if tl["copyB"] is not None:
            P.wait("t", s_ve, tl["copyB"])

        def bcB(t, _p=pV, _r=rpar):
            return t.matmul(_p[64:128, 512:1024], onesblk[64:65, 0:64],
                            recbf[_r][0:1, 512:1024],
                            start=True, stop=True, tile_position=(64, 64))
        bB = P.add("t", bcB, s_pe)
        P.wait("v", s_pe, bA)
        tl["copyA"] = P.add("v", lambda v, _p=pV: v.tensor_copy(
            out=rbsb[0:64, 0:512], in_=_p[64:128, 0:512]), s_ve)
        P.add("v", lambda v, _q=qt, _c=qcc, _p=pV: v.tensor_tensor(
            out=b2[0:64, _q, _c], in0=_p[0:64, 0:512], in1=rbsb[0:64, 0:512],
            op=ALU.mult), s_ve)
        P.wait("v", s_pe, bB)
        tl["copyB"] = P.add("v", lambda v, _p=pV: v.tensor_copy(
            out=rbsb[64:128, 512:1024], in_=_p[64:128, 512:1024]), s_ve)
        tl["mult"] = P.add("v", lambda v, _q=qt, _c=qcc, _p=pV: v.tensor_tensor(
            out=b2[64:128, _q, _c], in0=_p[0:64, 512:1024],
            in1=rbsb[64:128, 512:1024], op=ALU.mult), s_ve)
        if qt == 15 and qc == 0:
            wo0_state["on"] = True
            wo0_state["gate"] = tl["mult"]

    def emit_qc_iter(p, gp, i, qc):
        qt = 4 * gp + i
        if qc == 0:
            ensure_wq(qt)
        exp_ctr = {}
        mask_ctr = {}
        exp_par = {}
        q_hat = 16 * qc + p

        def emit_pv_quad(j, last_kt):
            kt0, kt1 = 2 * j, 2 * j + 1
            for side in ("A", "B"):
                for kt in (kt0, kt1):
                    if (side, kt) in mask_ctr:
                        P.wait("t", s_ve, mask_ctr[(side, kt)])
            P.wait("t", s_sc, exp_ctr[j])
            if kt0 == 0 and st["tl"][1]["mult"] is not None:
                P.wait("t", s_ve, st["tl"][1]["mult"])
            par = exp_par[j]

            def pvq(t, _gp=gp, _qc=qc, _k0=kt0, _k1=kt1, _last=last_kt,
                    _par=par):
                i_ = None
                for kt in (_k0, _k1):
                    c0 = c0f(_qc, kt)
                    for g, eoff, off in ((2 * _gp, 0, 0), (2 * _gp + 1, 1024, 512)):
                        i_ = t.matmul(
                            pB[0:65, off + c0:off + 512],
                            vt3[:, g * 8 + kt, 0:65],
                            ehAB[_par][:, eoff + (kt % 2) * 512 + c0:
                                       eoff + (kt % 2) * 512 + 512],
                            start=(kt == 0), stop=(kt == _last))
                return i_
            d = P.add("t", pvq, s_pe)
            st["eh"][par] = d
            return d

        kts = [0, 1, 2, 3] if qc == 0 else list(range(8))
        njs = len(kts) // 2
        for j in range(njs):
            kt0, kt1 = 2 * j, 2 * j + 1
            if p == 0 and qc == 0 and j == 0:
                P.wait("t", s_sc, q_evac[qt])
                P.wait("t", s_ve, kbf_evac[gp])
                P.wait("t", s_ve, tb_evac[3])
                P.wait("t", s_gp, GP_MEMSETS)
            elif qc == 0 and j == 0:
                P.wait("t", s_sc, q_evac[qt])

            def spair(t, _gp=gp, _qt=qt, _qc=qc, _k0=kt0, _k1=kt1,
                      _qb=0, _soff=0):
                i_ = None
                for kt in (_k0, _k1):
                    c0 = c0f(_qc, kt)
                    i_ = t.matmul(
                        pS[:, _soff + (kt % 2) * 512 + c0:
                           _soff + (kt % 2) * 512 + 512],
                        kbf[_qb:_qb + 64, _gp, kt * 128:(kt + 1) * 128],
                        b1[_qb:_qb + 64, _qt, _qc * 512 + c0:(_qc + 1) * 512],
                        start=True, stop=True)
                return i_
            if st["expA"] is not None:
                P.wait("t", s_sc, st["expA"])
            sqA = P.add("t", lambda t, _f=spair: _f(t), s_pe)
            if st["expB"] is not None:
                P.wait("t", s_sc, st["expB"])
            sqB = P.add("t", lambda t, _f=spair: _f(t, _qb=64, _soff=1024), s_pe)

            if j == min(1, njs - 1) and st["pend"][1] is not None:
                emit_filler()
                emit_filler()
                emit_norm_finish(*st["pend"][1])
                st["pend"][1] = None

            emit_filler()

            c0e = c0f(qc, kt0)
            par = st["jS"] % 2
            st["jS"] += 1
            exp_par[j] = par
            if p == 0 and qc == 0 and j <= 1:
                P.wait("s", s_ve, tb_evac[3])
            if st["eh"][par] is not None:
                P.wait("s", s_pe, st["eh"][par])
            P.wait("s", s_pe, sqA)
            ecA = P.add(
                "s", lambda s, _eh=ehAB[par], _c0=c0e: s.activation(
                    out=_eh[:, _c0:1024], in_=pS[:, _c0:1024], func=AF.Exp,
                    scale=0.125), s_sc)
            P.wait("s", s_pe, sqB)
            ecB = P.add(
                "s", lambda s, _eh=ehAB[par], _c0=c0e: s.activation(
                    out=_eh[:, 1024 + _c0:2048], in_=pS[:, 1024 + _c0:2048],
                    func=AF.Exp, scale=0.125), s_sc)
            exp_ctr[j] = ecB
            exp_ctr[(j, "A")] = ecA
            st["expA"] = ecA
            st["expB"] = ecB
            for kt in (kt0, kt1):
                if 128 * kt >= 512 * qc:
                    c0 = c0f(qc, kt)
                    for side, eoff, ec in (("A", 0, ecA), ("B", 1024, ecB)):
                        P.wait("v", s_sc, ec)
                        mask_ctr[(side, kt)] = P.add(
                            "v", lambda v, _eh=ehAB[par],
                            _o=eoff + (kt % 2) * 512 + c0:
                            v.tensor_tensor(out=_eh[:, _o:_o + 128],
                                            in0=_eh[:, _o:_o + 128], in1=tri,
                                            op=ALU.mult), s_ve)

            if j >= 1:
                emit_pv_quad(j - 1, kts[-1])
                emit_filler()
        pv_last = emit_pv_quad(njs - 1, kts[-1])
        P.wait("s", s_pe, pv_last)
        st["pend"][1] = (qt, qc, emit_norm_recips(pB, q_hat % 2), 1, pB,
                         q_hat % 2)
        emit_filler()
        emit_loads()

    pairs = [(g_, i_) for g_ in range(4) for i_ in range(4)]
    for p, (gp, i) in enumerate(pairs):
        emit_qc_iter(p, gp, i, 0)
    ensure_wq(15)
    for p, (gp, i) in enumerate(pairs):
        emit_qc_iter(p, gp, i, 1)
    emit_norm_finish(*st["pend"][1])
    st["pend"][1] = None
    b2_ready = st["tl"][1]["mult"]
    while emit_wo0_chunk():
        pass
    emit_st0(15)
    SC_AB = s_sc.n

    if stop_after == "ph2":
        return finish()

    rowA = scrB[0:1, 0:1024].bitcast(F32)
    rowB = scrB[0:1, 1024:2048].bitcast(F32)
    rsq = scrB[:, 2048:4096].bitcast(F32)
    msq0 = esb[:, 0:1024].bitcast(F32)[0:1, 0:512]
    x2_last = None

    def emit_ln_tail(sum_ap, sq_ap, rX, half, cb=None):
        P.add("v", lambda v: v.tensor_scalar(out=rowA, in0=sum_ap,
                                             scalar1=float(1.0 / D), scalar2=None,
                                             op0=ALU.mult), s_ve)
        P.add("v", lambda v: v.tensor_tensor(out=rowB, in0=rowA, in1=rowA,
                                             op=ALU.mult), s_ve)
        stats_read = P.add("v", lambda v: v.tensor_scalar(
            out=msq0, in0=sq_ap, scalar1=float(1.0 / D), scalar2=None,
            op0=ALU.mult), s_ve)
        vv_done = P.add("v", lambda v: v.tensor_tensor(
            out=rowB, in0=msq0, in1=rowB, op=ALU.subtract), s_ve)
        vv_done = P.add("v", lambda v: v.tensor_scalar(
            out=rowB, in0=rowB, scalar1=float(EPS), scalar2=None, op0=ALU.add), s_ve)
        P.wait("s", s_ve, vv_done)
        P.add("s", lambda s: s.activation(out=rowB, in_=rowB, func=AF.Ln), s_sc)
        a_done = P.add("s", lambda s: s.activation(out=rowB, in_=rowB,
                                                   func=AF.Exp, scale=-0.5), s_sc)
        P.wait("v", s_sc, a_done)
        P.add("v", lambda v: v.tensor_tensor(out=rowA, in0=rowA, in1=rowB,
                                             op=ALU.mult), s_ve)
        b_done = P.add("v", lambda v: v.tensor_scalar(
            out=rowA, in0=rowA, scalar1=-1.0, scalar2=None, op0=ALU.mult), s_ve)
        P.wait("t", s_ve, b_done)

        def bcln(t):
            t.matmul(pD[:, 0:512], ones_f[0:1, 0:128], rowB,
                     start=True, stop=True)
            return t.matmul(pD[:, 512:1024], ones_f[0:1, 0:128], rowA,
                            start=True, stop=True)
        bc1 = P.add("t", bcln, s_pe)

        x2t = rsq[:, 512:1024]
        P.wait("v", s_pe, bc1)
        last = None
        for m in range(16):
            P.add("v", lambda v, _m=m, _r=rX: v.tensor_tensor(
                out=x2t, in0=_r[:, _m, :], in1=pD[:, 0:512],
                op=ALU.mult), s_ve)
            last = P.add("v", lambda v, _m=m, _h=half: v.tensor_tensor(
                out=b3[:, _m, _h * 512:(_h + 1) * 512], in0=x2t,
                in1=pD[:, 512:1024], op=ALU.add), s_ve)
            if cb is not None:
                cb(m, last)
        return stats_read, last

    P.wait("v", s_pe, st0_done[15])
    P.wait("t", s_sc, SC_AB)
    LN0_PA_READ, x2_h0_last = emit_ln_tail(
        pA[0:1, 512:1024], pA[32:33, 512:1024], r0_f32, 0)

    xp0 = esb.rearrange("p (a n) -> p a n", a=4)
    ph40 = {"t": 0, "j": 0, "on": False}
    z1h0 = {}
    z1ev0 = {}
    z1out0 = {}
    xpd0 = {}
    x1rd0 = {}
    h16x0 = {}
    x1ev0 = {}

    def emit_ph40_chunk():
        if not ph40["on"]:
            return False
        t_i, j = ph40["t"], ph40["j"]
        if t_i < 16:
            if t_i >= 2:
                P.wait("v", s_pe, z1h0[t_i - 2])
            xpd0[t_i] = P.add("v", lambda v, _t=t_i: v.tensor_scalar(
                out=xp0[:, _t % 2, :], in0=b3[:, _t, 0:512],
                scalar1=su_eff[:, _t:_t + 1], scalar2=None, op0=ALU.mult), s_ve)
            P.wait("t", s_ve, xpd0[t_i])
            if t_i == 0:
                P.wait("t", s_ve, b2_ready)
                P.wait("t", s_pre, PRE_T)
            if t_i >= 2:
                P.wait("t", s_sc, z1ev0[t_i - 2])
            z1h0[t_i] = P.add("t", lambda t, _t=t_i: t.matmul(
                pB[:, (_t % 2) * 512:(_t % 2) * 512 + 512], hadU,
                xp0[:, _t % 2, :], start=True, stop=True), s_pe)
            P.wait("s", s_pe, z1h0[t_i])
            jj = t_i // 2
            if jj >= 2 and t_i % 2 == 0:
                P.wait("s", s_z[jj % 2], z1out0[jj - 2])
            z1ev0[t_i] = P.add("s", lambda s, _t=t_i: s.copy(
                out=zb6[:, (_t // 2 % 2) * 2 + (_t % 2), 0:512],
                in_=pB[:, (_t % 2) * 512:(_t % 2) * 512 + 512]), s_sc)
            if t_i % 2 == 1:
                P.wait("y", s_sc, z1ev0[t_i])
                z1out0[jj] = P.add("y", lambda y, _j=jj: y.dma_start(
                    out=dz[2 * _j:2 * _j + 2].rearrange("t p n -> p t n")[:, :, 0:512],
                    in_=zb6[:, (_j % 2) * 2:(_j % 2) * 2 + 2, 0:512]), s_z[jj % 2], 16)
            ph40["t"] = t_i + 1
            return True
        if j < 8:
            if j == 0:
                P.wait("y", s_z[0], z1out0[6])
                P.wait("y", s_z[1], z1out0[7])
            if j >= 2:
                P.wait("y", s_pe, h16x0[j - 2])
            x1rd0[j] = P.add("y", lambda y, _j=j: y.dma_start(
                out=zb6[:, (_j % 2) * 2:(_j % 2) * 2 + 2, 0:512],
                in_=dz16[:, 2 * _j:2 * _j + 2, 0:512]), s_x[j % 2], 16)
            P.wait("t", s_x[j % 2], x1rd0[j])
            if j == 0:
                P.wait("t", s_pre, PRE_T)
                P.wait("t", s_sc, z1ev0[15])
            else:
                P.wait("t", s_sc, x1ev0[2 * (j - 1) + 1])

            def h16(t, _j=j):
                i_ = None
                for uu in range(2):
                    i_ = t.matmul(pB[:, uu * 512:(uu + 1) * 512], hadA,
                                  zb6[:, (_j % 2) * 2 + uu, 0:512],
                                  start=True, stop=True)
                return i_
            h16x0[j] = P.add("t", h16, s_pe)
            for uu in range(2):
                P.wait("s", s_pe, h16x0[j])
                x1ev0[2 * j + uu] = P.add("s", lambda s, _j=j, _uu=uu: s.copy(
                    out=b4[:, 2 * _j + _uu, 0:512],
                    in_=pB[:, _uu * 512:(_uu + 1) * 512]), s_sc)
            ph40["j"] = j + 1
            return True
        return False

    dz16 = dz[0:16].rearrange("t (b u) n -> (t b) u n", u=16)
    ph40["on"] = True

    o_done = {}
    r_done = {}
    sq_done = {}
    st_done = {}
    for m in range(17):
        if m < 16:
            lab = f"wo1_{m}"
            P.wait("t", *g_target[lab])
            if m >= 1:
                P.wait("t", s_ve, r_done[m - 1])
            else:
                P.wait("t", s_ve, LN0_PA_READ)
            w3 = ring_slot(lab).rearrange("p (k j) -> p k j", k=16)

            def omm(t, _w3=w3):
                i = None
                for k in range(16):
                    i = t.matmul(pA[:, 0:512], _w3[:, k, :],
                                 b2[:, k, 512:1024],
                                 start=(k == 0), stop=(k == 15))
                return i
            o_done[m] = P.add("t", omm, s_pe)
            ring_free[ring_idx[lab]] = s_pe.n

            P.wait("v", s_pe, o_done[m])
            r_done[m] = P.add("v", lambda v, _m=m: v.tensor_tensor(
                out=r_f32[:, _m, :], in0=pA[:, 0:512],
                in1=b0[:, _m, 512:1024], op=ALU.add), s_ve)

            P.wait("s", s_ve, r_done[m])
            if m >= 2:
                P.wait("s", s_pe, st_done[m - 2])
            sq_done[m] = P.add("s", lambda s, _m=m: s.activation(
                out=rsq[:, (_m % 2) * 512:(_m % 2) * 512 + 512],
                in_=r_f32[:, _m, :], func=AF.Square), s_sc)
        if m >= 1:
            mm_ = m - 1
            P.wait("t", s_ve, r_done[mm_])
            P.wait("t", s_sc, sq_done[mm_])

            def stmm(t, _m=mm_):
                t.matmul(pC[0:1, 0:512], ones_f[:, 0:1], r_f32[:, _m, :],
                         start=(_m == 0), stop=(_m == 15))
                return t.matmul(pC[0:1, 512:1024], ones_f[:, 0:1],
                                rsq[:, (_m % 2) * 512:(_m % 2) * 512 + 512],
                                start=(_m == 0), stop=(_m == 15))
            st_done[mm_] = P.add("t", stmm, s_pe)
        emit_ph40_chunk()
        emit_ph40_chunk()
        emit_loads()
    while emit_ph40_chunk():
        pass

    xp1 = scrB.rearrange("p (a n) -> p a n", a=8)
    z1h1 = {}
    z1ev1 = {}
    z1out1 = {}

    def ph41_cb(t_i, x2_ctr):
        if t_i >= 2:
            P.wait("v", s_pe, z1h1[t_i - 2])
        xpd = P.add("v", lambda v, _t=t_i: v.tensor_scalar(
            out=xp1[:, _t % 2, :], in0=b3[:, _t, 512:1024],
            scalar1=su_eff[:, _t:_t + 1], scalar2=None, op0=ALU.mult), s_ve)
        P.wait("t", s_ve, xpd)
        if t_i >= 2:
            P.wait("t", s_sc, z1ev1[t_i - 2])
        z1h1[t_i] = P.add("t", lambda t, _t=t_i: t.matmul(
            pA[:, (_t % 2) * 512:(_t % 2) * 512 + 512], hadU,
            xp1[:, _t % 2, :], start=True, stop=True), s_pe)
        P.wait("s", s_pe, z1h1[t_i])
        jj = t_i // 2
        if jj >= 2 and t_i % 2 == 0:
            P.wait("s", s_z[jj % 2], z1out1[jj - 2])
        z1ev1[t_i] = P.add("s", lambda s, _t=t_i: s.copy(
            out=zb6[:, (_t // 2 % 2) * 2 + (_t % 2), 512:1024],
            in_=pA[:, (_t % 2) * 512:(_t % 2) * 512 + 512]), s_sc)
        if t_i % 2 == 1:
            P.wait("y", s_sc, z1ev1[t_i])
            z1out1[jj] = P.add("y", lambda y, _j=jj: y.dma_start(
                out=dz[2 * _j:2 * _j + 2].rearrange("t p n -> p t n")[:, :, 512:1024],
                in_=zb6[:, (_j % 2) * 2:(_j % 2) * 2 + 2, 512:1024]), s_z[jj % 2], 16)

    P.wait("v", s_pe, st_done[15])
    P.wait("t", s_ve, x2_h0_last)
    _, x2_last = emit_ln_tail(pC[0:1, 0:512], pC[0:1, 512:1024], r_f32, 1,
                              cb=ph41_cb)

    x1rd1 = {}
    h16x1 = {}
    x1ev1 = {}
    for j in range(8):
        if j == 0:
            P.wait("y", s_z[0], z1out1[6])
            P.wait("y", s_z[1], z1out1[7])
        if j >= 2:
            P.wait("y", s_pe, h16x1[j - 2])
        x1rd1[j] = P.add("y", lambda y, _j=j: y.dma_start(
            out=zb6[:, (_j % 2) * 2:(_j % 2) * 2 + 2, 512:1024],
            in_=dz16[:, 2 * _j:2 * _j + 2, 512:1024]), s_x[j % 2], 16)
        P.wait("t", s_x[j % 2], x1rd1[j])
        if j == 0:
            P.wait("t", s_sc, z1ev1[15])
        else:
            P.wait("t", s_sc, x1ev1[2 * (j - 1) + 1])

        def h16b(t, _j=j):
            i_ = None
            for uu in range(2):
                i_ = t.matmul(pA[:, uu * 512:(uu + 1) * 512], hadA,
                              zb6[:, (_j % 2) * 2 + uu, 512:1024],
                              start=True, stop=True)
            return i_
        h16x1[j] = P.add("t", h16b, s_pe)
        for uu in range(2):
            P.wait("s", s_pe, h16x1[j])
            x1ev1[2 * j + uu] = P.add("s", lambda s, _j=j, _uu=uu: s.copy(
                out=b4[:, 2 * _j + _uu, 512:1024],
                in_=pA[:, _uu * 512:(_uu + 1) * 512]), s_sc)
        emit_loads()
    X1_READY = x1ev1[15]

    if stop_after == "ph3":
        return finish()

    silu_f = scrB[:, 0:2048].bitcast(F32)
    htmp_f = scrB[:, 2048:4096].bitcast(F32)
    hbf = vtbuf[:, 0:4096].rearrange("p (a n) -> p a n", a=4)
    zev = {}
    zout = {}
    hevac = {}
    zh = {}
    silu_done = {}

    def emit_z(m):
        psGz = pC if m % 2 == 0 else pD
        P.wait("t", s_ve, hevac[m])
        P.wait("t", s_sc, silu_done[m])

        def zmm(t, _ps=psGz, _m=m):
            i = None
            for (c0, c1) in [(0, 512), (512, 1024)]:
                i = t.matmul(_ps[:, c0:c1], hadU, hbf[:, _m % 2, c0:c1],
                             start=True, stop=True)
            return i
        zh[m] = P.add("t", zmm, s_pe)
        P.wait("s", s_pe, zh[m])
        j = m // 2
        if j >= 2 and m % 2 == 0:
            P.wait("s", s_z[j % 2], zout[j - 2])
        zev[m] = P.add("s", lambda s, _ps=psGz, _m=m: s.copy(
            out=zb6[:, (_m // 2 % 2) * 2 + (_m % 2), :], in_=_ps[:]), s_sc)
        if m % 2 == 1:
            P.wait("y", s_sc, zev[m])
            zout[j] = P.add("y", lambda y, _j=j: y.dma_start(
                out=dz[2 * _j:2 * _j + 2].rearrange("t p n -> p t n"),
                in_=zb6[:, (_j % 2) * 2:(_j % 2) * 2 + 2, :]), s_z[j % 2], 16)

    for m in range(64):
        upl, gal = f"wu{m}", f"wg{m}"
        psU = pA if m % 2 == 0 else pB
        psG = pC if m % 2 == 0 else pD
        P.wait("t", *g_target[upl])
        if m == 0:
            P.wait("t", s_sc, X1_READY)
        if m >= 2:
            P.wait("t", s_ve, hevac[m - 2])

        def upmm(t, _ps=psU, _lab=upl):
            w3 = ring_slot(_lab).rearrange("p (k j) -> p k j", k=16)
            i = None
            for nh in range(2):
                for k in range(16):
                    i = t.matmul(_ps[:, nh * 512:(nh + 1) * 512], w3[:, k, :],
                                 b4[:, k, nh * 512:(nh + 1) * 512],
                                 start=(k == 0), stop=(k == 15))
            return i
        up_done = P.add("t", upmm, s_pe)
        ring_free[ring_idx[upl]] = s_pe.n

        P.wait("t", *g_target[gal])
        if m >= 2:
            P.wait("t", s_sc, zev[m - 2])

        def gamm(t, _ps=psG, _lab=gal):
            w3 = ring_slot(_lab).rearrange("p (k j) -> p k j", k=16)
            i = None
            for nh in range(2):
                for k in range(16):
                    i = t.matmul(_ps[:, nh * 512:(nh + 1) * 512], w3[:, k, :],
                                 b4[:, k, nh * 512:(nh + 1) * 512],
                                 start=(k == 0), stop=(k == 15))
            return i
        ga_done = P.add("t", gamm, s_pe)
        ring_free[ring_idx[gal]] = s_pe.n

        if m >= 1:
            emit_z(m - 1)

        P.wait("s", s_pe, ga_done)
        if m >= 1:
            P.wait("s", s_ve, hevac[m - 1])
        silu_done[m] = P.add("s", lambda s, _ps=psG, _m=m: s.activation(
            out=silu_f[:, :], in_=_ps[:], func=AF.Silu, scale=wsg64[:, _m:_m + 1]),
            s_sc)

        P.wait("v", s_sc, silu_done[m])
        P.wait("v", s_pe, up_done)
        if m >= 2:
            P.wait("v", s_pe, zh[m - 2])
        P.add("v", lambda v, _ps=psU: v.tensor_tensor(
            out=htmp_f[:, :], in0=silu_f[:, :], in1=_ps[:], op=ALU.mult), s_ve)
        hevac[m] = P.add("v", lambda v, _m=m: v.tensor_scalar(
            out=hbf[:, _m % 2, :], in0=htmp_f[:, :], scalar1=cu_eff[:, _m:_m + 1],
            scalar2=None, op0=ALU.mult), s_ve)
        emit_loads()
    emit_z(63)

    if stop_after == "ph5":
        return finish()

    dz64 = dz.rearrange("t (b v) n -> (t b) v n", v=64)
    brd = {}
    bh = {}
    bh2 = {}
    bev = {}
    bev_s = {}
    bufA4 = scrA[:, 0:4096].rearrange("p (a n) -> p a n", a=4)
    bufB4 = scrB[:, 0:4096].rearrange("p (a n) -> p a n", a=4)
    for j in range(16):
        buf = bufA4 if j % 2 == 0 else bufB4
        if j == 0:
            P.wait("y", s_z[0], zout[30])
            P.wait("y", s_z[1], zout[31])
        if j == 1:
            P.wait("y", s_ve, hevac[63])
        if j >= 2:
            P.wait("y", s_pe, bh2[j - 2])
        brd[j] = P.add("y", lambda y, _j=j, _b=buf: y.dma_start(
            out=_b[:, :, :], in_=dz64[:, 4 * _j:4 * _j + 4, :]), s_bd[j % 2], 16)
        P.wait("t", s_bd[j % 2], brd[j])
        P.wait("t", s_pre, PRE_T)
        if j >= 1:
            P.wait("t", s_ve, bev[4 * (j - 1) + 1])
        def h64a(t, _b=buf):
            i = None
            for vv, ps in ((0, pA), (1, pC)):
                for (c0, c1) in [(0, 512), (512, 1024)]:
                    i = t.matmul(ps[:, c0:c1], hadB, _b[:, vv, c0:c1],
                                 start=True, stop=True)
            return i
        bh[j] = P.add("t", h64a, s_pe)
        if j >= 1:
            P.wait("t", s_sc, bev_s[4 * (j - 1) + 3])
        def h64b(t, _b=buf):
            i = None
            for vv, ps in ((2, pB), (3, pD)):
                for (c0, c1) in [(0, 512), (512, 1024)]:
                    i = t.matmul(ps[:, c0:c1], hadB, _b[:, vv, c0:c1],
                                 start=True, stop=True)
            return i
        bh2[j] = P.add("t", h64b, s_pe)
        P.wait("v", s_pe, bh[j])
        for vv, ps in ((0, pA), (1, pC)):
            bev[4 * j + vv] = P.add("v", lambda v, _j=j, _vv=vv, _ps=ps: v.tensor_copy(
                out=h2B(4 * _j + _vv), in_=_ps[:]), s_ve)
        P.wait("s", s_pe, bh2[j])
        for vv, ps in ((2, pB), (3, pD)):
            bev_s[4 * j + vv] = P.add("s", lambda s, _j=j, _vv=vv, _ps=ps: s.copy(
                out=h2B(4 * _j + _vv), in_=_ps[:]), s_sc)
        emit_loads()

    if stop_after == "ph6":
        return finish()

    rowA2 = scrB[0:1, 0:2048].bitcast(F32)
    rowB2 = scrB[0:1, 2048:4096].bitcast(F32)
    rsq2 = scrB[:, 0:2048]
    dn_done = {}
    r2_done = {}
    sq2 = {}
    st2 = {}
    for m in range(17):
        if m < 16:
            psDn = pA if m % 2 == 0 else pB
            if m == 0:
                P.wait("t", s_ve, bev[61])
                P.wait("t", s_sc, bev_s[63])
            if m >= 2:
                P.wait("t", s_ve, r2_done[m - 2])
            for kq in range(4):
                lab = f"wd{m}_{kq}"
                P.wait("t", *g_target[lab])

                def dnmm(t, _lab=lab, _ps=psDn, _kq=kq):
                    w3 = ring_slot(_lab).rearrange("p (k j) -> p k j", k=16)
                    i = None
                    for k in range(16):
                        vv = _kq * 16 + k
                        for nh in range(2):
                            i = t.matmul(_ps[:, nh * 512:(nh + 1) * 512], w3[:, k, :],
                                         h2B(vv)[:, nh * 512:(nh + 1) * 512],
                                         start=(_kq == 0 and k == 0),
                                         stop=(_kq == 3 and k == 15))
                    return i
                dn_done[m] = P.add("t", dnmm, s_pe)
                ring_free[ring_idx[lab]] = s_pe.n
            P.wait("s", s_pe, dn_done[m])
            scl = P.add("s", lambda s, _ps=psDn, _m=m: s.activation(
                out=_ps[:], in_=_ps[:], func=AF.Copy, scale=wsd64[:, _m:_m + 1]), s_sc)
            P.wait("v", s_sc, scl)
            r2_done[m] = P.add("v", lambda v, _ps=psDn, _m=m: v.tensor_tensor(
                out=b3[:, _m, :], in0=_ps[:], in1=b3[:, _m, :], op=ALU.add), s_ve)
            P.wait("s", s_ve, r2_done[m])
            if m >= 2:
                P.wait("s", s_pe, st2[m - 2])
            sq2[m] = P.add("s", lambda s, _m=m: s.activation(
                out=rsq2[:, (_m % 2) * 1024:(_m % 2 + 1) * 1024], in_=b3[:, _m, :],
                func=AF.Square), s_sc)
        if m >= 1:
            mm_ = m - 1
            P.wait("t", s_ve, r2_done[mm_])
            P.wait("t", s_sc, sq2[mm_])

            def st2mm(t, _m=mm_):
                i = None
                for (c0, c1) in [(0, 512), (512, 1024)]:
                    t.matmul(pC[0:1, c0:c1], ones_bf, b3[:, _m, c0:c1],
                             start=(_m == 0), stop=(_m == 15))
                    i = t.matmul(pD[0:1, c0:c1], ones_bf,
                                 rsq2[:, (_m % 2) * 1024 + c0:(_m % 2) * 1024 + c1],
                                 start=(_m == 0), stop=(_m == 15))
                return i
            st2[mm_] = P.add("t", st2mm, s_pe)
        emit_loads()

    P.wait("v", s_pe, st2[15])
    P.add("v", lambda v: v.tensor_scalar(out=rowA2, in0=pC[0:1, :],
                                         scalar1=float(1.0 / D), scalar2=None,
                                         op0=ALU.mult), s_ve)
    P.add("v", lambda v: v.tensor_tensor(out=rowB2, in0=rowA2, in1=rowA2,
                                         op=ALU.mult), s_ve)
    P.add("v", lambda v: v.tensor_scalar(out=pD[0:1, :], in0=pD[0:1, :],
                                         scalar1=float(1.0 / D), scalar2=None,
                                         op0=ALU.mult), s_ve)
    vv2 = P.add("v", lambda v: v.tensor_tensor(out=rowB2, in0=pD[0:1, :],
                                               in1=rowB2, op=ALU.subtract), s_ve)
    vv2 = P.add("v", lambda v: v.tensor_scalar(
        out=rowB2, in0=rowB2, scalar1=float(EPS), scalar2=None, op0=ALU.add), s_ve)
    P.wait("s", s_ve, vv2)
    P.add("s", lambda s: s.activation(out=rowB2, in_=rowB2, func=AF.Ln), s_sc)
    a2_done = P.add("s", lambda s: s.activation(out=rowB2, in_=rowB2,
                                                func=AF.Exp, scale=-0.5), s_sc)
    P.wait("v", s_sc, a2_done)
    P.add("v", lambda v: v.tensor_tensor(out=rowA2, in0=rowA2, in1=rowB2,
                                         op=ALU.mult), s_ve)
    b2_done = P.add("v", lambda v: v.tensor_scalar(out=rowA2, in0=rowA2,
                                                   scalar1=-1.0, scalar2=None,
                                                   op0=ALU.mult), s_ve)
    P.wait("t", s_ve, b2_done)

    def bcln2(t):
        i = None
        for (c0, c1) in [(0, 512), (512, 1024)]:
            t.matmul(pC[:, c0:c1], ones_f[0:1, 0:128], rowB2[0:1, c0:c1],
                     start=True, stop=True)
            i = t.matmul(pD[:, c0:c1], ones_f[0:1, 0:128], rowA2[0:1, c0:c1],
                         start=True, stop=True)
        return i
    bc2 = P.add("t", bcln2, s_pe)

    outst = scrB.bitcast(F32)
    outg = big1.bitcast(F32)
    a_sb = vtbuf[:, 0:2048].bitcast(F32)
    b_sb = vtbuf[:, 2048:4096].bitcast(F32)
    out_dma = {}
    P.wait("v", s_pe, bc2)
    P.add("v", lambda v: v.tensor_copy(out=a_sb, in_=pC[:]), s_ve)
    ab_done = P.add("v", lambda v: v.tensor_copy(out=b_sb, in_=pD[:]), s_ve)
    P.wait("g", s_ve, ab_done)
    pool_set = (1, 4, 7, 10, 13, 15)
    prev_v, prev_g = {}, {}
    nv = ng = 0
    for m in range(16):
        if m not in pool_set:
            sl = slice((nv % 2) * 1024, (nv % 2 + 1) * 1024)
            if nv >= 2:
                P.wait("v", s_od[nv % 2], out_dma[prev_v[nv % 2]])
            P.add("v", lambda v, _m=m, _s=sl: v.tensor_tensor(
                out=outst[:, _s], in0=b3[:, _m, :], in1=pC[:], op=ALU.mult), s_ve)
            od = P.add("v", lambda v, _m=m, _s=sl: v.tensor_tensor(
                out=outst[:, _s], in0=outst[:, _s], in1=pD[:], op=ALU.add), s_ve)
            P.wait("y", s_ve, od)
            out_dma[m] = P.add("y", lambda y, _m=m, _s=sl: y.dma_start(
                out=dy[_m], in_=outst[:, _s]), s_od[nv % 2], 16)
            prev_v[nv % 2] = m
            nv += 1
        else:
            sl = slice((ng % 2) * 1024, (ng % 2 + 1) * 1024)
            if ng >= 2:
                P.wait("g", s_op[ng % 2], out_dma[prev_g[ng % 2]])
            gm = P.add("g", lambda g, _m=m, _s=sl: g.tensor_tensor(
                out=outg[:, _s], in0=b3[:, _m, :], in1=a_sb, op=ALU.mult), s_gp)
            P.wait("g", s_gp, gm)
            od = P.add("g", lambda g, _m=m, _s=sl: g.tensor_tensor(
                out=outg[:, _s], in0=outg[:, _s], in1=b_sb, op=ALU.add), s_gp)
            P.wait("s", s_gp, od)
            out_dma[m] = P.add("s", lambda s, _m=m, _s=sl: s.dma_start(
                out=dy[_m], in_=outg[:, _s]), s_op[ng % 2], 16)
            prev_g[ng % 2] = m
            ng += 1
    return finish()



def kernel(**inputs):
    g, shards = host_prepare(inputs)
    nc = build_program()
    shared = {k: g[k] for k in ("wq", "wk", "wv", "wo", "wu", "wg", "wd",
                                "cstall", "fall")}
    in_maps = [dict(shared, x=shards[c]) for c in range(NCORES)]
    res = run_bass_kernel_spmd(nc, in_maps, core_ids=list(range(NCORES)))
    out = np.empty((8 * T, D), dtype=np.float32)
    for c in range(NCORES):
        yc = np.asarray(res.results[c]["y"])
        out[c * T:(c + 1) * T, :] = yc.reshape(D, T).T
    return out.reshape(4, 2048, 2048)
